# version 53
# baseline (speedup 1.0000x reference)
import numpy as np
import ml_dtypes
import concourse.bass as bass
import concourse.mybir as mybir
from concourse.bass_utils import run_bass_kernel_spmd
from contextlib import ExitStack

F32 = mybir.dt.float32
BF16 = mybir.dt.bfloat16
AF = mybir.ActivationFunctionType
ALU = mybir.AluOpType
AX = mybir.AxisListType

T = 2048
D = 1024
NCH = 8
NEXP = 8
NEG = -20.0
EPS = 1e-5

DBG = []


class Emit:
    DMACH = ("dy0", "dy1", "dy2", "dy3", "dy4", "dg0", "dg1", "dg2",
             "dsc0", "dsc1")

    def __init__(self, nc, stack):
        self.nc = nc
        self.sem = {}
        for k in ("pe", "act", "dve", "pool") + self.DMACH:
            self.sem[k] = stack.enter_context(nc.semaphore("sem_" + k))
        self.cnt = {k: 0 for k in self.sem}
        self.wcache = {}
        self.pe_mode = None
        self.eng = {"pe": nc.tensor, "act": nc.scalar, "dve": nc.vector,
                    "pool": nc.gpsimd, "dg": nc.gpsimd, "dy": nc.sync,
                    "dsc": nc.scalar}
        self.dmaeng = {}
        for ch in self.DMACH:
            self.dmaeng[ch] = (nc.sync if ch.startswith("dy")
                               else nc.gpsimd if ch.startswith("dg")
                               else nc.scalar)

    def tk(self, instr, key, step=1):
        self.cnt[key] += step
        instr.then_inc(self.sem[key], step)
        return (key, self.cnt[key])

    def w(self, consumer, *tokens):
        eng = self.eng[consumer]
        for tok in tokens:
            if tok is None:
                continue
            key, val = tok
            ck = (consumer, key)
            if self.wcache.get(ck, 0) >= val:
                continue
            eng.wait_ge(self.sem[key], val)
            self.wcache[ck] = val

    def bar(self):
        return tuple((k, self.cnt[k]) for k in self.cnt)

    def mode(self, m):
        if self.pe_mode != m:
            self.nc.tensor.drain()
            self.pe_mode = m

    def dma(self, ch, out, in_):
        eng = self.dmaeng[ch]
        if self.cnt[ch]:
            ck = ("q" + ch, ch)
            if self.wcache.get(ck, 0) < self.cnt[ch]:
                eng.wait_ge(self.sem[ch], self.cnt[ch])
                self.wcache[ck] = self.cnt[ch]
        return self.tk(eng.dma_start(out=out, in_=in_), ch, 16)


def _host_layout(x, W_attn, b_attn, W_proj, b_proj,
                 router_w, router_b, gate_up_w, gate_up_b, down_w, down_b):
    fl = np.ascontiguousarray
    DH = 64
    head_perm = np.concatenate([np.arange(0, DH, 2), np.arange(1, DH, 2)])
    qk_cols = []
    for m in range(16):
        base = 0 if m < 8 else D
        hA = 2 * (m % 8)
        qk_cols.append(np.concatenate([base + hA * DH + head_perm,
                                       base + (hA + 1) * DH + head_perm]))
    qk_cols = np.stack(qk_cols)

    wqk = np.empty((128, 16 * NCH * 128), np.float32)
    for m in range(16):
        sub = W_attn[:, qk_cols[m]].reshape(NCH, 128, 128)
        wqk[:, m * NCH * 128:(m + 1) * NCH * 128] = \
            np.transpose(sub, (1, 0, 2)).reshape(128, NCH * 128)
    bqk = np.stack([b_attn[qk_cols[m]] for m in range(16)], axis=1)

    wv = fl(np.transpose(W_attn[:, 2 * D:3 * D].reshape(NCH, 128, D),
                         (1, 0, 2)).reshape(128, NCH * D))
    bv = fl(b_attn[2 * D:3 * D][None, :])

    tmp = np.transpose(W_proj.reshape(NCH, 128, NCH, 128), (1, 2, 0, 3))
    wproj = fl(tmp.reshape(128, NCH * NCH * 128))
    bproj = fl(b_proj.reshape(NCH, 128).T)

    rwt = fl(np.transpose(router_w.reshape(NEXP, NCH, 128), (2, 1, 0))
             .reshape(128, NCH * NEXP))
    rbr = fl(router_b[None, :])

    wg = np.empty((NEXP, 128, 16 * NCH * 128), np.float32)
    gub = np.empty((128, NEXP * 16), np.float32)
    for e in range(NEXP):
        gu = np.empty((16, NCH, 128, 128), np.float32)
        for m in range(16):
            if m < 8:
                rows = 2 * (m * 128 + np.arange(128))
            else:
                rows = 2 * ((m - 8) * 128 + np.arange(128)) + 1
            sub = gate_up_w[e][rows, :]
            gu[m] = np.transpose(sub.reshape(128, NCH, 128), (1, 2, 0))
            gub[:, e * 16 + m] = gate_up_b[e][rows]
        wg[e] = np.transpose(gu, (2, 0, 1, 3)).reshape(128, 16 * NCH * 128)

    wd = np.empty((NEXP, 128, NCH * NCH * 128), np.float32)
    for e in range(NEXP):
        dwe = np.empty((NCH, NCH, 128, 128), np.float32)
        for m2 in range(NCH):
            sub = down_w[e][m2 * 128:(m2 + 1) * 128, :]
            dwe[m2] = np.transpose(sub.reshape(128, NCH, 128), (1, 2, 0))
        wd[e] = np.transpose(dwe, (2, 0, 1, 3)).reshape(128, NCH * NCH * 128)
    dbt = fl(down_b)

    i32 = np.arange(32, dtype=np.float64)
    theta = 10000.0 ** (-i32 / 32.0)
    ang = np.arange(T, dtype=np.float64)[:, None] * theta[None, :]
    cos = np.cos(ang).T.astype(np.float32)
    sin = np.sin(ang).T.astype(np.float32)
    cc = np.concatenate([cos, cos, cos, cos], axis=0)
    ss = np.concatenate([-sin, sin, -sin, sin], axis=0)
    kq = np.arange(128)
    cmask = (kq[None, :] >= kq[:, None]).astype(np.float32)
    ident = np.eye(128, dtype=np.float32)
    onesc = np.ones((128, 1), np.float32)
    onesr = np.ones((1, 128), np.float32)
    consts = np.zeros((128, 8), np.float32)
    consts[:, 0] = NEG
    consts[:, 1] = EPS
    consts[:, 2] = 1.0
    consts[:, 4] = 7.0
    sel = np.zeros((16, NCH * 128), np.float32)
    for c in range(NCH):
        sel[c, c * 128:c * 128 + 64] = 1.0
        sel[8 + c, c * 128 + 64:(c + 1) * 128] = 1.0

    bf = ml_dtypes.bfloat16
    wqk, wv, wproj = wqk.astype(bf), wv.astype(bf), wproj.astype(bf)
    wg, wd = wg.astype(bf), wd.astype(bf)
    shared = dict(wqk=wqk, bqk=bqk, wv=wv, bv=bv, wproj=wproj, bproj=bproj,
                  rwt=rwt, rbr=rbr, wg=wg, gub=gub, wd=wd, dbt=dbt,
                  cc=cc, ss=ss, cmask=cmask, ident=ident,
                  onesc=onesc, onesr=onesr, consts=consts, sel=sel)
    per_core = []
    for b in range(x.shape[0]):
        m = dict(shared)
        m["x_t"] = fl(x[b].T)
        per_core.append(m)
    return per_core


def _ln_cast(nc, E, acc, bf16buf, sq16, consts, tt, m, ready_m, prev_apply,
             bstride=None, cast_eng="dve"):
    act = nc.scalar
    eng = nc.gpsimd if cast_eng == "pool" else nc.vector
    sl = slice(m * T + tt * 512, m * T + (tt + 1) * 512)
    bsl = (slice(m * T + tt * 512, m * T + (tt + 1) * 512) if bstride is None
           else slice(m * bstride, m * bstride + 512))
    E.w("act", ready_m, prev_apply)
    sq_tok = E.tk(act.activation(sq16[:, m * 512:(m + 1) * 512],
                                 acc[:, sl], AF.Square,
                                 bias=consts[:, 3:4], scale=1.0), "act")
    E.w(cast_eng, ready_m, prev_apply)
    cast_tok = E.tk(eng.tensor_copy(bf16buf[:, bsl], acc[:, sl]), cast_eng)
    return cast_tok, sq_tok


def _ln_stats(nc, E, acc, bf16buf, sq16, srB, srC, onesc, consts,
              tt, ready, pstat, psq, prev_apply, bstride=None,
              cast_toks=None, sq_toks=None):
    pe, act, dve = nc.tensor, nc.scalar, nc.vector

    def bsl(m):
        if bstride is None:
            return slice(m * T + tt * 512, m * T + (tt + 1) * 512)
        return slice(m * bstride, m * bstride + 512)
    if cast_toks is None:
        cast_toks, sq_toks = {}, {}
        for m in range(NCH):
            cast_toks[m], sq_toks[m] = _ln_cast(nc, E, acc, bf16buf, sq16,
                                                consts, tt, m, ready[m],
                                                prev_apply, bstride)
    E.mode(("mm", "k128m1", "bf16"))
    for m in range(NCH):
        E.w("pe", cast_toks[m], sq_toks[m])
        pe.matmul(pstat[0:1, 0:512], onesc[:, 0:1], bf16buf[:, bsl(m)],
                  start=(m == 0), stop=(m == NCH - 1))
    mm2 = None
    for m in range(NCH):
        mm2 = pe.matmul(psq[0:1, 0:512], onesc[:, 0:1],
                        sq16[:, m * 512:(m + 1) * 512],
                        start=(m == 0), stop=(m == NCH - 1))
    tok_stat = E.tk(mm2, "pe")
    E.w("dve", tok_stat)
    tokA = E.tk(dve.tensor_scalar(srC[0:1, :], pstat[0:1, 0:512], 1.0 / D, None,
                                  ALU.mult), "dve")
    E.w("act", tokA)
    tokQ = E.tk(act.activation(srB[0:1, :], srC[0:1, :], AF.Square,
                               bias=consts[0:1, 3:4], scale=1.0), "act")
    E.w("dve", tokQ)
    tokD = E.tk(dve.scalar_tensor_tensor(srB[0:1, :], psq[0:1, 0:512], 1.0 / D,
                                         srB[0:1, :], ALU.mult, ALU.subtract),
                "dve")
    E.w("act", tokD)
    tokE = E.tk(act.activation(srB[0:1, :], srB[0:1, :], AF.Ln,
                               bias=consts[0:1, 1:2], scale=1.0), "act")
    E.w("act", tokE)
    tokF = E.tk(act.activation(srB[0:1, :], srB[0:1, :], AF.Exp,
                               bias=consts[0:1, 3:4], scale=-0.5), "act")
    E.w("dve", tokF)
    tokG = E.tk(dve.tensor_tensor(srC[0:1, :], srC[0:1, :], srB[0:1, :],
                                  ALU.mult), "dve")
    return (tokF, tokG)


def _ln_apply(nc, E, acc, bf16buf, srB, srC, onesrf, tt, row_toks,
              out_cast, pb1, pb2, bstride=None, consts=None, ret_per_m=False):
    pe, dve, act = nc.tensor, nc.vector, nc.scalar

    def bsl(m):
        if bstride is None:
            return slice(m * T + tt * 512, m * T + (tt + 1) * 512)
        return slice(m * bstride, m * bstride + 512)
    E.mode(("mm", "k1", "f32"))
    E.w("pe", row_toks[0], row_toks[1])
    tok_b1 = E.tk(pe.matmul(pb1[:, 0:512], onesrf[0:1, :], srB[0:1, :],
                            start=True, stop=True), "pe")
    tok_b2 = E.tk(pe.matmul(pb2[:, 0:512], onesrf[0:1, :], srC[0:1, :],
                            start=True, stop=True), "pe")
    out_tok = None
    per_m = []
    for m in range(NCH):
        sl = slice(m * T + tt * 512, m * T + (tt + 1) * 512)
        E.w("dve", tok_b1, tok_b2)
        tok8 = E.tk(dve.tensor_tensor(acc[:, sl], acc[:, sl], pb1[:, 0:512],
                                      ALU.mult), "dve")
        E.w("dve", tok8)
        tok9 = E.tk(dve.tensor_tensor(acc[:, sl], acc[:, sl], pb2[:, 0:512],
                                      ALU.subtract), "dve")
        out_tok = tok9
        per_m.append(tok9)
        if out_cast:
            E.w("act", tok9)
            out_tok = E.tk(act.activation(bf16buf[:, bsl(m)], acc[:, sl],
                                          AF.Identity, bias=consts[:, 3:4],
                                          scale=1.0), "act")
    return (out_tok, per_m) if ret_per_m else out_tok


def build_nc():
    nc = bass.Bass("TRN2", target_bir_lowering=False, debug=False)
    def I(name, shape):
        return nc.dram_tensor(name, list(shape), F32, kind="ExternalInput").ap()

    def IB(name, shape):
        return nc.dram_tensor(name, list(shape), BF16,
                              kind="ExternalInput").ap()

    x_t = I("x_t", (D, T))
    wqk_i = IB("wqk", (128, 16 * NCH * 128))
    bqk_i = I("bqk", (128, 16))
    wv_i = IB("wv", (128, NCH * D))
    bv_i = I("bv", (1, D))
    wproj_i = IB("wproj", (128, NCH * NCH * 128))
    bproj_i = I("bproj", (128, NCH))
    rwt_i = I("rwt", (128, NCH * NEXP))
    rbr_i = I("rbr", (1, NEXP))
    wg_i = IB("wg", (NEXP, 128, 16 * NCH * 128))
    gub_i = I("gub", (128, NEXP * 16))
    wd_i = IB("wd", (NEXP, 128, NCH * NCH * 128))
    dbt_i = I("dbt", (NEXP, D))
    cc_i = I("cc", (128, T))
    ss_i = I("ss", (128, T))
    cmask_i = I("cmask", (128, 128))
    ident_i = I("ident", (128, 128))
    onesc_i = I("onesc", (128, 1))
    onesr_i = I("onesr", (1, 128))
    consts_i = I("consts", (128, 8))
    sel_i = I("sel", (16, NCH * 128))
    out = nc.dram_tensor("out", [D, T], F32, kind="ExternalOutput").ap()
    ys_d = nc.dram_tensor("ys_d", [130, 32 * 512], BF16).ap()
    x1_d = nc.dram_tensor("x1_d", [128, NCH * T], BF16).ap()

    dbg_out = {}
    for name, shape in (("vaug", (128, 16 * 1056)),
                        ("x1", (D, T)), ("cw", (128, 128)),
                        ("accmoe", (D, T))):
        if name in DBG:
            dbg_out[name] = nc.dram_tensor("dbg_" + name, list(shape), F32,
                                           kind="ExternalOutput").ap()

    with ExitStack() as st:
        a = st.enter_context
        E = Emit(nc, st)
        pe, act, dve = nc.tensor, nc.scalar, nc.vector

        bqk = a(nc.sbuf_tensor("bqk_s", [128, 16], F32))
        bproj = a(nc.sbuf_tensor("bproj_s", [128, NCH], F32))
        rwt = a(nc.sbuf_tensor("rwt_s", [128, NCH * NEXP], F32))
        rbr_row = a(nc.sbuf_tensor("rbr_s", [1, NEXP], F32))
        gubs = a(nc.sbuf_tensor("gubs_s", [128, NEXP * 16], F32))
        gub7 = a(nc.sbuf_tensor("gub7_s", [128, NEXP * 16], F32))
        dbt16 = a(nc.sbuf_tensor("dbt16_s", [NEXP, D], BF16))
        cmask = a(nc.sbuf_tensor("cmask_s", [128, 128], BF16))
        onesc = a(nc.sbuf_tensor("onesc_s", [128, 1], BF16))
        onesr = a(nc.sbuf_tensor("onesr_s", [1, 128], BF16))
        onesrf = a(nc.sbuf_tensor("onesrf_s", [1, 128], F32))
        consts = a(nc.sbuf_tensor("consts_s", [128, 8], F32))
        rbrep = a(nc.sbuf_tensor("rbrep_s", [128, NEXP], F32))
        cwT16 = a(nc.sbuf_tensor("cwT16", [NEXP, T], BF16))

        t_bqk = E.dma("dy4", bqk[:], bqk_i)
        t_bproj = E.dma("dy3", bproj[:], bproj_i)
        t_rwt = E.dma("dy2", rwt[:], rwt_i)
        t_rbr = E.dma("dy1", rbr_row[:], rbr_i)
        t_dbt = E.dma("dg0", dbt16[:], dbt_i)
        t_cm = E.dma("dg1", cmask[:], cmask_i)
        t_oc = E.dma("dg2", onesc[:], onesc_i)
        t_or = E.dma("dg2", onesr[:], onesr_i)
        t_orf = E.dma("dsc0", onesrf[:], onesr_i)
        t_cst = E.dma("dsc1", consts[:], consts_i)

        with nc.sbuf_tensor("gub_s", [128, NEXP * 16], F32) as gub, \
             nc.psum_tensor("pb_init", [128, 512], F32) as pbi:
            t_gub = E.dma("dy0", gub[:], gub_i)
            E.w("act", t_gub, t_cst)
            t_gubs = None
            for e2 in range(NEXP):
                act.activation(gubs[:, e2 * 16:e2 * 16 + 8],
                               gub[:, e2 * 16:e2 * 16 + 8],
                               AF.Identity, bias=consts[:, 3:4], scale=1.702)
                i2 = act.activation(gubs[:, e2 * 16 + 8:e2 * 16 + 16],
                                    gub[:, e2 * 16 + 8:e2 * 16 + 16],
                                    AF.Identity, bias=consts[:, 2:3], scale=1.0)
                t_gubs = E.tk(i2, "act")
            i3 = act.activation(gub7[:], gub[:], AF.Identity,
                                bias=consts[:, 4:5], scale=1.0)
            t_gub7 = E.tk(i3, "act")
            E.w("pe", t_gubs, t_gub7)
            E.mode(("mm", "k1", "f32"))
            E.w("pe", t_orf, t_rbr)
            tok_mm = E.tk(pe.matmul(pbi[:, 0:8], onesrf[0:1, :], rbr_row[0:1, :],
                                    start=True, stop=True), "pe")
            E.w("dve", tok_mm)
            t_rbrep = E.tk(dve.tensor_copy(rbrep[:], pbi[:, 0:8]), "dve")
            E.w("pe", t_rbrep)

        t_attn_done = None
        t_yt_dma = None
        with ExitStack() as s12:
            b = s12.enter_context
            bvrep = b(nc.sbuf_tensor("bvrep_s", [128, D], BF16))
            with nc.sbuf_tensor("bv_row", [1, D], F32) as bv_s, \
                 nc.psum_tensor("pb_init2", [128, 512], F32) as pbi:
                t_bv = E.dma("dy1", bv_s[:], bv_i)
                E.mode(("mm", "k1", "f32"))
                E.w("pe", t_bv, t_orf)
                for half in range(2):
                    tok_mm = E.tk(pe.matmul(pbi[:, 0:512], onesrf[0:1, :],
                                            bv_s[0:1, half * 512:(half + 1) * 512],
                                            start=True, stop=True), "pe")
                    E.w("dve", tok_mm)
                    t_bvr = E.tk(dve.tensor_copy(
                        bvrep[:, half * 512:(half + 1) * 512],
                        pbi[:, 0:512]), "dve")
                    E.w("pe", t_bvr)
            qkraw = b(nc.sbuf_tensor("qkraw", [128, 16 * T], BF16))
            vaug = b(nc.sbuf_tensor("vaug", [128, 16 * 1056], BF16))
            cc16 = b(nc.sbuf_tensor("cc16", [128, T], BF16))
            ss16 = b(nc.sbuf_tensor("ss16", [128, T], BF16))
            qswp = [b(nc.sbuf_tensor(f"qswp{i}", [128, T], BF16))
                    for i in range(2)]
            rt1 = b(nc.sbuf_tensor("rt1", [128, T], BF16))
            rt2 = b(nc.sbuf_tensor("rt2", [128, T], BF16))

            with ExitStack() as s1:
                b1 = s1.enter_context
                wqk16 = b1(nc.sbuf_tensor("wqk16", [128, 16 * NCH * 128], BF16))
                wv16 = b1(nc.sbuf_tensor("wv16", [128, NCH * D], BF16))
                xsl = [b1(nc.sbuf_tensor(f"xsl{i}", [128, NCH * 512], BF16))
                       for i in range(2)]
                def load_xsl(par, tt):
                    return E.dma(
                        "dg1" if par == 0 else "dg2",
                        xsl[par][:].rearrange("p (c t) -> p c t", t=512),
                        x_t[:, tt * 512:(tt + 1) * 512]
                        .rearrange("(c p) t -> p c t", p=128))

                t_xsl = [load_xsl(0, 0), None]
                SW = 4 * NCH * 128
                t_wqk = [E.dma("dg0", wqk16[:, 0 * SW:1 * SW], wqk_i[:, 0 * SW:1 * SW])]
                t_xsl[1] = load_xsl(1, 1)
                t_wqk += [E.dma("dg0", wqk16[:, s_ * SW:(s_ + 1) * SW],
                                wqk_i[:, s_ * SW:(s_ + 1) * SW]) for s_ in (1, 2, 3)]
                t_wv = E.dma("dg0", wv16[:], wv_i)
                t_cc = E.dma("dg0", cc16[:], cc_i)
                t_ss = E.dma("dg0", ss16[:], ss_i)
                tok_vz = E.tk(dve.memset(vaug[:], 0.0), "dve")

                with nc.psum_tensor("pq0", [128, 512], F32) as pq0, \
                     nc.psum_tensor("pq1", [128, 512], F32) as pq1, \
                     nc.psum_tensor("pv0", [128, 1024], F32) as pv0, \
                     nc.psum_tensor("pv1", [128, 1024], F32) as pv1:
                    pq, pv = [pq0, pq1], [pv0, pv1]
                    E.mode(("mm", "k128", "bf16"))
                    qk_ev = {}
                    v_ev = {0: None, 1: None}
                    xsl_users = [None, None]
                    for tt in range(4):
                        par = tt % 2
                        if tt >= 2:
                            E.w("dg", xsl_users[par])
                            t_xsl[par] = load_xsl(par, tt)
                        for m in range(16):
                            mp = m % 2
                            E.w("pe", t_wqk[m // 4], t_xsl[par])
                            if (m - 2, tt) in qk_ev:
                                E.w("pe", qk_ev[(m - 2, tt)])
                            elif tt > 0 and (m + 14, tt - 1) in qk_ev:
                                E.w("pe", qk_ev[(m + 14, tt - 1)])
                            mm_last = None
                            for c in range(NCH):
                                mm_last = pe.matmul(
                                    pq[mp][:, 0:512],
                                    wqk16[:, (m * NCH + c) * 128:(m * NCH + c + 1) * 128],
                                    xsl[par][:, c * 512:(c + 1) * 512],
                                    start=(c == 0), stop=(c == NCH - 1))
                            tok_mm = E.tk(mm_last, "pe")
                            E.w("act", tok_mm, t_bqk)
                            ei = act.activation(
                                qkraw[:, m * T + tt * 512:m * T + (tt + 1) * 512],
                                pq[mp][:, 0:512], AF.Identity,
                                bias=bqk[:, m:m + 1], scale=1.0)
                            qk_ev[(m, tt)] = E.tk(ei, "act")
                        for vi in range(4):
                            ti = tt * 4 + vi
                            vp = vi % 2
                            E.w("pe", t_wv)
                            if ti >= 2:
                                E.w("pe", v_ev[vp])
                            mm_last = None
                            for half in range(2):
                                for c in range(NCH):
                                    mm_last = pe.matmul(
                                        pv[vp][:, half * 512:(half + 1) * 512],
                                        xsl[par][:, c * 512 + vi * 128:
                                                 c * 512 + (vi + 1) * 128],
                                        wv16[:, c * D + half * 512:
                                             c * D + (half + 1) * 512],
                                        start=(c == 0), stop=(c == NCH - 1))
                            tok_mm = E.tk(mm_last, "pe")
                            E.w("dve", tok_mm, t_bvr, tok_vz)
                            ovw = vaug[:, ti * 1056:(ti + 1) * 1056]
                            ovw = ovw.rearrange("p (h x) -> p h x", x=66)[:, :, 0:64]
                            v_ev[vp] = E.tk(dve.tensor_tensor(ovw, pv[vp][:],
                                                              bvrep[:], ALU.add),
                                            "dve")
                        xsl_users[par] = tok_mm
                    E.w("pe", qk_ev[(14, 3)], qk_ev[(15, 3)], v_ev[0], v_ev[1])
                E.w("dve", v_ev[0], v_ev[1])
                tok_vones = None
                for ti in range(16):
                    vv = vaug[:, ti * 1056:(ti + 1) * 1056]
                    vv = vv.rearrange("p (h x) -> p h x", x=66)[:, :, 64:65]
                    tok_vones = E.tk(dve.memset(vv, 1.0), "dve")

            with ExitStack() as s2:
                b2 = s2.enter_context
                kpadB = b2(nc.sbuf_tensor("kpadB", [128, 8 * T], BF16))
                attAB = [b2(nc.sbuf_tensor(f"attAB{i}", [128, 1024], BF16))
                         for i in range(3)]
                ytst = [b2(nc.sbuf_tensor(f"ytst{i}", [65, 1024], BF16))
                        for i in range(2)]
                pe_end = E.bar()
                tok_kbz = None

                def emit_rope(m, prev_tok):
                    par = m % 2
                    E.w("dy", *[qk_ev[(m, ttx)] for ttx in range(4)])
                    if prev_tok[par] is not None:
                        E.w("dy", prev_tok[par][1])
                    tok_sw = []
                    chs = ("dy0", "dy1") if m < 8 else ("dy2", "dy3")
                    for bi, (blk, srb) in enumerate(((0, 1), (1, 0),
                                                    (2, 3), (3, 2))):
                        tok_sw.append(E.dma(chs[bi % 2],
                                            qswp[par][blk * 32:(blk + 1) * 32, :],
                                            qkraw[srb * 32:(srb + 1) * 32,
                                                  m * T:(m + 1) * T]))
                    E.w("dve", *[qk_ev[(m, ttx)] for ttx in range(4)])
                    E.w("dve", t_cc, t_ss)
                    if prev_tok[1 - par] is not None:
                        E.w("dve", prev_tok[1 - par][0])
                    if prev_tok[par] is not None:
                        E.w("dve", prev_tok[par][0])
                    tok_t1 = E.tk(dve.tensor_tensor(rt1[:],
                                                    qkraw[:, m * T:(m + 1) * T],
                                                    cc16[:], ALU.mult), "dve")
                    E.w("dve", *tok_sw)
                    tok_t2 = E.tk(dve.tensor_tensor(rt2[:], qswp[par][:], ss16[:],
                                                    ALU.mult), "dve")
                    E.w("dve", tok_t1, tok_t2)
                    if m < 8:
                        tok_t3 = E.tk(dve.tensor_tensor(qkraw[:, m * T:(m + 1) * T],
                                                        rt1[:], rt2[:], ALU.add),
                                      "dve")
                        return (tok_t3, tok_t2), tok_t3
                    c = m - 8
                    nonlocal tok_kbz
                    if tok_kbz is None:
                        E.w("dve", *pe_end)
                        tok_kbz = E.tk(dve.memset(kpadB[0:64, :], 0.0), "dve")
                    E.tk(dve.tensor_tensor(qkraw[0:64, m * T:(m + 1) * T],
                                           rt1[0:64, :], rt2[0:64, :], ALU.add),
                         "dve")
                    tok_b = E.tk(dve.tensor_tensor(kpadB[64:128, c * T:(c + 1) * T],
                                                   rt1[64:128, :], rt2[64:128, :],
                                                   ALU.add), "dve")
                    tok_z = E.tk(dve.memset(qkraw[64:128, m * T:(m + 1) * T], 0.0),
                                 "dve")
                    return (tok_z, tok_t2), tok_z

                with nc.psum_tensor("psab0", [128, 1024], F32) as psab0, \
                     nc.psum_tensor("psab1", [128, 1024], F32) as psab1, \
                     nc.psum_tensor("psy0", [128, 1024], F32) as psy0, \
                     nc.psum_tensor("psy1", [128, 1024], F32) as psy1:
                    psAB = [psab0, psab1]
                    psy = [psy0, psy1]
                    E.mode(("mm", "k128", "bf16"))
                    rope_prev = [None, None]
                    rope_q = {}
                    rope_k = {}

                    def rope0_piece(m, ttx, prev):
                        cs = slice(ttx * 512, (ttx + 1) * 512)
                        qs = slice(m * T + ttx * 512, m * T + (ttx + 1) * 512)
                        chs = ("dy0", "dy1") if m < 8 else ("dy2", "dy3")
                        qsw = qswp[0] if m < 8 else qswp[1]
                        E.w("dy", qk_ev[(m, ttx)])
                        tok_sw = [E.dma(chs[bi % 2],
                                        qsw[blk * 32:(blk + 1) * 32, cs],
                                        qkraw[srb * 32:(srb + 1) * 32, qs])
                                  for bi, (blk, srb) in enumerate(
                                      ((0, 1), (1, 0), (2, 3), (3, 2)))]
                        if m >= 8:
                            E.tk(dve.memset(kpadB[0:64, cs], 0.0), "dve")
                        E.w("dve", qk_ev[(m, ttx)], t_cc, t_ss, prev)
                        tok_t1 = E.tk(dve.tensor_tensor(rt1[:, cs], qkraw[:, qs],
                                                        cc16[:, cs], ALU.mult),
                                      "dve")
                        E.w("dve", *tok_sw)
                        tok_t2 = E.tk(dve.tensor_tensor(rt2[:, cs], qsw[:, cs],
                                                        ss16[:, cs], ALU.mult),
                                      "dve")
                        E.w("dve", tok_t1, tok_t2)
                        if m < 8:
                            return E.tk(dve.tensor_tensor(
                                qkraw[:, qs], rt1[:, cs], rt2[:, cs],
                                ALU.add), "dve"), tok_t2
                        E.tk(dve.tensor_tensor(qkraw[0:64, qs], rt1[0:64, cs],
                                               rt2[0:64, cs], ALU.add), "dve")
                        E.tk(dve.tensor_tensor(kpadB[64:128, cs],
                                               rt1[64:128, cs], rt2[64:128, cs],
                                               ALU.add), "dve")
                        return E.tk(dve.memset(qkraw[64:128, qs], 0.0),
                                    "dve"), tok_t2

                    E.w("dve", *pe_end)
                    q0t, k0t = [], []
                    qt2 = kt2 = None
                    pv = None
                    for ttx in range(4):
                        a_, qt2 = rope0_piece(0, ttx, pv)
                        q0t.append(a_)
                        b_, kt2 = rope0_piece(8, ttx, a_)
                        k0t.append(b_)
                        pv = b_
                    tok_kbz = E.tk(dve.memset(kpadB[0:64, T:8 * T], 0.0), "dve")
                    rope_prev[0] = (q0t[-1], qt2)
                    rope_prev[1] = (k0t[-1], kt2)
                    rope_q[0] = q0t
                    rope_k[0] = k0t
                    exp_tok = [None, None]
                    av_tok = [None, None, None]
                    evac_tok = [None, None]
                    spill_tok = [None, None]
                    qi = 0
                    for cch in range(8):
                        hA, hB = 2 * cch, 2 * cch + 1
                        qcol = cch * T
                        kcol = (8 + cch) * T
                        for Q in range(4):
                            if Q == 2 and cch + 1 < 8:
                                rope_prev[(cch + 1) % 2], rope_q[cch + 1] = \
                                    emit_rope(cch + 1, rope_prev)
                                rope_prev[(cch + 9) % 2], rope_k[cch + 1] = \
                                    emit_rope(cch + 9, rope_prev)
                            tq0 = Q * 512
                            jmax = 4 * Q + 4
                            qp = qi % 2

                            def jslice(j):
                                tqs = max(tq0, 128 * j)
                                return tqs, tq0 + 512 - tqs

                            def emit_scores(j):
                                par = j % 2
                                ab = j % 3
                                tqs, w_ = jslice(j)
                                rq, rk = rope_q[cch], rope_k[cch]
                                E.w("pe", exp_tok[par],
                                    rq[Q] if isinstance(rq, list) else rq,
                                    rk[j // 4] if isinstance(rk, list) else rk,
                                    tok_kbz if cch > 0 else None)
                                mA = pe.matmul(
                                    psAB[par][:, 0:w_],
                                    qkraw[:, kcol + 128 * j:kcol + 128 * j + 128],
                                    qkraw[:, qcol + tqs:qcol + tq0 + 512],
                                    start=True, stop=True)
                                E.tk(mA, "pe")
                                mB = pe.matmul(
                                    psAB[par][:, 512:512 + w_],
                                    kpadB[:, cch * T + 128 * j:cch * T + 128 * j + 128],
                                    qkraw[:, qcol + tqs:qcol + tq0 + 512],
                                    start=True, stop=True)
                                tok_mm = E.tk(mB, "pe")
                                E.w("act", tok_mm, t_cst, av_tok[ab])
                                src3 = psAB[par][:].rearrange(
                                    "p (b t) -> p b t", t=512)[:, :, 0:w_]
                                dst3 = attAB[ab][:].rearrange(
                                    "p (b t) -> p b t", t=512)[:, :, 0:w_]
                                eAB = act.activation(dst3, src3, AF.Exp,
                                                     bias=consts[:, 0:1],
                                                     scale=0.125)
                                tok_e = E.tk(eAB, "act")
                                if 128 * j >= tq0:
                                    E.w("pool", tok_e, t_cm)
                                    mdst = attAB[ab][:].rearrange(
                                        "p (b t) -> p b t", t=512)[:, :, 0:128]
                                    mi = nc.gpsimd.tensor_tensor(
                                        mdst, mdst,
                                        cmask[:].unsqueeze(1)
                                        .broadcast_to((128, 2, 128)), ALU.mult)
                                    tok_e = E.tk(mi, "pool")
                                exp_tok[par] = tok_e

                            def emit_av(j):
                                par = j % 2
                                ab = j % 3
                                tqs, w_ = jslice(j)
                                E.w("pe", exp_tok[par], tok_vones)
                                if j == 0:
                                    E.w("pe", evac_tok[qp])
                                aA = pe.matmul(
                                    psy[qp][0:65, tqs - tq0:512],
                                    vaug[:, j * 1056 + hA * 66:
                                         j * 1056 + hA * 66 + 65],
                                    attAB[ab][:, 0:w_],
                                    start=(j == 0), stop=(j == jmax - 1))
                                E.tk(aA, "pe")
                                aB = pe.matmul(
                                    psy[qp][0:65, 512 + tqs - tq0:1024],
                                    vaug[:, j * 1056 + hB * 66:
                                         j * 1056 + hB * 66 + 65],
                                    attAB[ab][:, 512:512 + w_],
                                    start=(j == 0), stop=(j == jmax - 1))
                                av_tok[ab] = E.tk(aB, "pe")

                            emit_scores(0)
                            emit_scores(1)
                            for j in range(2, jmax):
                                emit_av(j - 2)
                                emit_scores(j)
                            emit_av(jmax - 2)
                            emit_av(jmax - 1)
                            t_av = av_tok[(jmax - 1) % 3]
                            sp = qi % 2
                            E.w("dve", t_av, spill_tok[sp], evac_tok[sp])
                            ev = dve.tensor_copy(ytst[sp][:], psy[qp][0:65, 0:1024])
                            tok_ev = E.tk(ev, "dve")
                            evac_tok[sp] = tok_ev
                            E.w("dy", tok_ev)
                            ych = "dy2" if sp == 0 else "dy3"
                            ysc = Q * 4096 + cch * 512
                            E.dma(ych, ys_d[0:65, ysc:ysc + 512],
                                  ytst[sp][:, 0:512])
                            t_yt_dma = E.dma(ych, ys_d[65:130, ysc:ysc + 512],
                                             ytst[sp][:, 512:1024])
                            spill_tok[sp] = t_yt_dma
                            qi += 1
                    t_attn_done = evac_tok[(qi - 1) % 2]
                    E.w("pe", t_attn_done, evac_tok[qi % 2])

            if "vaug" in DBG:
                with nc.sbuf_tensor("dbgv", [128, 1056], F32) as tmpd:
                    E.w("dve", tok_vones)
                    for ti in range(16):
                        tok = E.tk(dve.tensor_copy(
                            tmpd[:], vaug[:, ti * 1056:(ti + 1) * 1056]), "dve")
                        E.w("dy", tok)
                        tok2 = E.dma("dy4",
                                     dbg_out["vaug"][:, ti * 1056:(ti + 1) * 1056],
                                     tmpd[:])
                        E.w("dve", tok2)

        acc = a(nc.sbuf_tensor("acc", [128, NCH * T], F32))
        tok_ln1 = {}
        tok_ln1_mul = {}
        with ExitStack() as s3:
            b3 = s3.enter_context
            x1b16 = b3(nc.sbuf_tensor("x1b16", [128, NCH * T], BF16))
            s3p = ExitStack()
            b3p = s3p.enter_context
            wproj16 = b3p(nc.sbuf_tensor("wproj16", [128, NCH * NCH * 128], BF16))
            ytsl = [b3p(nc.sbuf_tensor(f"ytsl{i}", [128, NCH * 512], BF16))
                    for i in range(2)]
            xfsl = [b3p(nc.sbuf_tensor(f"xfsl{i}", [128, NCH * 512], F32))
                    for i in range(2)]
            srow16 = [b3p(nc.sbuf_tensor(f"srow16_{i}", [16, 512], BF16))
                      for i in range(2)]
            srf = [b3p(nc.sbuf_tensor(f"srf{i}", [16, 512], F32)) for i in range(2)]
            rs16 = [b3p(nc.sbuf_tensor(f"rs16_{i}", [16, 512], BF16))
                    for i in range(2)]
            sel16 = b3p(nc.sbuf_tensor("sel16", [16, NCH * 128], BF16))
            ptmp = [b3p(nc.sbuf_tensor(f"ptmp{i}", [128, 512], F32)) for i in range(2)]
            sq16 = b3p(nc.sbuf_tensor("sq16", [128, NCH * 512], BF16))
            srBs2 = [b3p(nc.sbuf_tensor(f"srB{i}", [1, 512], F32)) for i in range(2)]
            srCs2 = [b3p(nc.sbuf_tensor(f"srC{i}", [1, 512], F32)) for i in range(2)]
            srBs = [srBs2[0], srBs2[1], srBs2[0], srBs2[1]]
            srCs = [srCs2[0], srCs2[1], srCs2[0], srCs2[1]]
            row_toks_ln = {}
            t_x1d = {}
            t_x1d_lo = {}

            E.w("dg", *E.bar())
            E.w("dy", *E.bar())
            E.w("dsc", *E.bar())
            t_sel = E.dma("dg0", sel16[:], sel_i)
            QWP = 2 * NCH * 128
            t_wp4 = [E.dma(("dg0", "dg1", "dg2", "dy4")[p],
                           wproj16[:, p * QWP:(p + 1) * QWP],
                           wproj_i[:, p * QWP:(p + 1) * QWP]) for p in range(4)]

            ysd3 = ys_d.rearrange("r (q c t) -> r q c t", c=NCH, t=512)

            with nc.psum_tensor("pp0", [128, 512], F32) as pp0, \
                 nc.psum_tensor("pp1", [128, 512], F32) as pp1, \
                 nc.psum_tensor("pbc0", [128, 512], F32) as pbc0, \
                 nc.psum_tensor("pbc1", [128, 512], F32) as pbc1, \
                 nc.psum_tensor("pstat", [128, 512], F32) as pstat, \
                 nc.psum_tensor("psq", [128, 512], F32) as psq, \
                 nc.psum_tensor("plb1", [128, 512], F32) as plb1, \
                 nc.psum_tensor("plb2", [128, 512], F32) as plb2:
                ppr = [pp0, pp1]
                pbc = [pbc0, pbc1]
                slice_users = [None, None]
                srow_users = [None, None]
                prev_norm = None
                rs_toks = [None, None]
                rf_toks = [None, None]
                t_srows = [None, None]
                t_ysl = [None, None]
                t_xfs_lo = [None, None]
                t_xfs = [None, None]

                def load_proj_tt(tt):
                    par = tt % 2
                    ch = "dy0" if par == 0 else "dy1"
                    ch2 = "dy1" if par == 0 else "dy0"
                    if tt >= 2:
                        E.w("dy", slice_users[par], rf_toks[par])
                        E.w("dsc", slice_users[par])
                    E.dma(ch, srow16[par][0:8, :],
                          ysd3[64:65, tt, :, :].squeeze(0))
                    t_srows[par] = E.dma(ch, srow16[par][8:16, :],
                                         ysd3[129:130, tt, :, :].squeeze(0))
                    ta = E.dma(ch, ytsl[par][0:64, :],
                               ys_d[0:64, tt * 4096:(tt + 1) * 4096])
                    tb = E.dma(ch2, ytsl[par][64:128, :],
                               ys_d[65:129, tt * 4096:(tt + 1) * 4096])
                    t_ysl[par] = (ta, tb)
                    xf3 = (x_t[:, tt * 512:(tt + 1) * 512]
                           .rearrange("(c p) t -> p c t", p=128))
                    xs3 = xfsl[par][:].rearrange("p (c t) -> p c t", t=512)
                    if tt == 0:
                        cl, chh = "dsc0", "dsc1"
                    elif tt == 1:
                        cl, chh = "dy2", "dy3"
                    else:
                        cl = "dsc0" if par == 0 else "dsc1"
                        chh = "dy2" if par == 0 else "dy3"
                    t_xfs_lo[par] = E.dma(cl, xs3[:, 0:4, :], xf3[:, 0:4, :])
                    t_xfs[par] = E.dma(chh, xs3[:, 4:8, :], xf3[:, 4:8, :])

                load_proj_tt(0)
                load_proj_tt(1)
                for tt in range(4):
                    par = tt % 2
                    ln_casts, ln_sqs = {}, {}
                    E.w("act", t_srows[par])
                    if rs_toks[par] is not None:
                        E.w("act", rs_toks[par])
                    tok_l = E.tk(act.activation(srf[par][:], srow16[par][:],
                                                AF.Ln, bias=consts[0:16, 3:4],
                                                scale=1.0), "act")
                    E.w("act", tok_l)
                    tok_rf = E.tk(act.activation(srf[par][:], srf[par][:],
                                                 AF.Exp, bias=consts[0:16, 3:4],
                                                 scale=-1.0), "act")
                    rf_toks[par] = tok_rf
                    E.w("dve", tok_rf)
                    tok_rs = E.tk(dve.tensor_copy(rs16[par][:], srf[par][:]), "dve")
                    rs_toks[par] = tok_rs
                    E.mode(("mm", "k16", "bf16"))
                    norm_toks = {}
                    for c in range(NCH):
                        cp = c % 2
                        E.w("pe", tok_rs, t_sel, t_xfs[par])
                        if c >= 2:
                            E.w("pe", norm_toks[c - 2])
                        elif prev_norm is not None:
                            E.w("pe", prev_norm)
                        bm = pe.matmul(pbc[cp][:, 0:512],
                                       sel16[:, c * 128:(c + 1) * 128],
                                       rs16[par][:], start=True, stop=True)
                        tok_bm = E.tk(bm, "pe")
                        E.w("dve", tok_bm, *t_ysl[par])
                        ni = dve.tensor_tensor(ytsl[par][:, c * 512:(c + 1) * 512],
                                               ytsl[par][:, c * 512:(c + 1) * 512],
                                               pbc[cp][:, 0:512], ALU.mult)
                        norm_toks[c] = E.tk(ni, "dve")
                    E.mode(("mm", "k128", "bf16"))
                    bias_tok = {}
                    ev_tok = {}
                    for m in range(NCH):
                        mp = m % 2
                        E.w("pe", t_wp4[m // 2],
                            norm_toks[NCH - 1], norm_toks[NCH - 2])
                        if m >= 2:
                            E.w("pe", ev_tok[m - 2])
                        mm_last = None
                        for c in range(NCH):
                            mm_last = pe.matmul(
                                ppr[mp][:, 0:512],
                                wproj16[:, (m * NCH + c) * 128:(m * NCH + c + 1) * 128],
                                ytsl[par][:, c * 512:(c + 1) * 512],
                                start=(c == 0), stop=(c == NCH - 1))
                        tok_mm = E.tk(mm_last, "pe")
                        E.w("dve", tok_mm,
                            t_xfs_lo[par] if m < 4 else t_xfs[par])
                        if m >= 2:
                            E.w("dve", bias_tok[m - 2])
                        i1 = dve.tensor_tensor(ptmp[mp][:], ppr[mp][:, 0:512],
                                               xfsl[par][:, m * 512:(m + 1) * 512],
                                               ALU.add)
                        ev_tok[m] = E.tk(i1, "dve")
                        E.w("act", ev_tok[m], t_bproj)
                        i2 = act.activation(
                            acc[:, m * T + tt * 512:m * T + (tt + 1) * 512],
                            ptmp[mp][:], AF.Identity, bias=bproj[:, m:m + 1],
                            scale=1.0)
                        bias_tok[m] = E.tk(i2, "act")
                        ln_casts[m], ln_sqs[m] = _ln_cast(
                            nc, E, acc, x1b16, sq16, consts, tt, m,
                            bias_tok[m], prev_norm, cast_eng="pool")
                    slice_users[par] = ev_tok[NCH - 1]
                    if tt + 2 < 4:
                        load_proj_tt(tt + 2)
                    if tt >= 1:
                        tok_n, pm_ = _ln_apply(nc, E, acc, x1b16, srBs[tt - 1],
                                               srCs[tt - 1], onesrf, tt - 1,
                                               row_toks_ln[tt - 1], True,
                                               plb1, plb2, consts=consts,
                                               ret_per_m=True)
                        tok_ln1_mul[tt - 1] = pm_[-1]
                        tok_ln1[tt - 1] = tok_n
                        prev_norm = tok_n
                        E.w("dy", tok_n)
                        x1dv = (x1_d[:, (tt - 1) * 4096:tt * 4096]
                                .rearrange("p (c t) -> p c t", t=512))
                        x1bv = (x1b16[:]
                                .rearrange("p (c q t) -> p c q t", q=4, t=512)
                                [:, :, tt - 1, :])
                        t_x1d_lo[tt - 1] = E.dma("dy3", x1dv[:, 0:4, :],
                                                 x1bv[:, 0:4, :])
                        t_x1d[tt - 1] = E.dma("dy4", x1dv[:, 4:8, :],
                                              x1bv[:, 4:8, :])
                    row_toks_ln[tt] = _ln_stats(nc, E, acc, x1b16, sq16,
                                                srBs[tt], srCs[tt],
                                                onesc, consts, tt, bias_tok,
                                                pstat, psq, prev_norm,
                                                cast_toks=ln_casts, sq_toks=ln_sqs)
                tok_n, pm_ = _ln_apply(nc, E, acc, x1b16, srBs[3], srCs[3],
                                       onesrf, 3, row_toks_ln[3], True,
                                       plb1, plb2, consts=consts,
                                       ret_per_m=True)
                tok_ln1_mul[3] = pm_[-1]
                tok_ln1[3] = tok_n
                E.w("dy", tok_n)
                x1dv = (x1_d[:, 3 * 4096:4 * 4096]
                        .rearrange("p (c t) -> p c t", t=512))
                x1bv = (x1b16[:].rearrange("p (c q t) -> p c q t", q=4, t=512)
                        [:, :, 3, :])
                t_x1d_lo[3] = E.dma("dy3", x1dv[:, 0:4, :], x1bv[:, 0:4, :])
                t_x1d[3] = E.dma("dy4", x1dv[:, 4:8, :], x1bv[:, 4:8, :])
                E.w("pe", tok_ln1[3])
            s3p.close()
            if "x1" in DBG:
                with nc.sbuf_tensor("dbgx1", [128, T], F32) as tmpd:
                    for c in range(NCH):
                        E.w("dve", *tok_ln1.values())
                        tok = E.tk(dve.tensor_copy(tmpd[:],
                                                   acc[:, c * T:(c + 1) * T]), "dve")
                        E.w("dy", tok)
                        tok2 = E.dma("dy4", dbg_out["x1"][c * 128:(c + 1) * 128, :],
                                     tmpd[:])
                        E.w("dve", tok2)

            sct = b3(nc.sbuf_tensor("sct", [NEXP, T], F32))
            ident = b3(nc.sbuf_tensor("ident_s", [128, 128], F32))
            t_id = E.dma("dy4", ident[:], ident_i)
            sctm = b3(nc.sbuf_tensor("sctm", [128, 16 * NEXP], F32))
            rm1 = b3(nc.sbuf_tensor("rm1", [128, 16], F32))
            rm2 = b3(nc.sbuf_tensor("rm2", [128, 16], F32))
            rs2 = b3(nc.sbuf_tensor("rs2", [128, 16 * NEXP], F32))
            req1 = b3(nc.sbuf_tensor("req1", [128, 16 * NEXP], F32))
            req2 = b3(nc.sbuf_tensor("req2", [128, 16 * NEXP], F32))
            rp1 = b3(nc.sbuf_tensor("rp1", [128, 16], F32))
            rp2 = b3(nc.sbuf_tensor("rp2", [128, 16], F32))
            rtt1 = b3(nc.sbuf_tensor("rtt1", [128, 16 * NEXP], F32))
            cwtm = b3(nc.sbuf_tensor("cwtm", [128, 16 * NEXP], F32))

            def r3(ap):
                return ap.rearrange("p (t e) -> p t e", e=NEXP)

            with nc.psum_tensor("prt0", [128, 512], F32) as prt0, \
                 nc.psum_tensor("prt1", [128, 512], F32) as prt1:
                E.mode(("mm", "k128m8", "f32"))
                sc_ev = []
                for tt in range(4):
                    par = tt % 2
                    if tt >= 2:
                        E.w("pe", sc_ev[tt - 2])
                    E.w("pe", t_rwt, tok_ln1_mul[tt])
                    mm_last = None
                    for c in range(NCH):
                        mm_last = pe.matmul(
                            [prt0, prt1][par][0:NEXP, 0:512],
                            rwt[:, c * NEXP:(c + 1) * NEXP],
                            acc[:, c * T + tt * 512:c * T + (tt + 1) * 512],
                            start=(c == 0), stop=(c == NCH - 1))
                    tok_mm = E.tk(mm_last, "pe")
                    E.w("dve", tok_mm)
                    iv = dve.tensor_copy(sct[:, tt * 512:(tt + 1) * 512],
                                         [prt0, prt1][par][0:NEXP, 0:512])
                    sc_ev.append(E.tk(iv, "dve"))
                E.w("pe", *sc_ev)

            with nc.psum_tensor("prt2", [128, 128], F32) as prt2:
                E.mode(("tr", "f32"))
                E.w("pe", t_id, *sc_ev)
                tr_tok = None
                for ti in range(16):
                    tr = pe.transpose(prt2[:, ti * 8:(ti + 1) * 8],
                                      sct[:, ti * 128:(ti + 1) * 128],
                                      ident[0:8, 0:8])
                    tr_tok = E.tk(tr, "pe")
                E.w("dve", tr_tok)
                tok_sctm = E.tk(dve.tensor_copy(sctm[:], prt2[:, 0:128]), "dve")
                E.w("pe", tok_sctm)

            E.w("dve", tok_sctm, t_rbrep)
            tok = E.tk(dve.tensor_tensor(
                r3(sctm[:]), r3(sctm[:]),
                rbrep[:].unsqueeze(1).broadcast_to((128, 16, NEXP)),
                ALU.add), "dve")
            E.w("dve", tok)
            tok_m1 = E.tk(dve.reduce_max(rm1[:], r3(sctm[:]), axis=AX.X), "dve")
            E.w("dve", tok_m1)
            m1b = rm1[:].unsqueeze(2).broadcast_to((128, 16, NEXP))
            tok_e1 = E.tk(dve.tensor_tensor(r3(req1[:]), r3(sctm[:]), m1b,
                                            ALU.is_ge), "dve")
            E.w("dve", tok_e1)
            tok_s2 = E.tk(dve.scalar_tensor_tensor(r3(rs2[:]), r3(req1[:]), -1e30,
                                                   r3(sctm[:]), ALU.mult, ALU.add),
                          "dve")
            E.w("dve", tok_s2)
            tok_m2 = E.tk(dve.reduce_max(rm2[:], r3(rs2[:]), axis=AX.X), "dve")
            E.w("dve", tok_m2)
            tok_d = E.tk(dve.tensor_tensor(rp1[:], rm1[:], rm2[:], ALU.subtract),
                         "dve")
            E.w("act", tok_d)
            tok_p1 = E.tk(act.activation(rp1[:], rp1[:], AF.Sigmoid,
                                         bias=consts[:, 3:4], scale=1.0), "act")
            E.w("act", tok_p1)
            tok_p2 = E.tk(act.activation(rp2[:], rp1[:], AF.Identity,
                                         bias=consts[:, 2:3], scale=-1.0), "act")
            E.w("dve", tok_m2, tok_s2)
            m2b = rm2[:].unsqueeze(2).broadcast_to((128, 16, NEXP))
            tok_e2 = E.tk(dve.tensor_tensor(r3(req2[:]), r3(rs2[:]), m2b, ALU.is_ge),
                          "dve")
            E.w("dve", tok_p1, tok_e1)
            p1b = rp1[:].unsqueeze(2).broadcast_to((128, 16, NEXP))
            tok_t1 = E.tk(dve.tensor_tensor(r3(rtt1[:]), r3(req1[:]), p1b, ALU.mult),
                          "dve")
            E.w("dve", tok_p2, tok_e2)
            p2b = rp2[:].unsqueeze(2).broadcast_to((128, 16, NEXP))
            tok_t2 = E.tk(dve.tensor_tensor(r3(req2[:]), r3(req2[:]), p2b, ALU.mult),
                          "dve")
            E.w("dve", tok_t1, tok_t2)
            tok_cw = E.tk(dve.tensor_tensor(cwtm[:], rtt1[:], req2[:], ALU.add),
                          "dve")
            if "cw" in DBG:
                E.w("dy", tok_cw)
                tok2 = E.dma("dy4", dbg_out["cw"], cwtm[:])
                E.w("dve", tok2)

            with nc.psum_tensor("prt3", [128, 2048], F32) as prt3:
                E.mode(("tr", "f32"))
                E.w("pe", tok_cw)
                tr_tok = None
                for ti in range(16):
                    tr = pe.transpose(prt3[0:NEXP, ti * 128:(ti + 1) * 128],
                                      cwtm[:, ti * 8:(ti + 1) * 8], ident[:, 0:128])
                    tr_tok = E.tk(tr, "pe")
                E.w("dve", tr_tok)
                tok_cwT16 = E.tk(dve.tensor_copy(cwT16[:], prt3[0:NEXP, 0:2048]),
                                 "dve")
                E.w("pe", tok_cwT16)

        dbias_ev = {}
        with ExitStack() as s5:
            b5 = s5.enter_context
            x1sl = [b5(nc.sbuf_tensor(f"x1sl{i}", [128, NCH * 512], BF16))
                    for i in range(2)]
            wgb = [b5(nc.sbuf_tensor(f"wg16_{i}", [128, 16 * NCH * 128], BF16))
                   for i in range(2)]
            wdb = b5(nc.sbuf_tensor("wd16", [128, NCH * NCH * 128], BF16))
            asb = [b5(nc.sbuf_tensor(f"as16_{i}", [128, NCH * 512], BF16))
                   for i in range(2)]
            cwb = b5(nc.sbuf_tensor("cwb16", [128, T], BF16))
            cwrow0 = b5(nc.sbuf_tensor("cwrow0", [1, T], BF16))
            cwrow = [cwrow0, cwrow0]
            sg_t = [b5(nc.sbuf_tensor(f"sgt{i}", [128, 512], BF16)) for i in range(2)]
            srB2s2 = [b5(nc.sbuf_tensor(f"srB2_{i}", [1, 512], F32))
                      for i in range(2)]
            srC2s2 = [b5(nc.sbuf_tensor(f"srC2_{i}", [1, 512], F32))
                      for i in range(2)]
            srB2s = [srB2s2[0], srB2s2[1], srB2s2[0], srB2s2[1]]
            srC2s = [srC2s2[0], srC2s2[1], srC2s2[0], srC2s2[1]]
            gc_t = [b5(nc.sbuf_tensor(f"gct{i}", [128, 512], BF16)) for i in range(2)]
            uc_t = [b5(nc.sbuf_tensor(f"uct{i}", [128, 512], BF16)) for i in range(2)]

            def load_x1(k, e, tt, split=False):
                ch = "dy0" if k % 2 == 0 else "dy1"
                E.w("dy", t_x1d_lo.get(tt), t_x1d.get(tt))
                dst = x1sl[k % 2]
                base = tt * 4096
                if split:
                    t_x1lo[k % 2] = E.dma("dy4" if k % 2 == 0 else "dy3",
                                          dst[:, 0:2048],
                                          x1_d[:, base:base + 2048])
                    return E.dma(ch, dst[:, 2048:4096],
                                 x1_d[:, base + 2048:base + 4096])
                return E.dma(ch, dst[:], x1_d[:, base:base + 4096])

            t_x1lo = [None, None]
            t_x1sl = [None, None]
            E.w("dy", t_x1d_lo[3], t_x1d[3])
            t_x1sl[0] = load_x1(0, 0, 0, split=True)
            t_x1sl[1] = load_x1(1, 0, 1, split=True)

            def load_wg(e, war=None):
                if war is not None:
                    E.w("dg", war)
                    E.w("dy", war)
                HW_ = 8 * NCH * 128
                lo = E.dma("dg1", wgb[e % 2][:, 0:HW_], wg_i[e][:, 0:HW_])
                hi = E.dma("dy2", wgb[e % 2][:, HW_:2 * HW_],
                           wg_i[e][:, HW_:2 * HW_])
                return (lo, hi)

            E.w("dg", tok_ln1_mul[3], t_x1d_lo[3], t_x1d[3])
            HW0 = 8 * NCH * 128
            _glo = E.dma("dg1", wgb[0][:, 0:HW0], wg_i[0][:, 0:HW0])

            E.w("dg", *E.bar())
            E.w("dsc", *E.bar())
            E.w("dy", *E.bar())
            _ghi = E.dma("dy2", wgb[0][:, HW0:2 * HW0], wg_i[0][:, HW0:2 * HW0])
            t_wg = [(_glo, _ghi)] + [None] * (NEXP - 1)
            _t0 = E.dma("dg2", wdb[:], wd_i[0])
            t_wd = [[_t0, _t0]] + [None] * (NEXP - 1)
            E.w("dg", tok_cwT16)
            t_cwrow = [E.dma("dg0", cwrow[0][:], cwT16[0:1, :]), None]


            with nc.psum_tensor("pg0", [128, 512], F32) as pg0, \
                 nc.psum_tensor("pg1", [128, 512], F32) as pg1, \
                 nc.psum_tensor("pu0", [128, 512], F32) as pu0, \
                 nc.psum_tensor("pu1", [128, 512], F32) as pu1, \
                 nc.psum_tensor("pd0", [128, 512], F32) as pd0, \
                 nc.psum_tensor("pd1", [128, 512], F32) as pd1, \
                 nc.psum_tensor("pcw", [128, 1024], F32) as pcw:
                pg, pu, pd = [pg0, pg1], [pu0, pu1], [pd0, pd1]
                swi_toks, down_ev = {}, {}
                down_hist = []
                pairs = [(e, tt) for e in range(NEXP) for tt in range(4)]
                gate_last = None

                def emit_down(k, m2_):
                    e_, tt_ = pairs[k]
                    tp_ = k % 2
                    dp = len(down_hist) % 2
                    if len(down_hist) >= 2:
                        E.w("pe", down_hist[-2])
                    twd = t_wd[e_]
                    E.w("pe", twd[0] if m2_ < 4 else twd[1],
                        *[swi_toks[(e_, tt_, hcx)][0] for hcx in range(NCH)])
                    mmd = None
                    for hc_ in range(NCH):
                        mmd = pe.matmul(
                            pd[dp][:, 0:512],
                            wdb[:, (m2_ * NCH + hc_) * 128:(m2_ * NCH + hc_ + 1) * 128],
                            asb[tp_][:, hc_ * 512:(hc_ + 1) * 512],
                            start=(hc_ == 0), stop=(hc_ == NCH - 1))
                    tok_dn = E.tk(mmd, "pe")
                    sl = slice(m2_ * T + tt_ * 512, m2_ * T + (tt_ + 1) * 512)
                    E.w("dve", tok_dn)
                    if e_ == 0:
                        E.w("dve", tok_ln1[tt_])
                    tok = E.tk(dve.tensor_tensor(acc[:, sl], acc[:, sl],
                                                 pd[dp][:, 0:512], ALU.add), "dve")
                    down_ev[(e_, tt_, m2_)] = tok
                    down_hist.append(tok)
                    if tt_ == 3 and e_ + 1 < NEXP and m2_ in (3, NCH - 1):
                        HWD = 4 * NCH * 128
                        half = 0 if m2_ == 3 else 1
                        E.w("dg", tok_dn)
                        tok_l = E.dma("dg2",
                                      wdb[:, half * HWD:(half + 1) * HWD],
                                      wd_i[e_ + 1][:, half * HWD:(half + 1) * HWD])
                        if half == 0:
                            t_wd[e_ + 1] = [tok_l, None]
                        else:
                            t_wd[e_ + 1][1] = tok_l

                rows2 = {}
                ln2_apply = {}
                tok_st = {}
                tail_casts = {}

                def stage_casts(s):
                    sp = s % 2
                    ycast = wgb[0][:, sp * 8192:sp * 8192 + 4096]
                    sqali = wgb[0][:, sp * 8192 + 4096:(sp + 1) * 8192]
                    prevu = rows2.get(s - 2)
                    prev = prevu[0] if prevu else pair_gate_last.get(27)
                    casts, sqs = {}, {}
                    for m in range(NCH):
                        casts[m], sqs[m] = _ln_cast(nc, E, acc, ycast, sqali,
                                                    consts, s, m,
                                                    down_ev[(NEXP - 1, s, m)],
                                                    prev, bstride=512,
                                                    cast_eng="pool")
                    tail_casts[s] = (casts, sqs)

                def stage_stats(s, pstat, psq, *war):
                    sp = s % 2
                    ycast = wgb[0][:, sp * 8192:sp * 8192 + 4096]
                    sqali = wgb[0][:, sp * 8192 + 4096:(sp + 1) * 8192]
                    casts, sqs = tail_casts[s]
                    E.w("pe", *[t for t in war if t is not None])
                    rows2[s] = _ln_stats(nc, E, acc, ycast, sqali,
                                         srB2s[s], srC2s[s], onesc, consts,
                                         s, None, pstat, psq, None, bstride=512,
                                         cast_toks=casts, sq_toks=sqs)

                def stage_apply(s, pb1, pb2, *war):
                    E.w("pe", *[t for t in war if t is not None])
                    tok_n, per_m = _ln_apply(nc, E, acc, None, srB2s[s],
                                             srC2s[s], onesrf, s, rows2[s],
                                             False, pb1, pb2, bstride=512,
                                             ret_per_m=True)
                    ln2_apply[s] = tok_n
                    if s == 3:
                        chans = ("dy4", "dy3", "dy1", "dy0",
                                 "dsc0", "dsc1", "dy2", "dg0")
                    else:
                        chans = ("dy4", "dy3", "dy1", "dy0",
                                 "dy2", "dy4", "dy3", "dy1")
                    for m in range(NCH):
                        ch = chans[m]
                        eng = ("dsc" if ch.startswith("dsc")
                               else "dg" if ch.startswith("dg") else "dy")
                        E.w(eng, per_m[m])
                        tok_st[ch] = E.dma(ch,
                                           out[m * 128:(m + 1) * 128,
                                               s * 512:(s + 1) * 512],
                                           acc[:, m * T + s * 512:
                                               m * T + (s + 1) * 512])

                tok_cwb_prev = None
                tok_cwb = None
                pair_gate_last = {}
                gexp_last = {}
                for k, (e, tt) in enumerate(pairs):
                    ep = e % 2
                    if tt == 0:
                        E.mode(("mm", "k1", "bf16"))
                        E.w("pe", t_cwrow[ep], t_or, tok_cwb_prev)
                        for wv_ in range(2):
                            mmc = None
                            for hh in range(2):
                                mmc = pe.matmul(
                                    pcw[:, hh * 512:(hh + 1) * 512], onesr[0:1, :],
                                    cwrow[ep][0:1, wv_ * 1024 + hh * 512:
                                              wv_ * 1024 + (hh + 1) * 512],
                                    start=True, stop=True)
                            tok_mmc = E.tk(mmc, "pe")
                            E.w("dve", tok_mmc)
                            if tok_cwb_prev is not None and wv_ == 0:
                                E.w("dve", swi_toks.get((e - 1, 3, 7), (None,))[0])
                            tok_cwb = E.tk(dve.tensor_copy(
                                cwb[:, wv_ * 1024:(wv_ + 1) * 1024], pcw[:]), "dve")
                            if wv_ == 0:
                                E.w("pe", tok_cwb)
                        tok_cwb_prev = tok_cwb
                        if e + 1 < NEXP:
                            E.w("dg", tok_mmc)
                            t_cwrow[(e + 1) % 2] = E.dma(
                                "dg0", cwrow[(e + 1) % 2][:],
                                cwT16[e + 1:e + 2, :])
                        if 1 <= e <= 4:
                            s_ = e - 1
                            E.mode(("mm", "k8", "bf16"))
                            E.w("pe", t_dbt, tok_cwT16)
                            for m_ in range(NCH):
                                dp_ = len(down_hist) % 2
                                if len(down_hist) >= 2:
                                    E.w("pe", down_hist[-2])
                                mmb = pe.matmul(
                                    pd[dp_][:, 0:512],
                                    dbt16[0:NEXP, m_ * 128:(m_ + 1) * 128],
                                    cwT16[0:NEXP, s_ * 512:(s_ + 1) * 512],
                                    start=True, stop=True)
                                tok_b = E.tk(mmb, "pe")
                                sl_ = slice(m_ * T + s_ * 512,
                                            m_ * T + (s_ + 1) * 512)
                                E.w("dve", tok_b)
                                tok_db = E.tk(dve.tensor_tensor(
                                    acc[:, sl_], acc[:, sl_],
                                    pd[dp_][:, 0:512], ALU.add), "dve")
                                down_hist.append(tok_db)
                        E.mode(("mm", "k128", "bf16"))
                    tp = k % 2
                    for hc in range(NCH):
                        hp = hc % 2
                        E.w("pe", t_wg[e][0], t_x1sl[k % 2],
                            t_x1lo[k % 2] if k < 2 else None)
                        prevk = None
                        if hc >= 2:
                            prevk = (e, tt, hc - 2)
                        elif k > 0:
                            prevk = pairs[k - 1] + (hc + 6,)
                        if prevk and prevk in swi_toks:
                            E.w("pe", swi_toks[prevk][1])
                        mmg = mmu = None
                        for c in range(NCH):
                            mmg = pe.matmul(
                                pg[hp][:, 0:512],
                                wgb[ep][:, (hc * NCH + c) * 128:(hc * NCH + c + 1) * 128],
                                x1sl[k % 2][:, c * 512:(c + 1) * 512],
                                start=(c == 0), stop=(c == NCH - 1))
                        tok_g = E.tk(mmg, "pe")
                        E.w("pe", t_wg[e][1])
                        for c in range(NCH):
                            mmu = pe.matmul(
                                pu[hp][:, 0:512],
                                wgb[ep][:, ((hc + 8) * NCH + c) * 128:
                                    ((hc + 8) * NCH + c + 1) * 128],
                                x1sl[k % 2][:, c * 512:(c + 1) * 512],
                                start=(c == 0), stop=(c == NCH - 1))
                        tok_u = E.tk(mmu, "pe")
                        gate_last = tok_u
                        bg = e * 16 + hc
                        bu = e * 16 + 8 + hc
                        E.w("act", tok_g, tok_u, t_gubs, t_gub7)
                        if prevk and prevk in swi_toks:
                            E.w("act", swi_toks[prevk][0])
                        tok_sg = E.tk(act.activation(sg_t[hp][:], pg[hp][:],
                                                     AF.Sigmoid,
                                                     bias=gubs[:, bg:bg + 1],
                                                     scale=1.702), "act")
                        tok_rg = E.tk(act.activation(gc_t[hp][:], pg[hp][:],
                                                     AF.Relu,
                                                     bias=gub7[:, bg:bg + 1],
                                                     scale=1.0), "act")
                        tok_ru = E.tk(act.activation(uc_t[hp][:], pu[hp][:],
                                                     AF.Relu,
                                                     bias=gub7[:, bu:bu + 1],
                                                     scale=1.0), "act")
                        E.w("dve", tok_rg, tok_ru)
                        if prevk and prevk in swi_toks:
                            E.w("dve", swi_toks[prevk][0])
                        tok_gc = E.tk(dve.tensor_scalar(gc_t[hp][:], gc_t[hp][:],
                                                        -7.0, 7.0,
                                                        ALU.add, ALU.min), "dve")
                        tok_uc = E.tk(dve.tensor_scalar(uc_t[hp][:], uc_t[hp][:],
                                                        -6.0, 8.0,
                                                        ALU.add, ALU.min), "dve")
                        E.w("dve", tok_gc, tok_sg)
                        tok_m1 = E.tk(dve.tensor_tensor(gc_t[hp][:], gc_t[hp][:],
                                                        sg_t[hp][:], ALU.mult), "dve")
                        E.w("dve", tok_uc, tok_cwb)
                        tok_m2 = E.tk(dve.tensor_tensor(
                            uc_t[hp][:], uc_t[hp][:],
                            cwb[:, tt * 512:(tt + 1) * 512], ALU.mult), "dve")
                        prev_dn = None
                        if k >= 2:
                            pk2 = pairs[k - 2] + (NCH - 1,)
                            prev_dn = down_ev.get(pk2)
                        E.w("dve", tok_m1, tok_m2, prev_dn)
                        tok_as = E.tk(dve.tensor_tensor(
                            asb[tp][:, hc * 512:(hc + 1) * 512],
                            gc_t[hp][:], uc_t[hp][:], ALU.mult), "dve")
                        swi_toks[(e, tt, hc)] = (tok_as, tok_ru)
                        if k > 0:
                            emit_down(k - 1, hc)
                    pair_gate_last[k] = gate_last
                    if k + 2 < len(pairs):
                        e2, tt2 = pairs[k + 2]
                        E.w("dy", pair_gate_last[k])
                        t_x1sl[k % 2] = load_x1(k + 2, e2, tt2)
                    if tt == 0 and e + 1 < NEXP:
                        t_wg[e + 1] = load_wg(
                            e + 1, pair_gate_last.get(4 * (e - 1) + 3)
                            if e >= 1 else None)
                    if e == NEXP - 1:
                        if tt == 1:
                            stage_casts(0)
                        elif tt == 2:
                            stage_casts(1)
                            stage_stats(0, pcw[:, 0:512], pcw[:, 512:1024],
                                        tok_cwb)
                        elif tt == 3:
                            stage_casts(2)
                            swi31 = [swi_toks[(7, 3, h)][0] for h in range(NCH)]
                            stage_stats(1, pg0[:], pg1[:], *swi31)
                            stage_apply(0, pcw[:, 0:512], pcw[:, 512:1024])
                        E.mode(("mm", "k128", "bf16"))
                for m2_ in range(NCH):
                    emit_down(len(pairs) - 1, m2_)
                stage_casts(3)
                stage_apply(1, pu0[:], pu1[:], *swi31)
                stage_stats(2, pcw[:, 0:512], pcw[:, 512:1024], ln2_apply[0])
                stage_stats(3, pd0[:], pd1[:], down_ev[(7, 3, NCH - 1)])
                stage_apply(2, pg0[:], pg1[:], *rows2[1])
                stage_apply(3, pcw[:, 0:512], pcw[:, 512:1024], *rows2[2])
                for chv, tokv in tok_st.items():
                    eng = ("dsc" if chv.startswith("dsc")
                           else "dg" if chv.startswith("dg") else "dy")
                    E.w(eng, tokv)
                E.w("pe", ln2_apply[3])

        if "accmoe" in DBG:
            with nc.sbuf_tensor("dbgacc", [128, T], F32) as tmpd:
                for c in range(NCH):
                    E.w("dve", dbias_ev[(c, 3)])
                    tok = E.tk(dve.tensor_copy(tmpd[:], acc[:, c * T:(c + 1) * T]),
                               "dve")
                    E.w("dy", tok)
                    tok2 = E.dma("dy4", dbg_out["accmoe"][c * 128:(c + 1) * 128, :],
                                 tmpd[:])
                    E.w("dve", tok2)

    return nc


def kernel(L0_attn_x, L0_attn_W_attn, L0_attn_b_attn, L0_attn_W_proj, L0_attn_b_proj,
           L0_ffn_router_w, L0_ffn_router_b, L0_ffn_gate_up_w, L0_ffn_gate_up_b,
           L0_ffn_down_w, L0_ffn_down_b):
    args = [np.asarray(v, np.float32) for v in
            (L0_attn_x, L0_attn_W_attn, L0_attn_b_attn, L0_attn_W_proj,
             L0_attn_b_proj, L0_ffn_router_w, L0_ffn_router_b, L0_ffn_gate_up_w,
             L0_ffn_gate_up_b, L0_ffn_down_w, L0_ffn_down_b)]
    in_maps = _host_layout(*args)
    nc = build_nc()
    res = run_bass_kernel_spmd(nc, in_maps, core_ids=list(range(8)))
    return np.stack([np.ascontiguousarray(r["out"].T) for r in res.results], axis=0)


# version 54
# speedup vs baseline: 1.0045x; 1.0045x over previous
import numpy as np
import ml_dtypes
import concourse.bass as bass
import concourse.mybir as mybir
from concourse.bass_utils import run_bass_kernel_spmd
from contextlib import ExitStack

F32 = mybir.dt.float32
BF16 = mybir.dt.bfloat16
AF = mybir.ActivationFunctionType
ALU = mybir.AluOpType
AX = mybir.AxisListType

T = 2048
D = 1024
NCH = 8
NEXP = 8
NEG = -20.0
EPS = 1e-5

DBG = []


class Emit:
    DMACH = ("dy0", "dy1", "dy2", "dy3", "dy4", "dg0", "dg1", "dg2",
             "dsc0", "dsc1")

    def __init__(self, nc, stack):
        self.nc = nc
        self.sem = {}
        for k in ("pe", "act", "dve", "pool") + self.DMACH:
            self.sem[k] = stack.enter_context(nc.semaphore("sem_" + k))
        self.cnt = {k: 0 for k in self.sem}
        self.wcache = {}
        self.pe_mode = None
        self.eng = {"pe": nc.tensor, "act": nc.scalar, "dve": nc.vector,
                    "pool": nc.gpsimd, "dg": nc.gpsimd, "dy": nc.sync,
                    "dsc": nc.scalar}
        self.dmaeng = {}
        for ch in self.DMACH:
            self.dmaeng[ch] = (nc.sync if ch.startswith("dy")
                               else nc.gpsimd if ch.startswith("dg")
                               else nc.scalar)

    def tk(self, instr, key, step=1):
        self.cnt[key] += step
        instr.then_inc(self.sem[key], step)
        return (key, self.cnt[key])

    def w(self, consumer, *tokens):
        eng = self.eng[consumer]
        best = {}
        for tok in tokens:
            if tok is None:
                continue
            key, val = tok
            if val > best.get(key, -1):
                best[key] = val
        for key, val in best.items():
            ck = (consumer, key)
            if self.wcache.get(ck, 0) >= val:
                continue
            eng.wait_ge(self.sem[key], val)
            self.wcache[ck] = val

    def bar(self):
        return tuple((k, self.cnt[k]) for k in self.cnt)

    def mode(self, m):
        if self.pe_mode != m:
            self.nc.tensor.drain()
            self.pe_mode = m

    def dma(self, ch, out, in_):
        eng = self.dmaeng[ch]
        if self.cnt[ch]:
            ck = ("q" + ch, ch)
            if self.wcache.get(ck, 0) < self.cnt[ch]:
                eng.wait_ge(self.sem[ch], self.cnt[ch])
                self.wcache[ck] = self.cnt[ch]
        return self.tk(eng.dma_start(out=out, in_=in_), ch, 16)


def _host_layout(x, W_attn, b_attn, W_proj, b_proj,
                 router_w, router_b, gate_up_w, gate_up_b, down_w, down_b):
    fl = np.ascontiguousarray
    DH = 64
    head_perm = np.concatenate([np.arange(0, DH, 2), np.arange(1, DH, 2)])
    qk_cols = []
    for m in range(16):
        base = 0 if m < 8 else D
        hA = 2 * (m % 8)
        qk_cols.append(np.concatenate([base + hA * DH + head_perm,
                                       base + (hA + 1) * DH + head_perm]))
    qk_cols = np.stack(qk_cols)

    wqk = np.empty((128, 16 * NCH * 128), np.float32)
    for m in range(16):
        sub = W_attn[:, qk_cols[m]].reshape(NCH, 128, 128)
        wqk[:, m * NCH * 128:(m + 1) * NCH * 128] = \
            np.transpose(sub, (1, 0, 2)).reshape(128, NCH * 128)
    bqk = np.stack([b_attn[qk_cols[m]] for m in range(16)], axis=1)

    wv = fl(np.transpose(W_attn[:, 2 * D:3 * D].reshape(NCH, 128, D),
                         (1, 0, 2)).reshape(128, NCH * D))
    bv = fl(b_attn[2 * D:3 * D][None, :])

    tmp = np.transpose(W_proj.reshape(NCH, 128, NCH, 128), (1, 2, 0, 3))
    wproj = fl(tmp.reshape(128, NCH * NCH * 128))
    bproj = fl(b_proj.reshape(NCH, 128).T)

    rwt = fl(np.transpose(router_w.reshape(NEXP, NCH, 128), (2, 1, 0))
             .reshape(128, NCH * NEXP))
    rbr = fl(router_b[None, :])

    wg = np.empty((NEXP, 128, 16 * NCH * 128), np.float32)
    gub = np.empty((128, NEXP * 16), np.float32)
    for e in range(NEXP):
        gu = np.empty((16, NCH, 128, 128), np.float32)
        for m in range(16):
            if m < 8:
                rows = 2 * (m * 128 + np.arange(128))
            else:
                rows = 2 * ((m - 8) * 128 + np.arange(128)) + 1
            sub = gate_up_w[e][rows, :]
            gu[m] = np.transpose(sub.reshape(128, NCH, 128), (1, 2, 0))
            gub[:, e * 16 + m] = gate_up_b[e][rows]
        wg[e] = np.transpose(gu, (2, 0, 1, 3)).reshape(128, 16 * NCH * 128)

    wd = np.empty((NEXP, 128, NCH * NCH * 128), np.float32)
    for e in range(NEXP):
        dwe = np.empty((NCH, NCH, 128, 128), np.float32)
        for m2 in range(NCH):
            sub = down_w[e][m2 * 128:(m2 + 1) * 128, :]
            dwe[m2] = np.transpose(sub.reshape(128, NCH, 128), (1, 2, 0))
        wd[e] = np.transpose(dwe, (2, 0, 1, 3)).reshape(128, NCH * NCH * 128)
    dbt = fl(down_b)

    i32 = np.arange(32, dtype=np.float64)
    theta = 10000.0 ** (-i32 / 32.0)
    ang = np.arange(T, dtype=np.float64)[:, None] * theta[None, :]
    cos = np.cos(ang).T.astype(np.float32)
    sin = np.sin(ang).T.astype(np.float32)
    cc = np.concatenate([cos, cos, cos, cos], axis=0)
    ss = np.concatenate([-sin, sin, -sin, sin], axis=0)
    kq = np.arange(128)
    cmask = (kq[None, :] >= kq[:, None]).astype(np.float32)
    ident = np.eye(128, dtype=np.float32)
    onesc = np.ones((128, 1), np.float32)
    onesr = np.ones((1, 128), np.float32)
    consts = np.zeros((128, 8), np.float32)
    consts[:, 0] = NEG
    consts[:, 1] = EPS
    consts[:, 2] = 1.0
    consts[:, 4] = 7.0
    sel = np.zeros((16, NCH * 128), np.float32)
    for c in range(NCH):
        sel[c, c * 128:c * 128 + 64] = 1.0
        sel[8 + c, c * 128 + 64:(c + 1) * 128] = 1.0

    bf = ml_dtypes.bfloat16
    wqk, wv, wproj = wqk.astype(bf), wv.astype(bf), wproj.astype(bf)
    wg, wd = wg.astype(bf), wd.astype(bf)
    shared = dict(wqk=wqk, bqk=bqk, wv=wv, bv=bv, wproj=wproj, bproj=bproj,
                  rwt=rwt, rbr=rbr, wg=wg, gub=gub, wd=wd, dbt=dbt,
                  cc=cc, ss=ss, cmask=cmask, ident=ident,
                  onesc=onesc, onesr=onesr, consts=consts, sel=sel)
    per_core = []
    for b in range(x.shape[0]):
        m = dict(shared)
        m["x_t"] = fl(x[b].T)
        per_core.append(m)
    return per_core


def _ln_cast(nc, E, acc, bf16buf, sq16, consts, tt, m, ready_m, prev_apply,
             bstride=None, cast_eng="dve"):
    act = nc.scalar
    eng = nc.gpsimd if cast_eng == "pool" else nc.vector
    sl = slice(m * T + tt * 512, m * T + (tt + 1) * 512)
    bsl = (slice(m * T + tt * 512, m * T + (tt + 1) * 512) if bstride is None
           else slice(m * bstride, m * bstride + 512))
    E.w("act", ready_m, prev_apply)
    sq_tok = E.tk(act.activation(sq16[:, m * 512:(m + 1) * 512],
                                 acc[:, sl], AF.Square,
                                 bias=consts[:, 3:4], scale=1.0), "act")
    E.w(cast_eng, ready_m, prev_apply)
    cast_tok = E.tk(eng.tensor_copy(bf16buf[:, bsl], acc[:, sl]), cast_eng)
    return cast_tok, sq_tok


def _ln_stats(nc, E, acc, bf16buf, sq16, srB, srC, onesc, consts,
              tt, ready, pstat, psq, prev_apply, bstride=None,
              cast_toks=None, sq_toks=None):
    pe, act, dve = nc.tensor, nc.scalar, nc.vector

    def bsl(m):
        if bstride is None:
            return slice(m * T + tt * 512, m * T + (tt + 1) * 512)
        return slice(m * bstride, m * bstride + 512)
    if cast_toks is None:
        cast_toks, sq_toks = {}, {}
        for m in range(NCH):
            cast_toks[m], sq_toks[m] = _ln_cast(nc, E, acc, bf16buf, sq16,
                                                consts, tt, m, ready[m],
                                                prev_apply, bstride)
    E.mode(("mm", "k128m1", "bf16"))
    for m in range(NCH):
        E.w("pe", cast_toks[m], sq_toks[m])
        pe.matmul(pstat[0:1, 0:512], onesc[:, 0:1], bf16buf[:, bsl(m)],
                  start=(m == 0), stop=(m == NCH - 1))
    mm2 = None
    for m in range(NCH):
        mm2 = pe.matmul(psq[0:1, 0:512], onesc[:, 0:1],
                        sq16[:, m * 512:(m + 1) * 512],
                        start=(m == 0), stop=(m == NCH - 1))
    tok_stat = E.tk(mm2, "pe")
    E.w("dve", tok_stat)
    tokA = E.tk(dve.tensor_scalar(srC[0:1, :], pstat[0:1, 0:512], 1.0 / D, None,
                                  ALU.mult), "dve")
    E.w("act", tokA)
    tokQ = E.tk(act.activation(srB[0:1, :], srC[0:1, :], AF.Square,
                               bias=consts[0:1, 3:4], scale=1.0), "act")
    E.w("dve", tokQ)
    tokD = E.tk(dve.scalar_tensor_tensor(srB[0:1, :], psq[0:1, 0:512], 1.0 / D,
                                         srB[0:1, :], ALU.mult, ALU.subtract),
                "dve")
    E.w("act", tokD)
    tokE = E.tk(act.activation(srB[0:1, :], srB[0:1, :], AF.Ln,
                               bias=consts[0:1, 1:2], scale=1.0), "act")
    E.w("act", tokE)
    tokF = E.tk(act.activation(srB[0:1, :], srB[0:1, :], AF.Exp,
                               bias=consts[0:1, 3:4], scale=-0.5), "act")
    E.w("dve", tokF)
    tokG = E.tk(dve.tensor_tensor(srC[0:1, :], srC[0:1, :], srB[0:1, :],
                                  ALU.mult), "dve")
    return (tokF, tokG)


def _ln_apply(nc, E, acc, bf16buf, srB, srC, onesrf, tt, row_toks,
              out_cast, pb1, pb2, bstride=None, consts=None, ret_per_m=False):
    pe, dve, act = nc.tensor, nc.vector, nc.scalar

    def bsl(m):
        if bstride is None:
            return slice(m * T + tt * 512, m * T + (tt + 1) * 512)
        return slice(m * bstride, m * bstride + 512)
    E.mode(("mm", "k1", "f32"))
    E.w("pe", row_toks[0], row_toks[1])
    tok_b1 = E.tk(pe.matmul(pb1[:, 0:512], onesrf[0:1, :], srB[0:1, :],
                            start=True, stop=True), "pe")
    tok_b2 = E.tk(pe.matmul(pb2[:, 0:512], onesrf[0:1, :], srC[0:1, :],
                            start=True, stop=True), "pe")
    out_tok = None
    per_m = []
    for m in range(NCH):
        sl = slice(m * T + tt * 512, m * T + (tt + 1) * 512)
        E.w("dve", tok_b1, tok_b2)
        tok8 = E.tk(dve.tensor_tensor(acc[:, sl], acc[:, sl], pb1[:, 0:512],
                                      ALU.mult), "dve")
        E.w("dve", tok8)
        tok9 = E.tk(dve.tensor_tensor(acc[:, sl], acc[:, sl], pb2[:, 0:512],
                                      ALU.subtract), "dve")
        out_tok = tok9
        per_m.append(tok9)
        if out_cast:
            E.w("act", tok9)
            out_tok = E.tk(act.activation(bf16buf[:, bsl(m)], acc[:, sl],
                                          AF.Identity, bias=consts[:, 3:4],
                                          scale=1.0), "act")
    return (out_tok, per_m) if ret_per_m else out_tok


def build_nc():
    nc = bass.Bass("TRN2", target_bir_lowering=False, debug=False)
    def I(name, shape):
        return nc.dram_tensor(name, list(shape), F32, kind="ExternalInput").ap()

    def IB(name, shape):
        return nc.dram_tensor(name, list(shape), BF16,
                              kind="ExternalInput").ap()

    x_t = I("x_t", (D, T))
    wqk_i = IB("wqk", (128, 16 * NCH * 128))
    bqk_i = I("bqk", (128, 16))
    wv_i = IB("wv", (128, NCH * D))
    bv_i = I("bv", (1, D))
    wproj_i = IB("wproj", (128, NCH * NCH * 128))
    bproj_i = I("bproj", (128, NCH))
    rwt_i = I("rwt", (128, NCH * NEXP))
    rbr_i = I("rbr", (1, NEXP))
    wg_i = IB("wg", (NEXP, 128, 16 * NCH * 128))
    gub_i = I("gub", (128, NEXP * 16))
    wd_i = IB("wd", (NEXP, 128, NCH * NCH * 128))
    dbt_i = I("dbt", (NEXP, D))
    cc_i = I("cc", (128, T))
    ss_i = I("ss", (128, T))
    cmask_i = I("cmask", (128, 128))
    ident_i = I("ident", (128, 128))
    onesc_i = I("onesc", (128, 1))
    onesr_i = I("onesr", (1, 128))
    consts_i = I("consts", (128, 8))
    sel_i = I("sel", (16, NCH * 128))
    out = nc.dram_tensor("out", [D, T], F32, kind="ExternalOutput").ap()
    ys_d = nc.dram_tensor("ys_d", [130, 32 * 512], BF16).ap()
    x1_d = nc.dram_tensor("x1_d", [128, NCH * T], BF16).ap()

    dbg_out = {}
    for name, shape in (("vaug", (128, 16 * 1056)),
                        ("x1", (D, T)), ("cw", (128, 128)),
                        ("accmoe", (D, T))):
        if name in DBG:
            dbg_out[name] = nc.dram_tensor("dbg_" + name, list(shape), F32,
                                           kind="ExternalOutput").ap()

    with ExitStack() as st:
        a = st.enter_context
        E = Emit(nc, st)
        pe, act, dve = nc.tensor, nc.scalar, nc.vector

        bqk = a(nc.sbuf_tensor("bqk_s", [128, 16], F32))
        bproj = a(nc.sbuf_tensor("bproj_s", [128, NCH], F32))
        rwt = a(nc.sbuf_tensor("rwt_s", [128, NCH * NEXP], F32))
        rbr_row = a(nc.sbuf_tensor("rbr_s", [1, NEXP], F32))
        gubs = a(nc.sbuf_tensor("gubs_s", [128, NEXP * 16], F32))
        gub7 = a(nc.sbuf_tensor("gub7_s", [128, NEXP * 16], F32))
        dbt16 = a(nc.sbuf_tensor("dbt16_s", [NEXP, D], BF16))
        cmask = a(nc.sbuf_tensor("cmask_s", [128, 128], BF16))
        onesc = a(nc.sbuf_tensor("onesc_s", [128, 1], BF16))
        onesr = a(nc.sbuf_tensor("onesr_s", [1, 128], BF16))
        onesrf = a(nc.sbuf_tensor("onesrf_s", [1, 128], F32))
        consts = a(nc.sbuf_tensor("consts_s", [128, 8], F32))
        rbrep = a(nc.sbuf_tensor("rbrep_s", [128, NEXP], F32))
        cwT16 = a(nc.sbuf_tensor("cwT16", [NEXP, T], BF16))

        t_bqk = E.dma("dy4", bqk[:], bqk_i)
        t_bproj = E.dma("dy3", bproj[:], bproj_i)
        t_rwt = E.dma("dy2", rwt[:], rwt_i)
        t_rbr = E.dma("dy1", rbr_row[:], rbr_i)
        t_dbt = E.dma("dg0", dbt16[:], dbt_i)
        t_cm = E.dma("dg1", cmask[:], cmask_i)
        t_oc = E.dma("dg2", onesc[:], onesc_i)
        t_or = E.dma("dg2", onesr[:], onesr_i)
        t_orf = E.dma("dsc0", onesrf[:], onesr_i)
        t_cst = E.dma("dsc1", consts[:], consts_i)

        with nc.sbuf_tensor("gub_s", [128, NEXP * 16], F32) as gub, \
             nc.psum_tensor("pb_init", [128, 512], F32) as pbi:
            t_gub = E.dma("dy0", gub[:], gub_i)
            E.w("act", t_gub, t_cst)
            t_gubs = None
            for e2 in range(NEXP):
                act.activation(gubs[:, e2 * 16:e2 * 16 + 8],
                               gub[:, e2 * 16:e2 * 16 + 8],
                               AF.Identity, bias=consts[:, 3:4], scale=1.702)
                i2 = act.activation(gubs[:, e2 * 16 + 8:e2 * 16 + 16],
                                    gub[:, e2 * 16 + 8:e2 * 16 + 16],
                                    AF.Identity, bias=consts[:, 2:3], scale=1.0)
                t_gubs = E.tk(i2, "act")
            i3 = act.activation(gub7[:], gub[:], AF.Identity,
                                bias=consts[:, 4:5], scale=1.0)
            t_gub7 = E.tk(i3, "act")
            E.w("pe", t_gubs, t_gub7)
            E.mode(("mm", "k1", "f32"))
            E.w("pe", t_orf, t_rbr)
            tok_mm = E.tk(pe.matmul(pbi[:, 0:8], onesrf[0:1, :], rbr_row[0:1, :],
                                    start=True, stop=True), "pe")
            E.w("dve", tok_mm)
            t_rbrep = E.tk(dve.tensor_copy(rbrep[:], pbi[:, 0:8]), "dve")
            E.w("pe", t_rbrep)

        t_attn_done = None
        t_yt_dma = None
        with ExitStack() as s12:
            b = s12.enter_context
            bvrep = b(nc.sbuf_tensor("bvrep_s", [128, D], BF16))
            with nc.sbuf_tensor("bv_row", [1, D], F32) as bv_s, \
                 nc.psum_tensor("pb_init2", [128, 512], F32) as pbi:
                t_bv = E.dma("dy1", bv_s[:], bv_i)
                E.mode(("mm", "k1", "f32"))
                E.w("pe", t_bv, t_orf)
                for half in range(2):
                    tok_mm = E.tk(pe.matmul(pbi[:, 0:512], onesrf[0:1, :],
                                            bv_s[0:1, half * 512:(half + 1) * 512],
                                            start=True, stop=True), "pe")
                    E.w("dve", tok_mm)
                    t_bvr = E.tk(dve.tensor_copy(
                        bvrep[:, half * 512:(half + 1) * 512],
                        pbi[:, 0:512]), "dve")
                    E.w("pe", t_bvr)
            qkraw = b(nc.sbuf_tensor("qkraw", [128, 16 * T], BF16))
            vaug = b(nc.sbuf_tensor("vaug", [128, 16 * 1056], BF16))
            cc16 = b(nc.sbuf_tensor("cc16", [128, T], BF16))
            ss16 = b(nc.sbuf_tensor("ss16", [128, T], BF16))
            qswp = [b(nc.sbuf_tensor(f"qswp{i}", [128, T], BF16))
                    for i in range(2)]
            rt1 = b(nc.sbuf_tensor("rt1", [128, T], BF16))
            rt2 = b(nc.sbuf_tensor("rt2", [128, T], BF16))

            with ExitStack() as s1:
                b1 = s1.enter_context
                wqk16 = b1(nc.sbuf_tensor("wqk16", [128, 16 * NCH * 128], BF16))
                wv16 = b1(nc.sbuf_tensor("wv16", [128, NCH * D], BF16))
                xsl = [b1(nc.sbuf_tensor(f"xsl{i}", [128, NCH * 512], BF16))
                       for i in range(2)]
                def load_xsl(par, tt):
                    return E.dma(
                        "dg1" if par == 0 else "dg2",
                        xsl[par][:].rearrange("p (c t) -> p c t", t=512),
                        x_t[:, tt * 512:(tt + 1) * 512]
                        .rearrange("(c p) t -> p c t", p=128))

                t_xsl = [load_xsl(0, 0), None]
                SW = 4 * NCH * 128
                t_wqk = [E.dma("dg0", wqk16[:, 0 * SW:1 * SW], wqk_i[:, 0 * SW:1 * SW])]
                t_xsl[1] = load_xsl(1, 1)
                t_wqk += [E.dma("dg0", wqk16[:, s_ * SW:(s_ + 1) * SW],
                                wqk_i[:, s_ * SW:(s_ + 1) * SW]) for s_ in (1, 2, 3)]
                t_wv = E.dma("dg0", wv16[:], wv_i)
                t_cc = E.dma("dg0", cc16[:], cc_i)
                t_ss = E.dma("dg0", ss16[:], ss_i)
                tok_vz = E.tk(dve.memset(vaug[:], 0.0), "dve")

                with nc.psum_tensor("pq0", [128, 512], F32) as pq0, \
                     nc.psum_tensor("pq1", [128, 512], F32) as pq1, \
                     nc.psum_tensor("pv0", [128, 1024], F32) as pv0, \
                     nc.psum_tensor("pv1", [128, 1024], F32) as pv1:
                    pq, pv = [pq0, pq1], [pv0, pv1]
                    E.mode(("mm", "k128", "bf16"))
                    qk_ev = {}
                    v_ev = {0: None, 1: None}
                    xsl_users = [None, None]
                    for tt in range(4):
                        par = tt % 2
                        if tt >= 2:
                            E.w("dg", xsl_users[par])
                            t_xsl[par] = load_xsl(par, tt)
                        for m in range(16):
                            mp = m % 2
                            E.w("pe", t_wqk[m // 4], t_xsl[par])
                            if (m - 2, tt) in qk_ev:
                                E.w("pe", qk_ev[(m - 2, tt)])
                            elif tt > 0 and (m + 14, tt - 1) in qk_ev:
                                E.w("pe", qk_ev[(m + 14, tt - 1)])
                            mm_last = None
                            for c in range(NCH):
                                mm_last = pe.matmul(
                                    pq[mp][:, 0:512],
                                    wqk16[:, (m * NCH + c) * 128:(m * NCH + c + 1) * 128],
                                    xsl[par][:, c * 512:(c + 1) * 512],
                                    start=(c == 0), stop=(c == NCH - 1))
                            tok_mm = E.tk(mm_last, "pe")
                            E.w("act", tok_mm, t_bqk)
                            ei = act.activation(
                                qkraw[:, m * T + tt * 512:m * T + (tt + 1) * 512],
                                pq[mp][:, 0:512], AF.Identity,
                                bias=bqk[:, m:m + 1], scale=1.0)
                            qk_ev[(m, tt)] = E.tk(ei, "act")
                        for vi in range(4):
                            ti = tt * 4 + vi
                            vp = vi % 2
                            E.w("pe", t_wv)
                            if ti >= 2:
                                E.w("pe", v_ev[vp])
                            mm_last = None
                            for half in range(2):
                                for c in range(NCH):
                                    mm_last = pe.matmul(
                                        pv[vp][:, half * 512:(half + 1) * 512],
                                        xsl[par][:, c * 512 + vi * 128:
                                                 c * 512 + (vi + 1) * 128],
                                        wv16[:, c * D + half * 512:
                                             c * D + (half + 1) * 512],
                                        start=(c == 0), stop=(c == NCH - 1))
                            tok_mm = E.tk(mm_last, "pe")
                            E.w("dve", tok_mm, t_bvr, tok_vz)
                            ovw = vaug[:, ti * 1056:(ti + 1) * 1056]
                            ovw = ovw.rearrange("p (h x) -> p h x", x=66)[:, :, 0:64]
                            v_ev[vp] = E.tk(dve.tensor_tensor(ovw, pv[vp][:],
                                                              bvrep[:], ALU.add),
                                            "dve")
                        xsl_users[par] = tok_mm
                    E.w("pe", qk_ev[(14, 3)], qk_ev[(15, 3)], v_ev[0], v_ev[1])
                E.w("dve", v_ev[0], v_ev[1])
                tok_vones = None
                for ti in range(16):
                    vv = vaug[:, ti * 1056:(ti + 1) * 1056]
                    vv = vv.rearrange("p (h x) -> p h x", x=66)[:, :, 64:65]
                    tok_vones = E.tk(dve.memset(vv, 1.0), "dve")

            with ExitStack() as s2:
                b2 = s2.enter_context
                kpadB = b2(nc.sbuf_tensor("kpadB", [128, 8 * T], BF16))
                attAB = [b2(nc.sbuf_tensor(f"attAB{i}", [128, 1024], BF16))
                         for i in range(3)]
                ytst = [b2(nc.sbuf_tensor(f"ytst{i}", [65, 1024], BF16))
                        for i in range(2)]
                pe_end = E.bar()
                tok_kbz = None

                def emit_rope(m, prev_tok):
                    par = m % 2
                    E.w("dy", *[qk_ev[(m, ttx)] for ttx in range(4)])
                    if prev_tok[par] is not None:
                        E.w("dy", prev_tok[par][1])
                    tok_sw = []
                    chs = ("dy0", "dy1") if m < 8 else ("dy2", "dy3")
                    for bi, (blk, srb) in enumerate(((0, 1), (1, 0),
                                                    (2, 3), (3, 2))):
                        tok_sw.append(E.dma(chs[bi % 2],
                                            qswp[par][blk * 32:(blk + 1) * 32, :],
                                            qkraw[srb * 32:(srb + 1) * 32,
                                                  m * T:(m + 1) * T]))
                    E.w("dve", *[qk_ev[(m, ttx)] for ttx in range(4)])
                    E.w("dve", t_cc, t_ss)
                    if prev_tok[1 - par] is not None:
                        E.w("dve", prev_tok[1 - par][0])
                    if prev_tok[par] is not None:
                        E.w("dve", prev_tok[par][0])
                    tok_t1 = E.tk(dve.tensor_tensor(rt1[:],
                                                    qkraw[:, m * T:(m + 1) * T],
                                                    cc16[:], ALU.mult), "dve")
                    E.w("dve", *tok_sw)
                    tok_t2 = E.tk(dve.tensor_tensor(rt2[:], qswp[par][:], ss16[:],
                                                    ALU.mult), "dve")
                    E.w("dve", tok_t1, tok_t2)
                    if m < 8:
                        tok_t3 = E.tk(dve.tensor_tensor(qkraw[:, m * T:(m + 1) * T],
                                                        rt1[:], rt2[:], ALU.add),
                                      "dve")
                        return (tok_t3, tok_t2), tok_t3
                    c = m - 8
                    nonlocal tok_kbz
                    if tok_kbz is None:
                        E.w("dve", *pe_end)
                        tok_kbz = E.tk(dve.memset(kpadB[0:64, :], 0.0), "dve")
                    E.tk(dve.tensor_tensor(qkraw[0:64, m * T:(m + 1) * T],
                                           rt1[0:64, :], rt2[0:64, :], ALU.add),
                         "dve")
                    tok_b = E.tk(dve.tensor_tensor(kpadB[64:128, c * T:(c + 1) * T],
                                                   rt1[64:128, :], rt2[64:128, :],
                                                   ALU.add), "dve")
                    tok_z = E.tk(dve.memset(qkraw[64:128, m * T:(m + 1) * T], 0.0),
                                 "dve")
                    return (tok_z, tok_t2), tok_z

                with nc.psum_tensor("psab0", [128, 1024], F32) as psab0, \
                     nc.psum_tensor("psab1", [128, 1024], F32) as psab1, \
                     nc.psum_tensor("psy0", [128, 1024], F32) as psy0, \
                     nc.psum_tensor("psy1", [128, 1024], F32) as psy1:
                    psAB = [psab0, psab1]
                    psy = [psy0, psy1]
                    E.mode(("mm", "k128", "bf16"))
                    rope_prev = [None, None]
                    rope_q = {}
                    rope_k = {}

                    def rope0_piece(m, ttx, prev):
                        cs = slice(ttx * 512, (ttx + 1) * 512)
                        qs = slice(m * T + ttx * 512, m * T + (ttx + 1) * 512)
                        chs = ("dy0", "dy1") if m < 8 else ("dy2", "dy3")
                        qsw = qswp[0] if m < 8 else qswp[1]
                        E.w("dy", qk_ev[(m, ttx)])
                        tok_sw = [E.dma(chs[bi % 2],
                                        qsw[blk * 32:(blk + 1) * 32, cs],
                                        qkraw[srb * 32:(srb + 1) * 32, qs])
                                  for bi, (blk, srb) in enumerate(
                                      ((0, 1), (1, 0), (2, 3), (3, 2)))]
                        if m >= 8:
                            E.tk(dve.memset(kpadB[0:64, cs], 0.0), "dve")
                        E.w("dve", qk_ev[(m, ttx)], t_cc, t_ss, prev)
                        tok_t1 = E.tk(dve.tensor_tensor(rt1[:, cs], qkraw[:, qs],
                                                        cc16[:, cs], ALU.mult),
                                      "dve")
                        E.w("dve", *tok_sw)
                        tok_t2 = E.tk(dve.tensor_tensor(rt2[:, cs], qsw[:, cs],
                                                        ss16[:, cs], ALU.mult),
                                      "dve")
                        E.w("dve", tok_t1, tok_t2)
                        if m < 8:
                            return E.tk(dve.tensor_tensor(
                                qkraw[:, qs], rt1[:, cs], rt2[:, cs],
                                ALU.add), "dve"), tok_t2
                        E.tk(dve.tensor_tensor(qkraw[0:64, qs], rt1[0:64, cs],
                                               rt2[0:64, cs], ALU.add), "dve")
                        E.tk(dve.tensor_tensor(kpadB[64:128, cs],
                                               rt1[64:128, cs], rt2[64:128, cs],
                                               ALU.add), "dve")
                        return E.tk(dve.memset(qkraw[64:128, qs], 0.0),
                                    "dve"), tok_t2

                    E.w("dve", *pe_end)
                    q0t, k0t = [], []
                    qt2 = kt2 = None
                    pv = None
                    for ttx in range(4):
                        a_, qt2 = rope0_piece(0, ttx, pv)
                        q0t.append(a_)
                        b_, kt2 = rope0_piece(8, ttx, a_)
                        k0t.append(b_)
                        pv = b_
                    tok_kbz = E.tk(dve.memset(kpadB[0:64, T:8 * T], 0.0), "dve")
                    rope_prev[0] = (q0t[-1], qt2)
                    rope_prev[1] = (k0t[-1], kt2)
                    rope_q[0] = q0t
                    rope_k[0] = k0t
                    exp_tok = [None, None]
                    av_tok = [None, None, None]
                    evac_tok = [None, None]
                    spill_tok = [None, None]
                    qi = 0
                    for cch in range(8):
                        hA, hB = 2 * cch, 2 * cch + 1
                        qcol = cch * T
                        kcol = (8 + cch) * T
                        for Q in range(4):
                            if Q == 2 and cch + 1 < 8:
                                rope_prev[(cch + 1) % 2], rope_q[cch + 1] = \
                                    emit_rope(cch + 1, rope_prev)
                                rope_prev[(cch + 9) % 2], rope_k[cch + 1] = \
                                    emit_rope(cch + 9, rope_prev)
                            tq0 = Q * 512
                            jmax = 4 * Q + 4
                            qp = qi % 2

                            def jslice(j):
                                tqs = max(tq0, 128 * j)
                                return tqs, tq0 + 512 - tqs

                            def emit_scores(j):
                                par = j % 2
                                ab = j % 3
                                tqs, w_ = jslice(j)
                                rq, rk = rope_q[cch], rope_k[cch]
                                E.w("pe", exp_tok[par],
                                    rq[Q] if isinstance(rq, list) else rq,
                                    rk[j // 4] if isinstance(rk, list) else rk,
                                    tok_kbz if cch > 0 else None)
                                mA = pe.matmul(
                                    psAB[par][:, 0:w_],
                                    qkraw[:, kcol + 128 * j:kcol + 128 * j + 128],
                                    qkraw[:, qcol + tqs:qcol + tq0 + 512],
                                    start=True, stop=True)
                                E.tk(mA, "pe")
                                mB = pe.matmul(
                                    psAB[par][:, 512:512 + w_],
                                    kpadB[:, cch * T + 128 * j:cch * T + 128 * j + 128],
                                    qkraw[:, qcol + tqs:qcol + tq0 + 512],
                                    start=True, stop=True)
                                tok_mm = E.tk(mB, "pe")
                                E.w("act", tok_mm, t_cst, av_tok[ab])
                                src3 = psAB[par][:].rearrange(
                                    "p (b t) -> p b t", t=512)[:, :, 0:w_]
                                dst3 = attAB[ab][:].rearrange(
                                    "p (b t) -> p b t", t=512)[:, :, 0:w_]
                                eAB = act.activation(dst3, src3, AF.Exp,
                                                     bias=consts[:, 0:1],
                                                     scale=0.125)
                                tok_e = E.tk(eAB, "act")
                                if 128 * j >= tq0:
                                    E.w("pool", tok_e, t_cm)
                                    mdst = attAB[ab][:].rearrange(
                                        "p (b t) -> p b t", t=512)[:, :, 0:128]
                                    mi = nc.gpsimd.tensor_tensor(
                                        mdst, mdst,
                                        cmask[:].unsqueeze(1)
                                        .broadcast_to((128, 2, 128)), ALU.mult)
                                    tok_e = E.tk(mi, "pool")
                                exp_tok[par] = tok_e

                            def emit_av(j):
                                par = j % 2
                                ab = j % 3
                                tqs, w_ = jslice(j)
                                E.w("pe", exp_tok[par], tok_vones)
                                if j == 0:
                                    E.w("pe", evac_tok[qp])
                                aA = pe.matmul(
                                    psy[qp][0:65, tqs - tq0:512],
                                    vaug[:, j * 1056 + hA * 66:
                                         j * 1056 + hA * 66 + 65],
                                    attAB[ab][:, 0:w_],
                                    start=(j == 0), stop=(j == jmax - 1))
                                E.tk(aA, "pe")
                                aB = pe.matmul(
                                    psy[qp][0:65, 512 + tqs - tq0:1024],
                                    vaug[:, j * 1056 + hB * 66:
                                         j * 1056 + hB * 66 + 65],
                                    attAB[ab][:, 512:512 + w_],
                                    start=(j == 0), stop=(j == jmax - 1))
                                av_tok[ab] = E.tk(aB, "pe")

                            emit_scores(0)
                            emit_scores(1)
                            for j in range(2, jmax):
                                emit_av(j - 2)
                                emit_scores(j)
                            emit_av(jmax - 2)
                            emit_av(jmax - 1)
                            t_av = av_tok[(jmax - 1) % 3]
                            sp = qi % 2
                            E.w("dve", t_av, spill_tok[sp], evac_tok[sp])
                            ev = dve.tensor_copy(ytst[sp][:], psy[qp][0:65, 0:1024])
                            tok_ev = E.tk(ev, "dve")
                            evac_tok[sp] = tok_ev
                            E.w("dy", tok_ev)
                            ych = "dy2" if sp == 0 else "dy3"
                            ysc = Q * 4096 + cch * 512
                            E.dma(ych, ys_d[0:65, ysc:ysc + 512],
                                  ytst[sp][:, 0:512])
                            t_yt_dma = E.dma(ych, ys_d[65:130, ysc:ysc + 512],
                                             ytst[sp][:, 512:1024])
                            spill_tok[sp] = t_yt_dma
                            qi += 1
                    t_attn_done = evac_tok[(qi - 1) % 2]
                    E.w("pe", t_attn_done, evac_tok[qi % 2])

            if "vaug" in DBG:
                with nc.sbuf_tensor("dbgv", [128, 1056], F32) as tmpd:
                    E.w("dve", tok_vones)
                    for ti in range(16):
                        tok = E.tk(dve.tensor_copy(
                            tmpd[:], vaug[:, ti * 1056:(ti + 1) * 1056]), "dve")
                        E.w("dy", tok)
                        tok2 = E.dma("dy4",
                                     dbg_out["vaug"][:, ti * 1056:(ti + 1) * 1056],
                                     tmpd[:])
                        E.w("dve", tok2)

        acc = a(nc.sbuf_tensor("acc", [128, NCH * T], F32))
        tok_ln1 = {}
        tok_ln1_mul = {}
        with ExitStack() as s3:
            b3 = s3.enter_context
            x1b16 = b3(nc.sbuf_tensor("x1b16", [128, NCH * T], BF16))
            s3p = ExitStack()
            b3p = s3p.enter_context
            wproj16 = b3p(nc.sbuf_tensor("wproj16", [128, NCH * NCH * 128], BF16))
            ytsl = [b3p(nc.sbuf_tensor(f"ytsl{i}", [128, NCH * 512], BF16))
                    for i in range(2)]
            xfsl = [b3p(nc.sbuf_tensor(f"xfsl{i}", [128, NCH * 512], F32))
                    for i in range(2)]
            srow16 = [b3p(nc.sbuf_tensor(f"srow16_{i}", [16, 512], BF16))
                      for i in range(2)]
            srf = [b3p(nc.sbuf_tensor(f"srf{i}", [16, 512], F32)) for i in range(2)]
            rs16 = [b3p(nc.sbuf_tensor(f"rs16_{i}", [16, 512], BF16))
                    for i in range(2)]
            sel16 = b3p(nc.sbuf_tensor("sel16", [16, NCH * 128], BF16))
            ptmp = [b3p(nc.sbuf_tensor(f"ptmp{i}", [128, 512], F32)) for i in range(2)]
            sq16 = b3p(nc.sbuf_tensor("sq16", [128, NCH * 512], BF16))
            srBs2 = [b3p(nc.sbuf_tensor(f"srB{i}", [1, 512], F32)) for i in range(2)]
            srCs2 = [b3p(nc.sbuf_tensor(f"srC{i}", [1, 512], F32)) for i in range(2)]
            srBs = [srBs2[0], srBs2[1], srBs2[0], srBs2[1]]
            srCs = [srCs2[0], srCs2[1], srCs2[0], srCs2[1]]
            row_toks_ln = {}
            t_x1d = {}
            t_x1d_lo = {}

            E.w("dg", *E.bar())
            E.w("dy", *E.bar())
            E.w("dsc", *E.bar())
            t_sel = E.dma("dg0", sel16[:], sel_i)
            QWP = 2 * NCH * 128
            t_wp4 = [E.dma(("dg0", "dg1", "dg2", "dy4")[p],
                           wproj16[:, p * QWP:(p + 1) * QWP],
                           wproj_i[:, p * QWP:(p + 1) * QWP]) for p in range(4)]

            ysd3 = ys_d.rearrange("r (q c t) -> r q c t", c=NCH, t=512)

            with nc.psum_tensor("pp0", [128, 512], F32) as pp0, \
                 nc.psum_tensor("pp1", [128, 512], F32) as pp1, \
                 nc.psum_tensor("pbc0", [128, 512], F32) as pbc0, \
                 nc.psum_tensor("pbc1", [128, 512], F32) as pbc1, \
                 nc.psum_tensor("pstat", [128, 512], F32) as pstat, \
                 nc.psum_tensor("psq", [128, 512], F32) as psq, \
                 nc.psum_tensor("plb1", [128, 512], F32) as plb1, \
                 nc.psum_tensor("plb2", [128, 512], F32) as plb2:
                ppr = [pp0, pp1]
                pbc = [pbc0, pbc1]
                slice_users = [None, None]
                srow_users = [None, None]
                prev_norm = None
                rs_toks = [None, None]
                rf_toks = [None, None]
                t_srows = [None, None]
                t_ysl = [None, None]
                t_xfs_lo = [None, None]
                t_xfs = [None, None]

                def load_proj_tt(tt):
                    par = tt % 2
                    ch = "dy0" if par == 0 else "dy1"
                    ch2 = "dy1" if par == 0 else "dy0"
                    if tt >= 2:
                        E.w("dy", slice_users[par], rf_toks[par])
                        E.w("dsc", slice_users[par])
                    E.dma(ch, srow16[par][0:8, :],
                          ysd3[64:65, tt, :, :].squeeze(0))
                    t_srows[par] = E.dma(ch, srow16[par][8:16, :],
                                         ysd3[129:130, tt, :, :].squeeze(0))
                    ta = E.dma(ch, ytsl[par][0:64, :],
                               ys_d[0:64, tt * 4096:(tt + 1) * 4096])
                    tb = E.dma(ch2, ytsl[par][64:128, :],
                               ys_d[65:129, tt * 4096:(tt + 1) * 4096])
                    t_ysl[par] = (ta, tb)
                    xf3 = (x_t[:, tt * 512:(tt + 1) * 512]
                           .rearrange("(c p) t -> p c t", p=128))
                    xs3 = xfsl[par][:].rearrange("p (c t) -> p c t", t=512)
                    if tt == 0:
                        cl, chh = "dsc0", "dsc1"
                    elif tt == 1:
                        cl, chh = "dy2", "dy3"
                    else:
                        cl = "dsc0" if par == 0 else "dsc1"
                        chh = "dy2" if par == 0 else "dy3"
                    t_xfs_lo[par] = E.dma(cl, xs3[:, 0:4, :], xf3[:, 0:4, :])
                    t_xfs[par] = E.dma(chh, xs3[:, 4:8, :], xf3[:, 4:8, :])

                load_proj_tt(0)
                load_proj_tt(1)
                for tt in range(4):
                    par = tt % 2
                    ln_casts, ln_sqs = {}, {}
                    E.w("act", t_srows[par])
                    if rs_toks[par] is not None:
                        E.w("act", rs_toks[par])
                    tok_l = E.tk(act.activation(srf[par][:], srow16[par][:],
                                                AF.Ln, bias=consts[0:16, 3:4],
                                                scale=1.0), "act")
                    E.w("act", tok_l)
                    tok_rf = E.tk(act.activation(srf[par][:], srf[par][:],
                                                 AF.Exp, bias=consts[0:16, 3:4],
                                                 scale=-1.0), "act")
                    rf_toks[par] = tok_rf
                    E.w("dve", tok_rf)
                    tok_rs = E.tk(dve.tensor_copy(rs16[par][:], srf[par][:]), "dve")
                    rs_toks[par] = tok_rs
                    E.mode(("mm", "k16", "bf16"))
                    norm_toks = {}
                    for c in range(NCH):
                        cp = c % 2
                        E.w("pe", tok_rs, t_sel, t_xfs[par])
                        if c >= 2:
                            E.w("pe", norm_toks[c - 2])
                        elif prev_norm is not None:
                            E.w("pe", prev_norm)
                        bm = pe.matmul(pbc[cp][:, 0:512],
                                       sel16[:, c * 128:(c + 1) * 128],
                                       rs16[par][:], start=True, stop=True)
                        tok_bm = E.tk(bm, "pe")
                        E.w("dve", tok_bm, *t_ysl[par])
                        ni = dve.tensor_tensor(ytsl[par][:, c * 512:(c + 1) * 512],
                                               ytsl[par][:, c * 512:(c + 1) * 512],
                                               pbc[cp][:, 0:512], ALU.mult)
                        norm_toks[c] = E.tk(ni, "dve")
                    E.mode(("mm", "k128", "bf16"))
                    bias_tok = {}
                    ev_tok = {}
                    for m in range(NCH):
                        mp = m % 2
                        E.w("pe", t_wp4[m // 2],
                            norm_toks[NCH - 1], norm_toks[NCH - 2])
                        if m >= 2:
                            E.w("pe", ev_tok[m - 2])
                        mm_last = None
                        for c in range(NCH):
                            mm_last = pe.matmul(
                                ppr[mp][:, 0:512],
                                wproj16[:, (m * NCH + c) * 128:(m * NCH + c + 1) * 128],
                                ytsl[par][:, c * 512:(c + 1) * 512],
                                start=(c == 0), stop=(c == NCH - 1))
                        tok_mm = E.tk(mm_last, "pe")
                        E.w("dve", tok_mm,
                            t_xfs_lo[par] if m < 4 else t_xfs[par])
                        if m >= 2:
                            E.w("dve", bias_tok[m - 2])
                        i1 = dve.tensor_tensor(ptmp[mp][:], ppr[mp][:, 0:512],
                                               xfsl[par][:, m * 512:(m + 1) * 512],
                                               ALU.add)
                        ev_tok[m] = E.tk(i1, "dve")
                        E.w("act", ev_tok[m], t_bproj)
                        i2 = act.activation(
                            acc[:, m * T + tt * 512:m * T + (tt + 1) * 512],
                            ptmp[mp][:], AF.Identity, bias=bproj[:, m:m + 1],
                            scale=1.0)
                        bias_tok[m] = E.tk(i2, "act")
                        ln_casts[m], ln_sqs[m] = _ln_cast(
                            nc, E, acc, x1b16, sq16, consts, tt, m,
                            bias_tok[m], prev_norm, cast_eng="pool")
                    slice_users[par] = ev_tok[NCH - 1]
                    if tt + 2 < 4:
                        load_proj_tt(tt + 2)
                    if tt >= 1:
                        tok_n, pm_ = _ln_apply(nc, E, acc, x1b16, srBs[tt - 1],
                                               srCs[tt - 1], onesrf, tt - 1,
                                               row_toks_ln[tt - 1], True,
                                               plb1, plb2, consts=consts,
                                               ret_per_m=True)
                        tok_ln1_mul[tt - 1] = pm_[-1]
                        tok_ln1[tt - 1] = tok_n
                        prev_norm = tok_n
                        E.w("dy", tok_n)
                        x1dv = (x1_d[:, (tt - 1) * 4096:tt * 4096]
                                .rearrange("p (c t) -> p c t", t=512))
                        x1bv = (x1b16[:]
                                .rearrange("p (c q t) -> p c q t", q=4, t=512)
                                [:, :, tt - 1, :])
                        t_x1d_lo[tt - 1] = E.dma("dy3", x1dv[:, 0:4, :],
                                                 x1bv[:, 0:4, :])
                        t_x1d[tt - 1] = E.dma("dy4", x1dv[:, 4:8, :],
                                              x1bv[:, 4:8, :])
                    row_toks_ln[tt] = _ln_stats(nc, E, acc, x1b16, sq16,
                                                srBs[tt], srCs[tt],
                                                onesc, consts, tt, bias_tok,
                                                pstat, psq, prev_norm,
                                                cast_toks=ln_casts, sq_toks=ln_sqs)
                tok_n, pm_ = _ln_apply(nc, E, acc, x1b16, srBs[3], srCs[3],
                                       onesrf, 3, row_toks_ln[3], True,
                                       plb1, plb2, consts=consts,
                                       ret_per_m=True)
                tok_ln1_mul[3] = pm_[-1]
                tok_ln1[3] = tok_n
                E.w("dy", tok_n)
                x1dv = (x1_d[:, 3 * 4096:4 * 4096]
                        .rearrange("p (c t) -> p c t", t=512))
                x1bv = (x1b16[:].rearrange("p (c q t) -> p c q t", q=4, t=512)
                        [:, :, 3, :])
                t_x1d_lo[3] = E.dma("dy3", x1dv[:, 0:4, :], x1bv[:, 0:4, :])
                t_x1d[3] = E.dma("dy4", x1dv[:, 4:8, :], x1bv[:, 4:8, :])
                E.w("pe", tok_ln1[3])
            s3p.close()
            if "x1" in DBG:
                with nc.sbuf_tensor("dbgx1", [128, T], F32) as tmpd:
                    for c in range(NCH):
                        E.w("dve", *tok_ln1.values())
                        tok = E.tk(dve.tensor_copy(tmpd[:],
                                                   acc[:, c * T:(c + 1) * T]), "dve")
                        E.w("dy", tok)
                        tok2 = E.dma("dy4", dbg_out["x1"][c * 128:(c + 1) * 128, :],
                                     tmpd[:])
                        E.w("dve", tok2)

            sct = b3(nc.sbuf_tensor("sct", [NEXP, T], F32))
            ident = b3(nc.sbuf_tensor("ident_s", [128, 128], F32))
            t_id = E.dma("dy4", ident[:], ident_i)
            sctm = b3(nc.sbuf_tensor("sctm", [128, 16 * NEXP], F32))
            rm1 = b3(nc.sbuf_tensor("rm1", [128, 16], F32))
            rm2 = b3(nc.sbuf_tensor("rm2", [128, 16], F32))
            rs2 = b3(nc.sbuf_tensor("rs2", [128, 16 * NEXP], F32))
            req1 = b3(nc.sbuf_tensor("req1", [128, 16 * NEXP], F32))
            req2 = b3(nc.sbuf_tensor("req2", [128, 16 * NEXP], F32))
            rp1 = b3(nc.sbuf_tensor("rp1", [128, 16], F32))
            rp2 = b3(nc.sbuf_tensor("rp2", [128, 16], F32))
            rtt1 = b3(nc.sbuf_tensor("rtt1", [128, 16 * NEXP], F32))
            cwtm = b3(nc.sbuf_tensor("cwtm", [128, 16 * NEXP], F32))

            def r3(ap):
                return ap.rearrange("p (t e) -> p t e", e=NEXP)

            with nc.psum_tensor("prt0", [128, 512], F32) as prt0, \
                 nc.psum_tensor("prt1", [128, 512], F32) as prt1:
                E.mode(("mm", "k128m8", "f32"))
                sc_ev = []
                for tt in range(4):
                    par = tt % 2
                    if tt >= 2:
                        E.w("pe", sc_ev[tt - 2])
                    E.w("pe", t_rwt, tok_ln1_mul[tt])
                    mm_last = None
                    for c in range(NCH):
                        mm_last = pe.matmul(
                            [prt0, prt1][par][0:NEXP, 0:512],
                            rwt[:, c * NEXP:(c + 1) * NEXP],
                            acc[:, c * T + tt * 512:c * T + (tt + 1) * 512],
                            start=(c == 0), stop=(c == NCH - 1))
                    tok_mm = E.tk(mm_last, "pe")
                    E.w("dve", tok_mm)
                    iv = dve.tensor_copy(sct[:, tt * 512:(tt + 1) * 512],
                                         [prt0, prt1][par][0:NEXP, 0:512])
                    sc_ev.append(E.tk(iv, "dve"))
                E.w("pe", *sc_ev)

            with nc.psum_tensor("prt2", [128, 128], F32) as prt2:
                E.mode(("tr", "f32"))
                E.w("pe", t_id, *sc_ev)
                tr_tok = None
                for ti in range(16):
                    tr = pe.transpose(prt2[:, ti * 8:(ti + 1) * 8],
                                      sct[:, ti * 128:(ti + 1) * 128],
                                      ident[0:8, 0:8])
                    tr_tok = E.tk(tr, "pe")
                E.w("dve", tr_tok)
                tok_sctm = E.tk(dve.tensor_copy(sctm[:], prt2[:, 0:128]), "dve")
                E.w("pe", tok_sctm)

            E.w("dve", tok_sctm, t_rbrep)
            tok = E.tk(dve.tensor_tensor(
                r3(sctm[:]), r3(sctm[:]),
                rbrep[:].unsqueeze(1).broadcast_to((128, 16, NEXP)),
                ALU.add), "dve")
            E.w("dve", tok)
            tok_m1 = E.tk(dve.reduce_max(rm1[:], r3(sctm[:]), axis=AX.X), "dve")
            E.w("dve", tok_m1)
            m1b = rm1[:].unsqueeze(2).broadcast_to((128, 16, NEXP))
            tok_e1 = E.tk(dve.tensor_tensor(r3(req1[:]), r3(sctm[:]), m1b,
                                            ALU.is_ge), "dve")
            E.w("dve", tok_e1)
            tok_s2 = E.tk(dve.scalar_tensor_tensor(r3(rs2[:]), r3(req1[:]), -1e30,
                                                   r3(sctm[:]), ALU.mult, ALU.add),
                          "dve")
            E.w("dve", tok_s2)
            tok_m2 = E.tk(dve.reduce_max(rm2[:], r3(rs2[:]), axis=AX.X), "dve")
            E.w("dve", tok_m2)
            tok_d = E.tk(dve.tensor_tensor(rp1[:], rm1[:], rm2[:], ALU.subtract),
                         "dve")
            E.w("act", tok_d)
            tok_p1 = E.tk(act.activation(rp1[:], rp1[:], AF.Sigmoid,
                                         bias=consts[:, 3:4], scale=1.0), "act")
            E.w("act", tok_p1)
            tok_p2 = E.tk(act.activation(rp2[:], rp1[:], AF.Identity,
                                         bias=consts[:, 2:3], scale=-1.0), "act")
            E.w("dve", tok_m2, tok_s2)
            m2b = rm2[:].unsqueeze(2).broadcast_to((128, 16, NEXP))
            tok_e2 = E.tk(dve.tensor_tensor(r3(req2[:]), r3(rs2[:]), m2b, ALU.is_ge),
                          "dve")
            E.w("dve", tok_p1, tok_e1)
            p1b = rp1[:].unsqueeze(2).broadcast_to((128, 16, NEXP))
            tok_t1 = E.tk(dve.tensor_tensor(r3(rtt1[:]), r3(req1[:]), p1b, ALU.mult),
                          "dve")
            E.w("dve", tok_p2, tok_e2)
            p2b = rp2[:].unsqueeze(2).broadcast_to((128, 16, NEXP))
            tok_t2 = E.tk(dve.tensor_tensor(r3(req2[:]), r3(req2[:]), p2b, ALU.mult),
                          "dve")
            E.w("dve", tok_t1, tok_t2)
            tok_cw = E.tk(dve.tensor_tensor(cwtm[:], rtt1[:], req2[:], ALU.add),
                          "dve")
            if "cw" in DBG:
                E.w("dy", tok_cw)
                tok2 = E.dma("dy4", dbg_out["cw"], cwtm[:])
                E.w("dve", tok2)

            with nc.psum_tensor("prt3", [128, 2048], F32) as prt3:
                E.mode(("tr", "f32"))
                E.w("pe", tok_cw)
                tr_tok = None
                for ti in range(16):
                    tr = pe.transpose(prt3[0:NEXP, ti * 128:(ti + 1) * 128],
                                      cwtm[:, ti * 8:(ti + 1) * 8], ident[:, 0:128])
                    tr_tok = E.tk(tr, "pe")
                E.w("dve", tr_tok)
                tok_cwT16 = E.tk(dve.tensor_copy(cwT16[:], prt3[0:NEXP, 0:2048]),
                                 "dve")
                E.w("pe", tok_cwT16)

        dbias_ev = {}
        with ExitStack() as s5:
            b5 = s5.enter_context
            x1sl = [b5(nc.sbuf_tensor(f"x1sl{i}", [128, NCH * 512], BF16))
                    for i in range(2)]
            wgb = [b5(nc.sbuf_tensor(f"wg16_{i}", [128, 16 * NCH * 128], BF16))
                   for i in range(2)]
            wdb = b5(nc.sbuf_tensor("wd16", [128, NCH * NCH * 128], BF16))
            asb = [b5(nc.sbuf_tensor(f"as16_{i}", [128, NCH * 512], BF16))
                   for i in range(2)]
            cwb = b5(nc.sbuf_tensor("cwb16", [128, T], BF16))
            cwrow0 = b5(nc.sbuf_tensor("cwrow0", [1, T], BF16))
            cwrow = [cwrow0, cwrow0]
            sg_t = [b5(nc.sbuf_tensor(f"sgt{i}", [128, 512], BF16)) for i in range(2)]
            srB2s2 = [b5(nc.sbuf_tensor(f"srB2_{i}", [1, 512], F32))
                      for i in range(2)]
            srC2s2 = [b5(nc.sbuf_tensor(f"srC2_{i}", [1, 512], F32))
                      for i in range(2)]
            srB2s = [srB2s2[0], srB2s2[1], srB2s2[0], srB2s2[1]]
            srC2s = [srC2s2[0], srC2s2[1], srC2s2[0], srC2s2[1]]
            gc_t = [b5(nc.sbuf_tensor(f"gct{i}", [128, 512], BF16)) for i in range(2)]
            uc_t = [b5(nc.sbuf_tensor(f"uct{i}", [128, 512], BF16)) for i in range(2)]

            def load_x1(k, e, tt, split=False):
                ch = "dy0" if k % 2 == 0 else "dy1"
                E.w("dy", t_x1d_lo.get(tt), t_x1d.get(tt))
                dst = x1sl[k % 2]
                base = tt * 4096
                if split:
                    t_x1lo[k % 2] = E.dma("dy4" if k % 2 == 0 else "dy3",
                                          dst[:, 0:2048],
                                          x1_d[:, base:base + 2048])
                    return E.dma(ch, dst[:, 2048:4096],
                                 x1_d[:, base + 2048:base + 4096])
                return E.dma(ch, dst[:], x1_d[:, base:base + 4096])

            t_x1lo = [None, None]
            t_x1sl = [None, None]
            E.w("dy", t_x1d_lo[3], t_x1d[3])
            t_x1sl[0] = load_x1(0, 0, 0, split=True)
            t_x1sl[1] = load_x1(1, 0, 1, split=True)

            E.w("dg", *E.bar())
            E.w("dsc", *E.bar())
            E.w("dy", *E.bar())
            E.w("dg", tok_cwT16)
            t_cwrow = [E.dma("dg0", cwrow[0][:], cwT16[0:1, :]), None]

            def load_wg(e, war=None):
                if war is not None:
                    E.w("dg", war)
                    E.w("dy", war)
                HW_ = 8 * NCH * 128
                lo = E.dma("dg1", wgb[e % 2][:, 0:HW_], wg_i[e][:, 0:HW_])
                hi = E.dma("dy2", wgb[e % 2][:, HW_:2 * HW_],
                           wg_i[e][:, HW_:2 * HW_])
                return (lo, hi)

            t_wg = [load_wg(0)] + [None] * (NEXP - 1)
            _t0 = E.dma("dg2", wdb[:], wd_i[0])
            t_wd = [[_t0, _t0]] + [None] * (NEXP - 1)

            with nc.psum_tensor("pg0", [128, 512], F32) as pg0, \
                 nc.psum_tensor("pg1", [128, 512], F32) as pg1, \
                 nc.psum_tensor("pu0", [128, 512], F32) as pu0, \
                 nc.psum_tensor("pu1", [128, 512], F32) as pu1, \
                 nc.psum_tensor("pd0", [128, 512], F32) as pd0, \
                 nc.psum_tensor("pd1", [128, 512], F32) as pd1, \
                 nc.psum_tensor("pcw", [128, 1024], F32) as pcw:
                pg, pu, pd = [pg0, pg1], [pu0, pu1], [pd0, pd1]
                swi_toks, down_ev = {}, {}
                down_hist = []
                pairs = [(e, tt) for e in range(NEXP) for tt in range(4)]
                gate_last = None

                def emit_down(k, m2_):
                    e_, tt_ = pairs[k]
                    tp_ = k % 2
                    dp = len(down_hist) % 2
                    if len(down_hist) >= 2:
                        E.w("pe", down_hist[-2])
                    twd = t_wd[e_]
                    E.w("pe", twd[0] if m2_ < 4 else twd[1],
                        *[swi_toks[(e_, tt_, hcx)][0] for hcx in range(NCH)])
                    mmd = None
                    for hc_ in range(NCH):
                        mmd = pe.matmul(
                            pd[dp][:, 0:512],
                            wdb[:, (m2_ * NCH + hc_) * 128:(m2_ * NCH + hc_ + 1) * 128],
                            asb[tp_][:, hc_ * 512:(hc_ + 1) * 512],
                            start=(hc_ == 0), stop=(hc_ == NCH - 1))
                    tok_dn = E.tk(mmd, "pe")
                    sl = slice(m2_ * T + tt_ * 512, m2_ * T + (tt_ + 1) * 512)
                    E.w("dve", tok_dn)
                    if e_ == 0:
                        E.w("dve", tok_ln1[tt_])
                    tok = E.tk(dve.tensor_tensor(acc[:, sl], acc[:, sl],
                                                 pd[dp][:, 0:512], ALU.add), "dve")
                    down_ev[(e_, tt_, m2_)] = tok
                    down_hist.append(tok)
                    if tt_ == 3 and e_ + 1 < NEXP and m2_ in (3, NCH - 1):
                        HWD = 4 * NCH * 128
                        half = 0 if m2_ == 3 else 1
                        E.w("dg", tok_dn)
                        tok_l = E.dma("dg2",
                                      wdb[:, half * HWD:(half + 1) * HWD],
                                      wd_i[e_ + 1][:, half * HWD:(half + 1) * HWD])
                        if half == 0:
                            t_wd[e_ + 1] = [tok_l, None]
                        else:
                            t_wd[e_ + 1][1] = tok_l

                rows2 = {}
                ln2_apply = {}
                tok_st = {}
                tail_casts = {}

                def stage_casts(s):
                    sp = s % 2
                    ycast = wgb[0][:, sp * 8192:sp * 8192 + 4096]
                    sqali = wgb[0][:, sp * 8192 + 4096:(sp + 1) * 8192]
                    prevu = rows2.get(s - 2)
                    prev = prevu[0] if prevu else pair_gate_last.get(27)
                    casts, sqs = {}, {}
                    for m in range(NCH):
                        casts[m], sqs[m] = _ln_cast(nc, E, acc, ycast, sqali,
                                                    consts, s, m,
                                                    down_ev[(NEXP - 1, s, m)],
                                                    prev, bstride=512,
                                                    cast_eng="pool")
                    tail_casts[s] = (casts, sqs)

                def stage_stats(s, pstat, psq, *war):
                    sp = s % 2
                    ycast = wgb[0][:, sp * 8192:sp * 8192 + 4096]
                    sqali = wgb[0][:, sp * 8192 + 4096:(sp + 1) * 8192]
                    casts, sqs = tail_casts[s]
                    E.w("pe", *[t for t in war if t is not None])
                    rows2[s] = _ln_stats(nc, E, acc, ycast, sqali,
                                         srB2s[s], srC2s[s], onesc, consts,
                                         s, None, pstat, psq, None, bstride=512,
                                         cast_toks=casts, sq_toks=sqs)

                def stage_apply(s, pb1, pb2, *war):
                    E.w("pe", *[t for t in war if t is not None])
                    tok_n, per_m = _ln_apply(nc, E, acc, None, srB2s[s],
                                             srC2s[s], onesrf, s, rows2[s],
                                             False, pb1, pb2, bstride=512,
                                             ret_per_m=True)
                    ln2_apply[s] = tok_n
                    if s == 3:
                        chans = ("dy4", "dy3", "dy1", "dy0",
                                 "dsc0", "dsc1", "dy2", "dg0")
                    else:
                        chans = ("dy4", "dy3", "dy1", "dy0",
                                 "dy2", "dy4", "dy3", "dy1")
                    for m in range(NCH):
                        ch = chans[m]
                        eng = ("dsc" if ch.startswith("dsc")
                               else "dg" if ch.startswith("dg") else "dy")
                        E.w(eng, per_m[m])
                        tok_st[ch] = E.dma(ch,
                                           out[m * 128:(m + 1) * 128,
                                               s * 512:(s + 1) * 512],
                                           acc[:, m * T + s * 512:
                                               m * T + (s + 1) * 512])

                tok_cwb_prev = None
                tok_cwb = None
                pair_gate_last = {}
                gexp_last = {}
                for k, (e, tt) in enumerate(pairs):
                    ep = e % 2
                    if tt == 0:
                        E.mode(("mm", "k1", "bf16"))
                        E.w("pe", t_cwrow[ep], t_or, tok_cwb_prev)
                        for wv_ in range(2):
                            mmc = None
                            for hh in range(2):
                                mmc = pe.matmul(
                                    pcw[:, hh * 512:(hh + 1) * 512], onesr[0:1, :],
                                    cwrow[ep][0:1, wv_ * 1024 + hh * 512:
                                              wv_ * 1024 + (hh + 1) * 512],
                                    start=True, stop=True)
                            tok_mmc = E.tk(mmc, "pe")
                            E.w("dve", tok_mmc)
                            if tok_cwb_prev is not None and wv_ == 0:
                                E.w("dve", swi_toks.get((e - 1, 3, 7), (None,))[0])
                            tok_cwb = E.tk(dve.tensor_copy(
                                cwb[:, wv_ * 1024:(wv_ + 1) * 1024], pcw[:]), "dve")
                            if wv_ == 0:
                                E.w("pe", tok_cwb)
                        tok_cwb_prev = tok_cwb
                        if e + 1 < NEXP:
                            E.w("dg", tok_mmc)
                            t_cwrow[(e + 1) % 2] = E.dma(
                                "dg0", cwrow[(e + 1) % 2][:],
                                cwT16[e + 1:e + 2, :])
                        if 1 <= e <= 4:
                            s_ = e - 1
                            E.mode(("mm", "k8", "bf16"))
                            E.w("pe", t_dbt, tok_cwT16)
                            for m_ in range(NCH):
                                dp_ = len(down_hist) % 2
                                if len(down_hist) >= 2:
                                    E.w("pe", down_hist[-2])
                                mmb = pe.matmul(
                                    pd[dp_][:, 0:512],
                                    dbt16[0:NEXP, m_ * 128:(m_ + 1) * 128],
                                    cwT16[0:NEXP, s_ * 512:(s_ + 1) * 512],
                                    start=True, stop=True)
                                tok_b = E.tk(mmb, "pe")
                                sl_ = slice(m_ * T + s_ * 512,
                                            m_ * T + (s_ + 1) * 512)
                                E.w("dve", tok_b)
                                tok_db = E.tk(dve.tensor_tensor(
                                    acc[:, sl_], acc[:, sl_],
                                    pd[dp_][:, 0:512], ALU.add), "dve")
                                down_hist.append(tok_db)
                        E.mode(("mm", "k128", "bf16"))
                    tp = k % 2
                    for hc in range(NCH):
                        hp = hc % 2
                        E.w("pe", t_wg[e][0], t_x1sl[k % 2],
                            t_x1lo[k % 2] if k < 2 else None)
                        prevk = None
                        if hc >= 2:
                            prevk = (e, tt, hc - 2)
                        elif k > 0:
                            prevk = pairs[k - 1] + (hc + 6,)
                        if prevk and prevk in swi_toks:
                            E.w("pe", swi_toks[prevk][1])
                        mmg = mmu = None
                        for c in range(NCH):
                            mmg = pe.matmul(
                                pg[hp][:, 0:512],
                                wgb[ep][:, (hc * NCH + c) * 128:(hc * NCH + c + 1) * 128],
                                x1sl[k % 2][:, c * 512:(c + 1) * 512],
                                start=(c == 0), stop=(c == NCH - 1))
                        tok_g = E.tk(mmg, "pe")
                        E.w("pe", t_wg[e][1])
                        for c in range(NCH):
                            mmu = pe.matmul(
                                pu[hp][:, 0:512],
                                wgb[ep][:, ((hc + 8) * NCH + c) * 128:
                                    ((hc + 8) * NCH + c + 1) * 128],
                                x1sl[k % 2][:, c * 512:(c + 1) * 512],
                                start=(c == 0), stop=(c == NCH - 1))
                        tok_u = E.tk(mmu, "pe")
                        gate_last = tok_u
                        bg = e * 16 + hc
                        bu = e * 16 + 8 + hc
                        E.w("act", tok_g, tok_u, t_gubs, t_gub7)
                        if prevk and prevk in swi_toks:
                            E.w("act", swi_toks[prevk][0])
                        tok_sg = E.tk(act.activation(sg_t[hp][:], pg[hp][:],
                                                     AF.Sigmoid,
                                                     bias=gubs[:, bg:bg + 1],
                                                     scale=1.702), "act")
                        tok_rg = E.tk(act.activation(gc_t[hp][:], pg[hp][:],
                                                     AF.Relu,
                                                     bias=gub7[:, bg:bg + 1],
                                                     scale=1.0), "act")
                        tok_ru = E.tk(act.activation(uc_t[hp][:], pu[hp][:],
                                                     AF.Relu,
                                                     bias=gub7[:, bu:bu + 1],
                                                     scale=1.0), "act")
                        E.w("dve", tok_rg, tok_ru)
                        if prevk and prevk in swi_toks:
                            E.w("dve", swi_toks[prevk][0])
                        tok_gc = E.tk(dve.tensor_scalar(gc_t[hp][:], gc_t[hp][:],
                                                        -7.0, 7.0,
                                                        ALU.add, ALU.min), "dve")
                        tok_uc = E.tk(dve.tensor_scalar(uc_t[hp][:], uc_t[hp][:],
                                                        -6.0, 8.0,
                                                        ALU.add, ALU.min), "dve")
                        E.w("dve", tok_gc, tok_sg)
                        tok_m1 = E.tk(dve.tensor_tensor(gc_t[hp][:], gc_t[hp][:],
                                                        sg_t[hp][:], ALU.mult), "dve")
                        E.w("dve", tok_uc, tok_cwb)
                        tok_m2 = E.tk(dve.tensor_tensor(
                            uc_t[hp][:], uc_t[hp][:],
                            cwb[:, tt * 512:(tt + 1) * 512], ALU.mult), "dve")
                        prev_dn = None
                        if k >= 2:
                            pk2 = pairs[k - 2] + (NCH - 1,)
                            prev_dn = down_ev.get(pk2)
                        E.w("dve", tok_m1, tok_m2, prev_dn)
                        tok_as = E.tk(dve.tensor_tensor(
                            asb[tp][:, hc * 512:(hc + 1) * 512],
                            gc_t[hp][:], uc_t[hp][:], ALU.mult), "dve")
                        swi_toks[(e, tt, hc)] = (tok_as, tok_ru)
                        if k > 0:
                            emit_down(k - 1, hc)
                    pair_gate_last[k] = gate_last
                    if k + 2 < len(pairs):
                        e2, tt2 = pairs[k + 2]
                        E.w("dy", pair_gate_last[k])
                        t_x1sl[k % 2] = load_x1(k + 2, e2, tt2)
                    if tt == 0 and e + 1 < NEXP:
                        t_wg[e + 1] = load_wg(
                            e + 1, pair_gate_last.get(4 * (e - 1) + 3)
                            if e >= 1 else None)
                    if e == NEXP - 1:
                        if tt == 1:
                            stage_casts(0)
                        elif tt == 2:
                            stage_casts(1)
                            stage_stats(0, pcw[:, 0:512], pcw[:, 512:1024],
                                        tok_cwb)
                        elif tt == 3:
                            stage_casts(2)
                            swi31 = [swi_toks[(7, 3, h)][0] for h in range(NCH)]
                            stage_stats(1, pg0[:], pg1[:], *swi31)
                            stage_apply(0, pcw[:, 0:512], pcw[:, 512:1024])
                        E.mode(("mm", "k128", "bf16"))
                for m2_ in range(NCH):
                    emit_down(len(pairs) - 1, m2_)
                stage_casts(3)
                stage_apply(1, pu0[:], pu1[:], *swi31)
                stage_stats(2, pcw[:, 0:512], pcw[:, 512:1024], ln2_apply[0])
                stage_stats(3, pd0[:], pd1[:], down_ev[(7, 3, NCH - 1)])
                stage_apply(2, pg0[:], pg1[:], *rows2[1])
                stage_apply(3, pcw[:, 0:512], pcw[:, 512:1024], *rows2[2])
                for chv, tokv in tok_st.items():
                    eng = ("dsc" if chv.startswith("dsc")
                           else "dg" if chv.startswith("dg") else "dy")
                    E.w(eng, tokv)
                E.w("pe", ln2_apply[3])

        if "accmoe" in DBG:
            with nc.sbuf_tensor("dbgacc", [128, T], F32) as tmpd:
                for c in range(NCH):
                    E.w("dve", dbias_ev[(c, 3)])
                    tok = E.tk(dve.tensor_copy(tmpd[:], acc[:, c * T:(c + 1) * T]),
                               "dve")
                    E.w("dy", tok)
                    tok2 = E.dma("dy4", dbg_out["accmoe"][c * 128:(c + 1) * 128, :],
                                 tmpd[:])
                    E.w("dve", tok2)

    return nc


def kernel(L0_attn_x, L0_attn_W_attn, L0_attn_b_attn, L0_attn_W_proj, L0_attn_b_proj,
           L0_ffn_router_w, L0_ffn_router_b, L0_ffn_gate_up_w, L0_ffn_gate_up_b,
           L0_ffn_down_w, L0_ffn_down_b):
    args = [np.asarray(v, np.float32) for v in
            (L0_attn_x, L0_attn_W_attn, L0_attn_b_attn, L0_attn_W_proj,
             L0_attn_b_proj, L0_ffn_router_w, L0_ffn_router_b, L0_ffn_gate_up_w,
             L0_ffn_gate_up_b, L0_ffn_down_w, L0_ffn_down_b)]
    in_maps = _host_layout(*args)
    nc = build_nc()
    res = run_bass_kernel_spmd(nc, in_maps, core_ids=list(range(8)))
    return np.stack([np.ascontiguousarray(r["out"].T) for r in res.results], axis=0)


# version 55
# speedup vs baseline: 1.0116x; 1.0071x over previous
import numpy as np
import ml_dtypes
import concourse.bass as bass
import concourse.mybir as mybir
from concourse.bass_utils import run_bass_kernel_spmd
from contextlib import ExitStack

F32 = mybir.dt.float32
BF16 = mybir.dt.bfloat16
AF = mybir.ActivationFunctionType
ALU = mybir.AluOpType
AX = mybir.AxisListType

T = 2048
D = 1024
NCH = 8
NEXP = 8
NEG = -20.0
EPS = 1e-5

DBG = []


class Emit:
    DMACH = ("dy0", "dy1", "dy2", "dy3", "dy4", "dg0", "dg1", "dg2",
             "dsc0", "dsc1")

    def __init__(self, nc, stack):
        self.nc = nc
        self.sem = {}
        for k in ("pe", "act", "dve", "pool") + self.DMACH:
            self.sem[k] = stack.enter_context(nc.semaphore("sem_" + k))
        self.cnt = {k: 0 for k in self.sem}
        self.wcache = {}
        self.pe_mode = None
        self.eng = {"pe": nc.tensor, "act": nc.scalar, "dve": nc.vector,
                    "pool": nc.gpsimd, "dg": nc.gpsimd, "dy": nc.sync,
                    "dsc": nc.scalar}
        self.dmaeng = {}
        for ch in self.DMACH:
            self.dmaeng[ch] = (nc.sync if ch.startswith("dy")
                               else nc.gpsimd if ch.startswith("dg")
                               else nc.scalar)

    def tk(self, instr, key, step=1):
        self.cnt[key] += step
        instr.then_inc(self.sem[key], step)
        return (key, self.cnt[key])

    def w(self, consumer, *tokens):
        eng = self.eng[consumer]
        best = {}
        for tok in tokens:
            if tok is None:
                continue
            key, val = tok
            if val > best.get(key, -1):
                best[key] = val
        for key, val in best.items():
            ck = (consumer, key)
            if self.wcache.get(ck, 0) >= val:
                continue
            eng.wait_ge(self.sem[key], val)
            self.wcache[ck] = val

    def bar(self):
        return tuple((k, self.cnt[k]) for k in self.cnt)

    def mode(self, m):
        if self.pe_mode != m:
            self.nc.tensor.drain()
            self.pe_mode = m

    def dma(self, ch, out, in_):
        eng = self.dmaeng[ch]
        if self.cnt[ch]:
            ck = ("q" + ch, ch)
            if self.wcache.get(ck, 0) < self.cnt[ch]:
                eng.wait_ge(self.sem[ch], self.cnt[ch])
                self.wcache[ck] = self.cnt[ch]
        return self.tk(eng.dma_start(out=out, in_=in_), ch, 16)


def _host_layout(x, W_attn, b_attn, W_proj, b_proj,
                 router_w, router_b, gate_up_w, gate_up_b, down_w, down_b):
    fl = np.ascontiguousarray
    DH = 64
    head_perm = np.concatenate([np.arange(0, DH, 2), np.arange(1, DH, 2)])
    qk_cols = []
    for m in range(16):
        base = 0 if m < 8 else D
        hA = 2 * (m % 8)
        qk_cols.append(np.concatenate([base + hA * DH + head_perm,
                                       base + (hA + 1) * DH + head_perm]))
    qk_cols = np.stack(qk_cols)

    wqk = np.empty((128, 16 * NCH * 128), np.float32)
    for m in range(16):
        sub = W_attn[:, qk_cols[m]].reshape(NCH, 128, 128)
        wqk[:, m * NCH * 128:(m + 1) * NCH * 128] = \
            np.transpose(sub, (1, 0, 2)).reshape(128, NCH * 128)
    bqk = np.stack([b_attn[qk_cols[m]] for m in range(16)], axis=1)

    wv = fl(np.transpose(W_attn[:, 2 * D:3 * D].reshape(NCH, 128, D),
                         (1, 0, 2)).reshape(128, NCH * D))
    bv = fl(b_attn[2 * D:3 * D][None, :])

    tmp = np.transpose(W_proj.reshape(NCH, 128, NCH, 128), (1, 2, 0, 3))
    wproj = fl(tmp.reshape(128, NCH * NCH * 128))
    bproj = fl(b_proj.reshape(NCH, 128).T)

    rwt = fl(np.transpose(router_w.reshape(NEXP, NCH, 128), (2, 1, 0))
             .reshape(128, NCH * NEXP))
    rbr = fl(router_b[None, :])

    wg = np.empty((NEXP, 128, 16 * NCH * 128), np.float32)
    gub = np.empty((128, NEXP * 16), np.float32)
    for e in range(NEXP):
        gu = np.empty((16, NCH, 128, 128), np.float32)
        for m in range(16):
            if m < 8:
                rows = 2 * (m * 128 + np.arange(128))
            else:
                rows = 2 * ((m - 8) * 128 + np.arange(128)) + 1
            sub = gate_up_w[e][rows, :]
            gu[m] = np.transpose(sub.reshape(128, NCH, 128), (1, 2, 0))
            gub[:, e * 16 + m] = gate_up_b[e][rows]
        wg[e] = np.transpose(gu, (2, 0, 1, 3)).reshape(128, 16 * NCH * 128)

    wd = np.empty((NEXP, 128, NCH * NCH * 128), np.float32)
    for e in range(NEXP):
        dwe = np.empty((NCH, NCH, 128, 128), np.float32)
        for m2 in range(NCH):
            sub = down_w[e][m2 * 128:(m2 + 1) * 128, :]
            dwe[m2] = np.transpose(sub.reshape(128, NCH, 128), (1, 2, 0))
        wd[e] = np.transpose(dwe, (2, 0, 1, 3)).reshape(128, NCH * NCH * 128)
    dbt = fl(down_b)

    i32 = np.arange(32, dtype=np.float64)
    theta = 10000.0 ** (-i32 / 32.0)
    ang = np.arange(T, dtype=np.float64)[:, None] * theta[None, :]
    cos = np.cos(ang).T.astype(np.float32)
    sin = np.sin(ang).T.astype(np.float32)
    cc = np.concatenate([cos, cos, cos, cos], axis=0)
    ss = np.concatenate([-sin, sin, -sin, sin], axis=0)
    kq = np.arange(128)
    cmask = (kq[None, :] >= kq[:, None]).astype(np.float32)
    ident = np.eye(128, dtype=np.float32)
    onesc = np.ones((128, 1), np.float32)
    onesr = np.ones((1, 128), np.float32)
    consts = np.zeros((128, 8), np.float32)
    consts[:, 0] = NEG
    consts[:, 1] = EPS
    consts[:, 2] = 1.0
    consts[:, 4] = 7.0
    onesq = np.ones((128, 128), np.float32)
    sel = np.zeros((128, NCH * 128), np.float32)
    for c in range(NCH):
        sel[c, c * 128:c * 128 + 64] = 1.0
        sel[8 + c, c * 128 + 64:(c + 1) * 128] = 1.0

    bf = ml_dtypes.bfloat16
    wqk, wv, wproj = wqk.astype(bf), wv.astype(bf), wproj.astype(bf)
    wg, wd = wg.astype(bf), wd.astype(bf)
    shared = dict(wqk=wqk, bqk=bqk, wv=wv, bv=bv, wproj=wproj, bproj=bproj,
                  rwt=rwt, rbr=rbr, wg=wg, gub=gub, wd=wd, dbt=dbt,
                  cc=cc, ss=ss, cmask=cmask, ident=ident,
                  onesc=onesc, onesr=onesr, consts=consts, sel=sel,
                  onesq=onesq)
    per_core = []
    for b in range(x.shape[0]):
        m = dict(shared)
        m["x_t"] = fl(x[b].T)
        per_core.append(m)
    return per_core


def _ln_cast(nc, E, acc, bf16buf, sq16, consts, tt, m, ready_m, prev_apply,
             bstride=None, cast_eng="dve"):
    act = nc.scalar
    eng = nc.gpsimd if cast_eng == "pool" else nc.vector
    sl = slice(m * T + tt * 512, m * T + (tt + 1) * 512)
    bsl = (slice(m * T + tt * 512, m * T + (tt + 1) * 512) if bstride is None
           else slice(m * bstride, m * bstride + 512))
    E.w("act", ready_m, prev_apply)
    sq_tok = E.tk(act.activation(sq16[:, m * 512:(m + 1) * 512],
                                 acc[:, sl], AF.Square,
                                 bias=consts[:, 3:4], scale=1.0), "act")
    E.w(cast_eng, ready_m, prev_apply)
    cast_tok = E.tk(eng.tensor_copy(bf16buf[:, bsl], acc[:, sl]), cast_eng)
    return cast_tok, sq_tok


def _ln_stats(nc, E, acc, bf16buf, sq16, srB, srC, onesc, consts,
              tt, ready, pstat, psq, prev_apply, bstride=None,
              cast_toks=None, sq_toks=None):
    pe, act, dve = nc.tensor, nc.scalar, nc.vector

    def bsl(m):
        if bstride is None:
            return slice(m * T + tt * 512, m * T + (tt + 1) * 512)
        return slice(m * bstride, m * bstride + 512)
    if cast_toks is None:
        cast_toks, sq_toks = {}, {}
        for m in range(NCH):
            cast_toks[m], sq_toks[m] = _ln_cast(nc, E, acc, bf16buf, sq16,
                                                consts, tt, m, ready[m],
                                                prev_apply, bstride)
    E.mode(("mm", "k128", "bf16"))
    for m in range(NCH):
        E.w("pe", cast_toks[m], sq_toks[m])
        pe.matmul(pstat[:, 0:512], onesc[:, 0:128], bf16buf[:, bsl(m)],
                  start=(m == 0), stop=(m == NCH - 1))
    mm2 = None
    for m in range(NCH):
        mm2 = pe.matmul(psq[:, 0:512], onesc[:, 0:128],
                        sq16[:, m * 512:(m + 1) * 512],
                        start=(m == 0), stop=(m == NCH - 1))
    tok_stat = E.tk(mm2, "pe")
    E.w("dve", tok_stat)
    tokA = E.tk(dve.tensor_scalar(srC[0:1, :], pstat[0:1, 0:512], 1.0 / D, None,
                                  ALU.mult), "dve")
    E.w("act", tokA)
    tokQ = E.tk(act.activation(srB[0:1, :], srC[0:1, :], AF.Square,
                               bias=consts[0:1, 3:4], scale=1.0), "act")
    E.w("dve", tokQ)
    tokD = E.tk(dve.scalar_tensor_tensor(srB[0:1, :], psq[0:1, 0:512], 1.0 / D,
                                         srB[0:1, :], ALU.mult, ALU.subtract),
                "dve")
    E.w("act", tokD)
    tokE = E.tk(act.activation(srB[0:1, :], srB[0:1, :], AF.Ln,
                               bias=consts[0:1, 1:2], scale=1.0), "act")
    E.w("act", tokE)
    tokF = E.tk(act.activation(srB[0:1, :], srB[0:1, :], AF.Exp,
                               bias=consts[0:1, 3:4], scale=-0.5), "act")
    E.w("dve", tokF)
    tokG = E.tk(dve.tensor_tensor(srC[0:1, :], srC[0:1, :], srB[0:1, :],
                                  ALU.mult), "dve")
    return (tokF, tokG)


def _ln_apply(nc, E, acc, bf16buf, srB, srC, onesrf, tt, row_toks,
              out_cast, pb1, pb2, bstride=None, consts=None, ret_per_m=False):
    pe, dve, act = nc.tensor, nc.vector, nc.scalar

    def bsl(m):
        if bstride is None:
            return slice(m * T + tt * 512, m * T + (tt + 1) * 512)
        return slice(m * bstride, m * bstride + 512)
    E.mode(("mm", "k1", "f32"))
    E.w("pe", row_toks[0], row_toks[1])
    tok_b1 = E.tk(pe.matmul(pb1[:, 0:512], onesrf[0:1, :], srB[0:1, :],
                            start=True, stop=True), "pe")
    tok_b2 = E.tk(pe.matmul(pb2[:, 0:512], onesrf[0:1, :], srC[0:1, :],
                            start=True, stop=True), "pe")
    out_tok = None
    per_m = []
    for m in range(NCH):
        sl = slice(m * T + tt * 512, m * T + (tt + 1) * 512)
        E.w("dve", tok_b1, tok_b2)
        tok8 = E.tk(dve.tensor_tensor(acc[:, sl], acc[:, sl], pb1[:, 0:512],
                                      ALU.mult), "dve")
        E.w("dve", tok8)
        tok9 = E.tk(dve.tensor_tensor(acc[:, sl], acc[:, sl], pb2[:, 0:512],
                                      ALU.subtract), "dve")
        out_tok = tok9
        per_m.append(tok9)
        if out_cast:
            E.w("act", tok9)
            out_tok = E.tk(act.activation(bf16buf[:, bsl(m)], acc[:, sl],
                                          AF.Identity, bias=consts[:, 3:4],
                                          scale=1.0), "act")
    return (out_tok, per_m) if ret_per_m else out_tok


def build_nc():
    nc = bass.Bass("TRN2", target_bir_lowering=False, debug=False)
    def I(name, shape):
        return nc.dram_tensor(name, list(shape), F32, kind="ExternalInput").ap()

    def IB(name, shape):
        return nc.dram_tensor(name, list(shape), BF16,
                              kind="ExternalInput").ap()

    x_t = I("x_t", (D, T))
    wqk_i = IB("wqk", (128, 16 * NCH * 128))
    bqk_i = I("bqk", (128, 16))
    wv_i = IB("wv", (128, NCH * D))
    bv_i = I("bv", (1, D))
    wproj_i = IB("wproj", (128, NCH * NCH * 128))
    bproj_i = I("bproj", (128, NCH))
    rwt_i = I("rwt", (128, NCH * NEXP))
    rbr_i = I("rbr", (1, NEXP))
    wg_i = IB("wg", (NEXP, 128, 16 * NCH * 128))
    gub_i = I("gub", (128, NEXP * 16))
    wd_i = IB("wd", (NEXP, 128, NCH * NCH * 128))
    dbt_i = I("dbt", (NEXP, D))
    cc_i = I("cc", (128, T))
    ss_i = I("ss", (128, T))
    cmask_i = I("cmask", (128, 128))
    ident_i = I("ident", (128, 128))
    onesc_i = I("onesc", (128, 1))
    onesr_i = I("onesr", (1, 128))
    consts_i = I("consts", (128, 8))
    sel_i = I("sel", (128, NCH * 128))
    onesq_i = I("onesq", (128, 128))
    out = nc.dram_tensor("out", [D, T], F32, kind="ExternalOutput").ap()
    ys_d = nc.dram_tensor("ys_d", [130, 32 * 512], BF16).ap()
    x1_d = nc.dram_tensor("x1_d", [128, NCH * T], BF16).ap()

    dbg_out = {}
    for name, shape in (("vaug", (128, 16 * 1056)),
                        ("x1", (D, T)), ("cw", (128, 128)),
                        ("accmoe", (D, T))):
        if name in DBG:
            dbg_out[name] = nc.dram_tensor("dbg_" + name, list(shape), F32,
                                           kind="ExternalOutput").ap()

    with ExitStack() as st:
        a = st.enter_context
        E = Emit(nc, st)
        pe, act, dve = nc.tensor, nc.scalar, nc.vector

        bqk = a(nc.sbuf_tensor("bqk_s", [128, 16], F32))
        bproj = a(nc.sbuf_tensor("bproj_s", [128, NCH], F32))
        rwt = a(nc.sbuf_tensor("rwt_s", [128, NCH * NEXP], F32))
        rbr_row = a(nc.sbuf_tensor("rbr_s", [1, NEXP], F32))
        gubs = a(nc.sbuf_tensor("gubs_s", [128, NEXP * 16], F32))
        gub7 = a(nc.sbuf_tensor("gub7_s", [128, NEXP * 16], F32))
        dbt16 = a(nc.sbuf_tensor("dbt16_s", [128, D], BF16))
        onesq16 = a(nc.sbuf_tensor("onesq_s", [128, 128], BF16))
        cmask = a(nc.sbuf_tensor("cmask_s", [128, 128], BF16))
        onesc = a(nc.sbuf_tensor("onesc_s", [128, 1], BF16))
        onesr = a(nc.sbuf_tensor("onesr_s", [1, 128], BF16))
        onesrf = a(nc.sbuf_tensor("onesrf_s", [1, 128], F32))
        consts = a(nc.sbuf_tensor("consts_s", [128, 8], F32))
        rbrep = a(nc.sbuf_tensor("rbrep_s", [128, NEXP], F32))
        cwT16 = a(nc.sbuf_tensor("cwT16", [128, T], BF16))

        t_bqk = E.dma("dy4", bqk[:], bqk_i)
        t_bproj = E.dma("dy3", bproj[:], bproj_i)
        t_rwt = E.dma("dy2", rwt[:], rwt_i)
        t_rbr = E.dma("dy1", rbr_row[:], rbr_i)
        tok_zpad = E.tk(dve.memset(dbt16[:], 0.0), "dve")
        tok_zcw = E.tk(dve.memset(cwT16[:], 0.0), "dve")
        E.w("dg", tok_zpad)
        t_dbt = E.dma("dg0", dbt16[0:NEXP, :], dbt_i)
        t_oq = E.dma("dg1", onesq16[:], onesq_i)
        t_cm = E.dma("dg1", cmask[:], cmask_i)
        t_oc = E.dma("dg2", onesc[:], onesc_i)
        t_or = E.dma("dg2", onesr[:], onesr_i)
        t_orf = E.dma("dsc0", onesrf[:], onesr_i)
        t_cst = E.dma("dsc1", consts[:], consts_i)

        with nc.sbuf_tensor("gub_s", [128, NEXP * 16], F32) as gub, \
             nc.psum_tensor("pb_init", [128, 512], F32) as pbi:
            t_gub = E.dma("dy0", gub[:], gub_i)
            E.w("act", t_gub, t_cst)
            t_gubs = None
            for e2 in range(NEXP):
                act.activation(gubs[:, e2 * 16:e2 * 16 + 8],
                               gub[:, e2 * 16:e2 * 16 + 8],
                               AF.Identity, bias=consts[:, 3:4], scale=1.702)
                i2 = act.activation(gubs[:, e2 * 16 + 8:e2 * 16 + 16],
                                    gub[:, e2 * 16 + 8:e2 * 16 + 16],
                                    AF.Identity, bias=consts[:, 2:3], scale=1.0)
                t_gubs = E.tk(i2, "act")
            i3 = act.activation(gub7[:], gub[:], AF.Identity,
                                bias=consts[:, 4:5], scale=1.0)
            t_gub7 = E.tk(i3, "act")
            E.w("pe", t_gubs, t_gub7)
            E.mode(("mm", "k1", "f32"))
            E.w("pe", t_orf, t_rbr)
            tok_mm = E.tk(pe.matmul(pbi[:, 0:8], onesrf[0:1, :], rbr_row[0:1, :],
                                    start=True, stop=True), "pe")
            E.w("dve", tok_mm)
            t_rbrep = E.tk(dve.tensor_copy(rbrep[:], pbi[:, 0:8]), "dve")
            E.w("pe", t_rbrep)

        t_attn_done = None
        t_yt_dma = None
        with ExitStack() as s12:
            b = s12.enter_context
            bvrep = b(nc.sbuf_tensor("bvrep_s", [128, D], BF16))
            with nc.sbuf_tensor("bv_row", [1, D], F32) as bv_s, \
                 nc.psum_tensor("pb_init2", [128, 512], F32) as pbi:
                t_bv = E.dma("dy1", bv_s[:], bv_i)
                E.mode(("mm", "k1", "f32"))
                E.w("pe", t_bv, t_orf)
                for half in range(2):
                    tok_mm = E.tk(pe.matmul(pbi[:, 0:512], onesrf[0:1, :],
                                            bv_s[0:1, half * 512:(half + 1) * 512],
                                            start=True, stop=True), "pe")
                    E.w("dve", tok_mm)
                    t_bvr = E.tk(dve.tensor_copy(
                        bvrep[:, half * 512:(half + 1) * 512],
                        pbi[:, 0:512]), "dve")
                    E.w("pe", t_bvr)
            qkraw = b(nc.sbuf_tensor("qkraw", [128, 16 * T], BF16))
            vaug = b(nc.sbuf_tensor("vaug", [128, 16 * 1056], BF16))
            cc16 = b(nc.sbuf_tensor("cc16", [128, T], BF16))
            ss16 = b(nc.sbuf_tensor("ss16", [128, T], BF16))
            qswp = [b(nc.sbuf_tensor(f"qswp{i}", [128, T], BF16))
                    for i in range(2)]
            rt1 = b(nc.sbuf_tensor("rt1", [128, T], BF16))
            rt2 = b(nc.sbuf_tensor("rt2", [128, T], BF16))

            with ExitStack() as s1:
                b1 = s1.enter_context
                wqk16 = b1(nc.sbuf_tensor("wqk16", [128, 16 * NCH * 128], BF16))
                wv16 = b1(nc.sbuf_tensor("wv16", [128, NCH * D], BF16))
                xsl = [b1(nc.sbuf_tensor(f"xsl{i}", [128, NCH * 512], BF16))
                       for i in range(2)]
                def load_xsl(par, tt):
                    return E.dma(
                        "dg1" if par == 0 else "dg2",
                        xsl[par][:].rearrange("p (c t) -> p c t", t=512),
                        x_t[:, tt * 512:(tt + 1) * 512]
                        .rearrange("(c p) t -> p c t", p=128))

                t_xsl = [load_xsl(0, 0), None]
                SW = 4 * NCH * 128
                t_wqk = [E.dma("dg0", wqk16[:, 0 * SW:1 * SW], wqk_i[:, 0 * SW:1 * SW])]
                t_xsl[1] = load_xsl(1, 1)
                t_wqk += [E.dma("dg0", wqk16[:, s_ * SW:(s_ + 1) * SW],
                                wqk_i[:, s_ * SW:(s_ + 1) * SW]) for s_ in (1, 2, 3)]
                t_wv = E.dma("dg0", wv16[:], wv_i)
                t_cc = E.dma("dg0", cc16[:], cc_i)
                t_ss = E.dma("dg0", ss16[:], ss_i)
                tok_vz = E.tk(dve.memset(vaug[:], 0.0), "dve")

                with nc.psum_tensor("pq0", [128, 512], F32) as pq0, \
                     nc.psum_tensor("pq1", [128, 512], F32) as pq1, \
                     nc.psum_tensor("pv0", [128, 1024], F32) as pv0, \
                     nc.psum_tensor("pv1", [128, 1024], F32) as pv1:
                    pq, pv = [pq0, pq1], [pv0, pv1]
                    E.mode(("mm", "k128", "bf16"))
                    qk_ev = {}
                    v_ev = {0: None, 1: None}
                    xsl_users = [None, None]
                    for tt in range(4):
                        par = tt % 2
                        if tt >= 2:
                            E.w("dg", xsl_users[par])
                            t_xsl[par] = load_xsl(par, tt)
                        for m in range(16):
                            mp = m % 2
                            E.w("pe", t_wqk[m // 4], t_xsl[par])
                            if (m - 2, tt) in qk_ev:
                                E.w("pe", qk_ev[(m - 2, tt)])
                            elif tt > 0 and (m + 14, tt - 1) in qk_ev:
                                E.w("pe", qk_ev[(m + 14, tt - 1)])
                            mm_last = None
                            for c in range(NCH):
                                mm_last = pe.matmul(
                                    pq[mp][:, 0:512],
                                    wqk16[:, (m * NCH + c) * 128:(m * NCH + c + 1) * 128],
                                    xsl[par][:, c * 512:(c + 1) * 512],
                                    start=(c == 0), stop=(c == NCH - 1))
                            tok_mm = E.tk(mm_last, "pe")
                            E.w("act", tok_mm, t_bqk)
                            ei = act.activation(
                                qkraw[:, m * T + tt * 512:m * T + (tt + 1) * 512],
                                pq[mp][:, 0:512], AF.Identity,
                                bias=bqk[:, m:m + 1], scale=1.0)
                            qk_ev[(m, tt)] = E.tk(ei, "act")
                        for vi in range(4):
                            ti = tt * 4 + vi
                            vp = vi % 2
                            E.w("pe", t_wv)
                            if ti >= 2:
                                E.w("pe", v_ev[vp])
                            mm_last = None
                            for half in range(2):
                                for c in range(NCH):
                                    mm_last = pe.matmul(
                                        pv[vp][:, half * 512:(half + 1) * 512],
                                        xsl[par][:, c * 512 + vi * 128:
                                                 c * 512 + (vi + 1) * 128],
                                        wv16[:, c * D + half * 512:
                                             c * D + (half + 1) * 512],
                                        start=(c == 0), stop=(c == NCH - 1))
                            tok_mm = E.tk(mm_last, "pe")
                            E.w("dve", tok_mm, t_bvr, tok_vz)
                            ovw = vaug[:, ti * 1056:(ti + 1) * 1056]
                            ovw = ovw.rearrange("p (h x) -> p h x", x=66)[:, :, 0:64]
                            v_ev[vp] = E.tk(dve.tensor_tensor(ovw, pv[vp][:],
                                                              bvrep[:], ALU.add),
                                            "dve")
                        xsl_users[par] = tok_mm
                    E.w("pe", qk_ev[(14, 3)], qk_ev[(15, 3)], v_ev[0], v_ev[1])
                E.w("dve", v_ev[0], v_ev[1])
                tok_vones = None
                for ti in range(16):
                    vv = vaug[:, ti * 1056:(ti + 1) * 1056]
                    vv = vv.rearrange("p (h x) -> p h x", x=66)[:, :, 64:65]
                    tok_vones = E.tk(dve.memset(vv, 1.0), "dve")

            with ExitStack() as s2:
                b2 = s2.enter_context
                kpadB = b2(nc.sbuf_tensor("kpadB", [128, 8 * T], BF16))
                attAB = [b2(nc.sbuf_tensor(f"attAB{i}", [128, 1024], BF16))
                         for i in range(3)]
                ytst = [b2(nc.sbuf_tensor(f"ytst{i}", [65, 1024], BF16))
                        for i in range(2)]
                pe_end = E.bar()
                tok_kbz = None

                def emit_rope(m, prev_tok):
                    par = m % 2
                    E.w("dy", *[qk_ev[(m, ttx)] for ttx in range(4)])
                    if prev_tok[par] is not None:
                        E.w("dy", prev_tok[par][1])
                    tok_sw = []
                    chs = ("dy0", "dy1") if m < 8 else ("dy2", "dy3")
                    for bi, (blk, srb) in enumerate(((0, 1), (1, 0),
                                                    (2, 3), (3, 2))):
                        tok_sw.append(E.dma(chs[bi % 2],
                                            qswp[par][blk * 32:(blk + 1) * 32, :],
                                            qkraw[srb * 32:(srb + 1) * 32,
                                                  m * T:(m + 1) * T]))
                    E.w("dve", *[qk_ev[(m, ttx)] for ttx in range(4)])
                    E.w("dve", t_cc, t_ss)
                    if prev_tok[1 - par] is not None:
                        E.w("dve", prev_tok[1 - par][0])
                    if prev_tok[par] is not None:
                        E.w("dve", prev_tok[par][0])
                    tok_t1 = E.tk(dve.tensor_tensor(rt1[:],
                                                    qkraw[:, m * T:(m + 1) * T],
                                                    cc16[:], ALU.mult), "dve")
                    E.w("dve", *tok_sw)
                    tok_t2 = E.tk(dve.tensor_tensor(rt2[:], qswp[par][:], ss16[:],
                                                    ALU.mult), "dve")
                    E.w("dve", tok_t1, tok_t2)
                    if m < 8:
                        tok_t3 = E.tk(dve.tensor_tensor(qkraw[:, m * T:(m + 1) * T],
                                                        rt1[:], rt2[:], ALU.add),
                                      "dve")
                        return (tok_t3, tok_t2), tok_t3
                    c = m - 8
                    nonlocal tok_kbz
                    if tok_kbz is None:
                        E.w("dve", *pe_end)
                        tok_kbz = E.tk(dve.memset(kpadB[0:64, :], 0.0), "dve")
                    E.tk(dve.tensor_tensor(qkraw[0:64, m * T:(m + 1) * T],
                                           rt1[0:64, :], rt2[0:64, :], ALU.add),
                         "dve")
                    tok_b = E.tk(dve.tensor_tensor(kpadB[64:128, c * T:(c + 1) * T],
                                                   rt1[64:128, :], rt2[64:128, :],
                                                   ALU.add), "dve")
                    tok_z = E.tk(dve.memset(qkraw[64:128, m * T:(m + 1) * T], 0.0),
                                 "dve")
                    return (tok_z, tok_t2), tok_z

                with nc.psum_tensor("psab0", [128, 1024], F32) as psab0, \
                     nc.psum_tensor("psab1", [128, 1024], F32) as psab1, \
                     nc.psum_tensor("psy0", [128, 1024], F32) as psy0, \
                     nc.psum_tensor("psy1", [128, 1024], F32) as psy1:
                    psAB = [psab0, psab1]
                    psy = [psy0, psy1]
                    E.mode(("mm", "k128", "bf16"))
                    rope_prev = [None, None]
                    rope_q = {}
                    rope_k = {}

                    def rope0_piece(m, ttx, prev):
                        cs = slice(ttx * 512, (ttx + 1) * 512)
                        qs = slice(m * T + ttx * 512, m * T + (ttx + 1) * 512)
                        chs = ("dy0", "dy1") if m < 8 else ("dy2", "dy3")
                        qsw = qswp[0] if m < 8 else qswp[1]
                        E.w("dy", qk_ev[(m, ttx)])
                        tok_sw = [E.dma(chs[bi % 2],
                                        qsw[blk * 32:(blk + 1) * 32, cs],
                                        qkraw[srb * 32:(srb + 1) * 32, qs])
                                  for bi, (blk, srb) in enumerate(
                                      ((0, 1), (1, 0), (2, 3), (3, 2)))]
                        if m >= 8:
                            E.tk(dve.memset(kpadB[0:64, cs], 0.0), "dve")
                        E.w("dve", qk_ev[(m, ttx)], t_cc, t_ss, prev)
                        tok_t1 = E.tk(dve.tensor_tensor(rt1[:, cs], qkraw[:, qs],
                                                        cc16[:, cs], ALU.mult),
                                      "dve")
                        E.w("dve", *tok_sw)
                        tok_t2 = E.tk(dve.tensor_tensor(rt2[:, cs], qsw[:, cs],
                                                        ss16[:, cs], ALU.mult),
                                      "dve")
                        E.w("dve", tok_t1, tok_t2)
                        if m < 8:
                            return E.tk(dve.tensor_tensor(
                                qkraw[:, qs], rt1[:, cs], rt2[:, cs],
                                ALU.add), "dve"), tok_t2
                        E.tk(dve.tensor_tensor(qkraw[0:64, qs], rt1[0:64, cs],
                                               rt2[0:64, cs], ALU.add), "dve")
                        E.tk(dve.tensor_tensor(kpadB[64:128, cs],
                                               rt1[64:128, cs], rt2[64:128, cs],
                                               ALU.add), "dve")
                        return E.tk(dve.memset(qkraw[64:128, qs], 0.0),
                                    "dve"), tok_t2

                    E.w("dve", *pe_end)
                    q0t, k0t = [], []
                    qt2 = kt2 = None
                    pv = None
                    for ttx in range(4):
                        a_, qt2 = rope0_piece(0, ttx, pv)
                        q0t.append(a_)
                        b_, kt2 = rope0_piece(8, ttx, a_)
                        k0t.append(b_)
                        pv = b_
                    tok_kbz = E.tk(dve.memset(kpadB[0:64, T:8 * T], 0.0), "dve")
                    rope_prev[0] = (q0t[-1], qt2)
                    rope_prev[1] = (k0t[-1], kt2)
                    rope_q[0] = q0t
                    rope_k[0] = k0t
                    exp_tok = [None, None]
                    av_tok = [None, None, None]
                    evac_tok = [None, None]
                    spill_tok = [None, None]
                    qi = 0
                    for cch in range(8):
                        hA, hB = 2 * cch, 2 * cch + 1
                        qcol = cch * T
                        kcol = (8 + cch) * T
                        for Q in range(4):
                            if Q == 2 and cch + 1 < 8:
                                rope_prev[(cch + 1) % 2], rope_q[cch + 1] = \
                                    emit_rope(cch + 1, rope_prev)
                                rope_prev[(cch + 9) % 2], rope_k[cch + 1] = \
                                    emit_rope(cch + 9, rope_prev)
                            tq0 = Q * 512
                            jmax = 4 * Q + 4
                            qp = qi % 2

                            def jslice(j):
                                tqs = max(tq0, 128 * j)
                                return tqs, tq0 + 512 - tqs

                            def emit_scores(j):
                                par = j % 2
                                ab = j % 3
                                tqs, w_ = jslice(j)
                                rq, rk = rope_q[cch], rope_k[cch]
                                E.w("pe", exp_tok[par],
                                    rq[Q] if isinstance(rq, list) else rq,
                                    rk[j // 4] if isinstance(rk, list) else rk,
                                    tok_kbz if cch > 0 else None)
                                mA = pe.matmul(
                                    psAB[par][:, 0:w_],
                                    qkraw[:, kcol + 128 * j:kcol + 128 * j + 128],
                                    qkraw[:, qcol + tqs:qcol + tq0 + 512],
                                    start=True, stop=True)
                                E.tk(mA, "pe")
                                mB = pe.matmul(
                                    psAB[par][:, 512:512 + w_],
                                    kpadB[:, cch * T + 128 * j:cch * T + 128 * j + 128],
                                    qkraw[:, qcol + tqs:qcol + tq0 + 512],
                                    start=True, stop=True)
                                tok_mm = E.tk(mB, "pe")
                                E.w("act", tok_mm, t_cst, av_tok[ab])
                                src3 = psAB[par][:].rearrange(
                                    "p (b t) -> p b t", t=512)[:, :, 0:w_]
                                dst3 = attAB[ab][:].rearrange(
                                    "p (b t) -> p b t", t=512)[:, :, 0:w_]
                                eAB = act.activation(dst3, src3, AF.Exp,
                                                     bias=consts[:, 0:1],
                                                     scale=0.125)
                                tok_e = E.tk(eAB, "act")
                                if 128 * j >= tq0:
                                    E.w("pool", tok_e, t_cm)
                                    mdst = attAB[ab][:].rearrange(
                                        "p (b t) -> p b t", t=512)[:, :, 0:128]
                                    mi = nc.gpsimd.tensor_tensor(
                                        mdst, mdst,
                                        cmask[:].unsqueeze(1)
                                        .broadcast_to((128, 2, 128)), ALU.mult)
                                    tok_e = E.tk(mi, "pool")
                                exp_tok[par] = tok_e

                            def emit_av(j):
                                par = j % 2
                                ab = j % 3
                                tqs, w_ = jslice(j)
                                E.w("pe", exp_tok[par], tok_vones)
                                if j == 0:
                                    E.w("pe", evac_tok[qp])
                                aA = pe.matmul(
                                    psy[qp][0:65, tqs - tq0:512],
                                    vaug[:, j * 1056 + hA * 66:
                                         j * 1056 + hA * 66 + 65],
                                    attAB[ab][:, 0:w_],
                                    start=(j == 0), stop=(j == jmax - 1))
                                E.tk(aA, "pe")
                                aB = pe.matmul(
                                    psy[qp][0:65, 512 + tqs - tq0:1024],
                                    vaug[:, j * 1056 + hB * 66:
                                         j * 1056 + hB * 66 + 65],
                                    attAB[ab][:, 512:512 + w_],
                                    start=(j == 0), stop=(j == jmax - 1))
                                av_tok[ab] = E.tk(aB, "pe")

                            emit_scores(0)
                            emit_scores(1)
                            for j in range(2, jmax):
                                emit_av(j - 2)
                                emit_scores(j)
                            emit_av(jmax - 2)
                            emit_av(jmax - 1)
                            t_av = av_tok[(jmax - 1) % 3]
                            sp = qi % 2
                            E.w("dve", t_av, spill_tok[sp], evac_tok[sp])
                            ev = dve.tensor_copy(ytst[sp][:], psy[qp][0:65, 0:1024])
                            tok_ev = E.tk(ev, "dve")
                            evac_tok[sp] = tok_ev
                            E.w("dy", tok_ev)
                            ych = "dy2" if sp == 0 else "dy3"
                            ysc = Q * 4096 + cch * 512
                            E.dma(ych, ys_d[0:65, ysc:ysc + 512],
                                  ytst[sp][:, 0:512])
                            t_yt_dma = E.dma(ych, ys_d[65:130, ysc:ysc + 512],
                                             ytst[sp][:, 512:1024])
                            spill_tok[sp] = t_yt_dma
                            qi += 1
                    t_attn_done = evac_tok[(qi - 1) % 2]
                    E.w("pe", t_attn_done, evac_tok[qi % 2])

            if "vaug" in DBG:
                with nc.sbuf_tensor("dbgv", [128, 1056], F32) as tmpd:
                    E.w("dve", tok_vones)
                    for ti in range(16):
                        tok = E.tk(dve.tensor_copy(
                            tmpd[:], vaug[:, ti * 1056:(ti + 1) * 1056]), "dve")
                        E.w("dy", tok)
                        tok2 = E.dma("dy4",
                                     dbg_out["vaug"][:, ti * 1056:(ti + 1) * 1056],
                                     tmpd[:])
                        E.w("dve", tok2)

        acc = a(nc.sbuf_tensor("acc", [128, NCH * T], F32))
        tok_ln1 = {}
        tok_ln1_mul = {}
        with ExitStack() as s3:
            b3 = s3.enter_context
            x1b16 = b3(nc.sbuf_tensor("x1b16", [128, NCH * T], BF16))
            s3p = ExitStack()
            b3p = s3p.enter_context
            wproj16 = b3p(nc.sbuf_tensor("wproj16", [128, NCH * NCH * 128], BF16))
            ytsl = [b3p(nc.sbuf_tensor(f"ytsl{i}", [128, NCH * 512], BF16))
                    for i in range(2)]
            xfsl = [b3p(nc.sbuf_tensor(f"xfsl{i}", [128, NCH * 512], F32))
                    for i in range(2)]
            srow16 = [b3p(nc.sbuf_tensor(f"srow16_{i}", [16, 512], BF16))
                      for i in range(2)]
            srf = [b3p(nc.sbuf_tensor(f"srf{i}", [16, 512], F32)) for i in range(2)]
            rs16 = [b3p(nc.sbuf_tensor(f"rs16_{i}", [128, 512], BF16))
                    for i in range(2)]
            sel16 = b3p(nc.sbuf_tensor("sel16", [128, NCH * 128], BF16))
            ptmp = [b3p(nc.sbuf_tensor(f"ptmp{i}", [128, 512], F32)) for i in range(2)]
            sq16 = b3p(nc.sbuf_tensor("sq16", [128, NCH * 512], BF16))
            srBs2 = [b3p(nc.sbuf_tensor(f"srB{i}", [1, 512], F32)) for i in range(2)]
            srCs2 = [b3p(nc.sbuf_tensor(f"srC{i}", [1, 512], F32)) for i in range(2)]
            srBs = [srBs2[0], srBs2[1], srBs2[0], srBs2[1]]
            srCs = [srCs2[0], srCs2[1], srCs2[0], srCs2[1]]
            row_toks_ln = {}
            t_x1d = {}
            t_x1d_lo = {}

            E.w("dg", *E.bar())
            E.w("dy", *E.bar())
            E.w("dsc", *E.bar())
            t_sel = E.dma("dg0", sel16[:], sel_i)
            rs_z = [E.tk(dve.memset(rs16[i][:], 0.0), "dve") for i in range(2)]
            QWP = 2 * NCH * 128
            t_wp4 = [E.dma(("dg0", "dg1", "dg2", "dy4")[p],
                           wproj16[:, p * QWP:(p + 1) * QWP],
                           wproj_i[:, p * QWP:(p + 1) * QWP]) for p in range(4)]

            ysd3 = ys_d.rearrange("r (q c t) -> r q c t", c=NCH, t=512)

            with nc.psum_tensor("pp0", [128, 512], F32) as pp0, \
                 nc.psum_tensor("pp1", [128, 512], F32) as pp1, \
                 nc.psum_tensor("pbc0", [128, 512], F32) as pbc0, \
                 nc.psum_tensor("pbc1", [128, 512], F32) as pbc1, \
                 nc.psum_tensor("pstat", [128, 512], F32) as pstat, \
                 nc.psum_tensor("psq", [128, 512], F32) as psq, \
                 nc.psum_tensor("plb1", [128, 512], F32) as plb1, \
                 nc.psum_tensor("plb2", [128, 512], F32) as plb2:
                ppr = [pp0, pp1]
                pbc = [pbc0, pbc1]
                slice_users = [None, None]
                srow_users = [None, None]
                prev_norm = None
                rs_toks = [None, None]
                rf_toks = [None, None]
                t_srows = [None, None]
                t_ysl = [None, None]
                t_xfs_lo = [None, None]
                t_xfs = [None, None]

                def load_proj_tt(tt):
                    par = tt % 2
                    ch = "dy0" if par == 0 else "dy1"
                    ch2 = "dy1" if par == 0 else "dy0"
                    if tt >= 2:
                        E.w("dy", slice_users[par], rf_toks[par])
                        E.w("dsc", slice_users[par])
                    E.dma(ch, srow16[par][0:8, :],
                          ysd3[64:65, tt, :, :].squeeze(0))
                    t_srows[par] = E.dma(ch, srow16[par][8:16, :],
                                         ysd3[129:130, tt, :, :].squeeze(0))
                    ta = E.dma(ch, ytsl[par][0:64, :],
                               ys_d[0:64, tt * 4096:(tt + 1) * 4096])
                    tb = E.dma(ch2, ytsl[par][64:128, :],
                               ys_d[65:129, tt * 4096:(tt + 1) * 4096])
                    t_ysl[par] = (ta, tb)
                    xf3 = (x_t[:, tt * 512:(tt + 1) * 512]
                           .rearrange("(c p) t -> p c t", p=128))
                    xs3 = xfsl[par][:].rearrange("p (c t) -> p c t", t=512)
                    if tt == 0:
                        cl, chh = "dsc0", "dsc1"
                    elif tt == 1:
                        cl, chh = "dy2", "dy3"
                    else:
                        cl = "dsc0" if par == 0 else "dsc1"
                        chh = "dy2" if par == 0 else "dy3"
                    t_xfs_lo[par] = E.dma(cl, xs3[:, 0:4, :], xf3[:, 0:4, :])
                    t_xfs[par] = E.dma(chh, xs3[:, 4:8, :], xf3[:, 4:8, :])

                load_proj_tt(0)
                load_proj_tt(1)
                for tt in range(4):
                    par = tt % 2
                    ln_casts, ln_sqs = {}, {}
                    E.w("act", t_srows[par])
                    if rs_toks[par] is not None:
                        E.w("act", rs_toks[par])
                    tok_l = E.tk(act.activation(srf[par][:], srow16[par][:],
                                                AF.Ln, bias=consts[0:16, 3:4],
                                                scale=1.0), "act")
                    E.w("act", tok_l)
                    tok_rf = E.tk(act.activation(srf[par][:], srf[par][:],
                                                 AF.Exp, bias=consts[0:16, 3:4],
                                                 scale=-1.0), "act")
                    rf_toks[par] = tok_rf
                    E.w("dve", tok_rf, rs_z[par])
                    tok_rs = E.tk(dve.tensor_copy(rs16[par][0:16, :],
                                                  srf[par][:]), "dve")
                    E.mode(("mm", "k128", "bf16"))
                    rs_toks[par] = tok_rs

                    norm_toks = {}
                    for c in range(NCH):
                        cp = c % 2
                        E.w("pe", tok_rs, t_sel, t_xfs[par])
                        if c >= 2:
                            E.w("pe", norm_toks[c - 2])
                        elif prev_norm is not None:
                            E.w("pe", prev_norm)
                        bm = pe.matmul(pbc[cp][:, 0:512],
                                       sel16[:, c * 128:(c + 1) * 128],
                                       rs16[par][:], start=True, stop=True)
                        tok_bm = E.tk(bm, "pe")
                        E.w("dve", tok_bm, *t_ysl[par])
                        ni = dve.tensor_tensor(ytsl[par][:, c * 512:(c + 1) * 512],
                                               ytsl[par][:, c * 512:(c + 1) * 512],
                                               pbc[cp][:, 0:512], ALU.mult)
                        norm_toks[c] = E.tk(ni, "dve")
                    E.mode(("mm", "k128", "bf16"))
                    bias_tok = {}
                    ev_tok = {}
                    for m in range(NCH):
                        mp = m % 2
                        E.w("pe", t_wp4[m // 2],
                            norm_toks[NCH - 1], norm_toks[NCH - 2])
                        if m >= 2:
                            E.w("pe", ev_tok[m - 2])
                        mm_last = None
                        for c in range(NCH):
                            mm_last = pe.matmul(
                                ppr[mp][:, 0:512],
                                wproj16[:, (m * NCH + c) * 128:(m * NCH + c + 1) * 128],
                                ytsl[par][:, c * 512:(c + 1) * 512],
                                start=(c == 0), stop=(c == NCH - 1))
                        tok_mm = E.tk(mm_last, "pe")
                        E.w("dve", tok_mm,
                            t_xfs_lo[par] if m < 4 else t_xfs[par])
                        if m >= 2:
                            E.w("dve", bias_tok[m - 2])
                        i1 = dve.tensor_tensor(ptmp[mp][:], ppr[mp][:, 0:512],
                                               xfsl[par][:, m * 512:(m + 1) * 512],
                                               ALU.add)
                        ev_tok[m] = E.tk(i1, "dve")
                        E.w("act", ev_tok[m], t_bproj)
                        i2 = act.activation(
                            acc[:, m * T + tt * 512:m * T + (tt + 1) * 512],
                            ptmp[mp][:], AF.Identity, bias=bproj[:, m:m + 1],
                            scale=1.0)
                        bias_tok[m] = E.tk(i2, "act")
                        ln_casts[m], ln_sqs[m] = _ln_cast(
                            nc, E, acc, x1b16, sq16, consts, tt, m,
                            bias_tok[m], prev_norm, cast_eng="pool")
                    slice_users[par] = ev_tok[NCH - 1]
                    if tt + 2 < 4:
                        load_proj_tt(tt + 2)
                    if tt >= 1:
                        tok_n, pm_ = _ln_apply(nc, E, acc, x1b16, srBs[tt - 1],
                                               srCs[tt - 1], onesrf, tt - 1,
                                               row_toks_ln[tt - 1], True,
                                               plb1, plb2, consts=consts,
                                               ret_per_m=True)
                        tok_ln1_mul[tt - 1] = pm_[-1]
                        tok_ln1[tt - 1] = tok_n
                        prev_norm = tok_n
                        E.w("dy", tok_n)
                        x1dv = (x1_d[:, (tt - 1) * 4096:tt * 4096]
                                .rearrange("p (c t) -> p c t", t=512))
                        x1bv = (x1b16[:]
                                .rearrange("p (c q t) -> p c q t", q=4, t=512)
                                [:, :, tt - 1, :])
                        t_x1d_lo[tt - 1] = E.dma("dy3", x1dv[:, 0:4, :],
                                                 x1bv[:, 0:4, :])
                        t_x1d[tt - 1] = E.dma("dy4", x1dv[:, 4:8, :],
                                              x1bv[:, 4:8, :])
                    row_toks_ln[tt] = _ln_stats(nc, E, acc, x1b16, sq16,
                                                srBs[tt], srCs[tt],
                                                onesq16, consts, tt, bias_tok,
                                                pstat, psq, prev_norm,
                                                cast_toks=ln_casts, sq_toks=ln_sqs)
                tok_n, pm_ = _ln_apply(nc, E, acc, x1b16, srBs[3], srCs[3],
                                       onesrf, 3, row_toks_ln[3], True,
                                       plb1, plb2, consts=consts,
                                       ret_per_m=True)
                tok_ln1_mul[3] = pm_[-1]
                tok_ln1[3] = tok_n
                E.w("dy", tok_n)
                x1dv = (x1_d[:, 3 * 4096:4 * 4096]
                        .rearrange("p (c t) -> p c t", t=512))
                x1bv = (x1b16[:].rearrange("p (c q t) -> p c q t", q=4, t=512)
                        [:, :, 3, :])
                t_x1d_lo[3] = E.dma("dy3", x1dv[:, 0:4, :], x1bv[:, 0:4, :])
                t_x1d[3] = E.dma("dy4", x1dv[:, 4:8, :], x1bv[:, 4:8, :])
                E.w("pe", tok_ln1[3])
            s3p.close()
            if "x1" in DBG:
                with nc.sbuf_tensor("dbgx1", [128, T], F32) as tmpd:
                    for c in range(NCH):
                        E.w("dve", *tok_ln1.values())
                        tok = E.tk(dve.tensor_copy(tmpd[:],
                                                   acc[:, c * T:(c + 1) * T]), "dve")
                        E.w("dy", tok)
                        tok2 = E.dma("dy4", dbg_out["x1"][c * 128:(c + 1) * 128, :],
                                     tmpd[:])
                        E.w("dve", tok2)

            sct = b3(nc.sbuf_tensor("sct", [NEXP, T], F32))
            ident = b3(nc.sbuf_tensor("ident_s", [128, 128], F32))
            t_id = E.dma("dy4", ident[:], ident_i)
            sctm = b3(nc.sbuf_tensor("sctm", [128, 16 * NEXP], F32))
            rm1 = b3(nc.sbuf_tensor("rm1", [128, 16], F32))
            rm2 = b3(nc.sbuf_tensor("rm2", [128, 16], F32))
            rs2 = b3(nc.sbuf_tensor("rs2", [128, 16 * NEXP], F32))
            req1 = b3(nc.sbuf_tensor("req1", [128, 16 * NEXP], F32))
            req2 = b3(nc.sbuf_tensor("req2", [128, 16 * NEXP], F32))
            rp1 = b3(nc.sbuf_tensor("rp1", [128, 16], F32))
            rp2 = b3(nc.sbuf_tensor("rp2", [128, 16], F32))
            rtt1 = b3(nc.sbuf_tensor("rtt1", [128, 16 * NEXP], F32))
            cwtm = b3(nc.sbuf_tensor("cwtm", [128, 16 * NEXP], F32))

            def r3(ap):
                return ap.rearrange("p (t e) -> p t e", e=NEXP)

            with nc.psum_tensor("prt0", [128, 512], F32) as prt0, \
                 nc.psum_tensor("prt1", [128, 512], F32) as prt1:
                E.mode(("mm", "k128m8", "f32"))
                sc_ev = []
                for tt in range(4):
                    par = tt % 2
                    if tt >= 2:
                        E.w("pe", sc_ev[tt - 2])
                    E.w("pe", t_rwt, tok_ln1_mul[tt])
                    mm_last = None
                    for c in range(NCH):
                        mm_last = pe.matmul(
                            [prt0, prt1][par][0:NEXP, 0:512],
                            rwt[:, c * NEXP:(c + 1) * NEXP],
                            acc[:, c * T + tt * 512:c * T + (tt + 1) * 512],
                            start=(c == 0), stop=(c == NCH - 1))
                    tok_mm = E.tk(mm_last, "pe")
                    E.w("dve", tok_mm)
                    iv = dve.tensor_copy(sct[:, tt * 512:(tt + 1) * 512],
                                         [prt0, prt1][par][0:NEXP, 0:512])
                    sc_ev.append(E.tk(iv, "dve"))
                E.w("pe", *sc_ev)

            with nc.psum_tensor("prt2", [128, 128], F32) as prt2:
                E.mode(("tr", "f32"))
                E.w("pe", t_id, *sc_ev)
                tr_tok = None
                for ti in range(16):
                    tr = pe.transpose(prt2[:, ti * 8:(ti + 1) * 8],
                                      sct[:, ti * 128:(ti + 1) * 128],
                                      ident[0:8, 0:8])
                    tr_tok = E.tk(tr, "pe")
                E.w("dve", tr_tok)
                tok_sctm = E.tk(dve.tensor_copy(sctm[:], prt2[:, 0:128]), "dve")
                E.w("pe", tok_sctm)

            E.w("dve", tok_sctm, t_rbrep)
            tok = E.tk(dve.tensor_tensor(
                r3(sctm[:]), r3(sctm[:]),
                rbrep[:].unsqueeze(1).broadcast_to((128, 16, NEXP)),
                ALU.add), "dve")
            E.w("dve", tok)
            tok_m1 = E.tk(dve.reduce_max(rm1[:], r3(sctm[:]), axis=AX.X), "dve")
            E.w("dve", tok_m1)
            m1b = rm1[:].unsqueeze(2).broadcast_to((128, 16, NEXP))
            tok_e1 = E.tk(dve.tensor_tensor(r3(req1[:]), r3(sctm[:]), m1b,
                                            ALU.is_ge), "dve")
            E.w("dve", tok_e1)
            tok_s2 = E.tk(dve.scalar_tensor_tensor(r3(rs2[:]), r3(req1[:]), -1e30,
                                                   r3(sctm[:]), ALU.mult, ALU.add),
                          "dve")
            E.w("dve", tok_s2)
            tok_m2 = E.tk(dve.reduce_max(rm2[:], r3(rs2[:]), axis=AX.X), "dve")
            E.w("dve", tok_m2)
            tok_d = E.tk(dve.tensor_tensor(rp1[:], rm1[:], rm2[:], ALU.subtract),
                         "dve")
            E.w("act", tok_d)
            tok_p1 = E.tk(act.activation(rp1[:], rp1[:], AF.Sigmoid,
                                         bias=consts[:, 3:4], scale=1.0), "act")
            E.w("act", tok_p1)
            tok_p2 = E.tk(act.activation(rp2[:], rp1[:], AF.Identity,
                                         bias=consts[:, 2:3], scale=-1.0), "act")
            E.w("dve", tok_m2, tok_s2)
            m2b = rm2[:].unsqueeze(2).broadcast_to((128, 16, NEXP))
            tok_e2 = E.tk(dve.tensor_tensor(r3(req2[:]), r3(rs2[:]), m2b, ALU.is_ge),
                          "dve")
            E.w("dve", tok_p1, tok_e1)
            p1b = rp1[:].unsqueeze(2).broadcast_to((128, 16, NEXP))
            tok_t1 = E.tk(dve.tensor_tensor(r3(rtt1[:]), r3(req1[:]), p1b, ALU.mult),
                          "dve")
            E.w("dve", tok_p2, tok_e2)
            p2b = rp2[:].unsqueeze(2).broadcast_to((128, 16, NEXP))
            tok_t2 = E.tk(dve.tensor_tensor(r3(req2[:]), r3(req2[:]), p2b, ALU.mult),
                          "dve")
            E.w("dve", tok_t1, tok_t2)
            tok_cw = E.tk(dve.tensor_tensor(cwtm[:], rtt1[:], req2[:], ALU.add),
                          "dve")
            if "cw" in DBG:
                E.w("dy", tok_cw)
                tok2 = E.dma("dy4", dbg_out["cw"], cwtm[:])
                E.w("dve", tok2)

            with nc.psum_tensor("prt3", [128, 2048], F32) as prt3:
                E.mode(("tr", "f32"))
                E.w("pe", tok_cw)
                tr_tok = None
                for ti in range(16):
                    tr = pe.transpose(prt3[0:NEXP, ti * 128:(ti + 1) * 128],
                                      cwtm[:, ti * 8:(ti + 1) * 8], ident[:, 0:128])
                    tr_tok = E.tk(tr, "pe")
                E.w("dve", tr_tok)
                tok_cwT16 = E.tk(dve.tensor_copy(cwT16[0:NEXP, :],
                                                 prt3[0:NEXP, 0:2048]), "dve")
                E.w("pe", tok_cwT16)

        dbias_ev = {}
        with ExitStack() as s5:
            b5 = s5.enter_context
            x1sl = [b5(nc.sbuf_tensor(f"x1sl{i}", [128, NCH * 512], BF16))
                    for i in range(2)]
            wgb = [b5(nc.sbuf_tensor(f"wg16_{i}", [128, 16 * NCH * 128], BF16))
                   for i in range(2)]
            wdb = b5(nc.sbuf_tensor("wd16", [128, NCH * NCH * 128], BF16))
            asb = [b5(nc.sbuf_tensor(f"as16_{i}", [128, NCH * 512], BF16))
                   for i in range(2)]
            cwb = b5(nc.sbuf_tensor("cwb16", [128, T], BF16))
            cwrow0 = b5(nc.sbuf_tensor("cwrow0", [128, T], BF16))
            cwrow = [cwrow0, cwrow0]
            sg_t = [b5(nc.sbuf_tensor(f"sgt{i}", [128, 512], BF16)) for i in range(2)]
            srB2s2 = [b5(nc.sbuf_tensor(f"srB2_{i}", [1, 512], F32))
                      for i in range(2)]
            srC2s2 = [b5(nc.sbuf_tensor(f"srC2_{i}", [1, 512], F32))
                      for i in range(2)]
            srB2s = [srB2s2[0], srB2s2[1], srB2s2[0], srB2s2[1]]
            srC2s = [srC2s2[0], srC2s2[1], srC2s2[0], srC2s2[1]]
            gc_t = [b5(nc.sbuf_tensor(f"gct{i}", [128, 512], BF16)) for i in range(2)]
            uc_t = [b5(nc.sbuf_tensor(f"uct{i}", [128, 512], BF16)) for i in range(2)]

            def load_x1(k, e, tt, split=False):
                ch = "dy0" if k % 2 == 0 else "dy1"
                E.w("dy", t_x1d_lo.get(tt), t_x1d.get(tt))
                dst = x1sl[k % 2]
                base = tt * 4096
                if split:
                    t_x1lo[k % 2] = E.dma("dy4" if k % 2 == 0 else "dy3",
                                          dst[:, 0:2048],
                                          x1_d[:, base:base + 2048])
                    return E.dma(ch, dst[:, 2048:4096],
                                 x1_d[:, base + 2048:base + 4096])
                return E.dma(ch, dst[:], x1_d[:, base:base + 4096])

            t_x1lo = [None, None]
            t_x1sl = [None, None]
            E.w("dy", t_x1d_lo[3], t_x1d[3])
            t_x1sl[0] = load_x1(0, 0, 0, split=True)
            t_x1sl[1] = load_x1(1, 0, 1, split=True)

            E.w("dg", *E.bar())
            E.w("dsc", *E.bar())
            E.w("dy", *E.bar())
            tok_crz = E.tk(dve.memset(cwrow0[:], 0.0), "dve")
            E.w("dg", tok_cwT16, tok_crz)
            t_cwrow = [E.dma("dg0", cwrow[0][0:1, :], cwT16[0:1, :]), None]

            def load_wg(e, war=None):
                if war is not None:
                    E.w("dg", war)
                    E.w("dy", war)
                HW_ = 8 * NCH * 128
                lo = E.dma("dg1", wgb[e % 2][:, 0:HW_], wg_i[e][:, 0:HW_])
                hi = E.dma("dy2", wgb[e % 2][:, HW_:2 * HW_],
                           wg_i[e][:, HW_:2 * HW_])
                return (lo, hi)

            t_wg = [load_wg(0)] + [None] * (NEXP - 1)
            _t0 = E.dma("dg2", wdb[:], wd_i[0])
            t_wd = [[_t0, _t0]] + [None] * (NEXP - 1)

            with nc.psum_tensor("pg0", [128, 512], F32) as pg0, \
                 nc.psum_tensor("pg1", [128, 512], F32) as pg1, \
                 nc.psum_tensor("pu0", [128, 512], F32) as pu0, \
                 nc.psum_tensor("pu1", [128, 512], F32) as pu1, \
                 nc.psum_tensor("pd0", [128, 512], F32) as pd0, \
                 nc.psum_tensor("pd1", [128, 512], F32) as pd1, \
                 nc.psum_tensor("pcw", [128, 1024], F32) as pcw:
                pg, pu, pd = [pg0, pg1], [pu0, pu1], [pd0, pd1]
                swi_toks, down_ev = {}, {}
                down_hist = []
                pairs = [(e, tt) for e in range(NEXP) for tt in range(4)]
                gate_last = None

                def emit_down(k, m2_):
                    e_, tt_ = pairs[k]
                    tp_ = k % 2
                    dp = len(down_hist) % 2
                    if len(down_hist) >= 2:
                        E.w("pe", down_hist[-2])
                    twd = t_wd[e_]
                    E.w("pe", twd[0] if m2_ < 4 else twd[1],
                        *[swi_toks[(e_, tt_, hcx)][0] for hcx in range(NCH)])
                    mmd = None
                    for hc_ in range(NCH):
                        mmd = pe.matmul(
                            pd[dp][:, 0:512],
                            wdb[:, (m2_ * NCH + hc_) * 128:(m2_ * NCH + hc_ + 1) * 128],
                            asb[tp_][:, hc_ * 512:(hc_ + 1) * 512],
                            start=(hc_ == 0), stop=(hc_ == NCH - 1))
                    tok_dn = E.tk(mmd, "pe")
                    sl = slice(m2_ * T + tt_ * 512, m2_ * T + (tt_ + 1) * 512)
                    E.w("dve", tok_dn)
                    if e_ == 0:
                        E.w("dve", tok_ln1[tt_])
                    tok = E.tk(dve.tensor_tensor(acc[:, sl], acc[:, sl],
                                                 pd[dp][:, 0:512], ALU.add), "dve")
                    down_ev[(e_, tt_, m2_)] = tok
                    down_hist.append(tok)
                    if tt_ == 3 and e_ + 1 < NEXP and m2_ in (3, NCH - 1):
                        HWD = 4 * NCH * 128
                        half = 0 if m2_ == 3 else 1
                        E.w("dg", tok_dn)
                        tok_l = E.dma("dg2",
                                      wdb[:, half * HWD:(half + 1) * HWD],
                                      wd_i[e_ + 1][:, half * HWD:(half + 1) * HWD])
                        if half == 0:
                            t_wd[e_ + 1] = [tok_l, None]
                        else:
                            t_wd[e_ + 1][1] = tok_l

                rows2 = {}
                ln2_apply = {}
                tok_st = {}
                tail_casts = {}

                def stage_casts(s):
                    sp = s % 2
                    ycast = wgb[0][:, sp * 8192:sp * 8192 + 4096]
                    sqali = wgb[0][:, sp * 8192 + 4096:(sp + 1) * 8192]
                    prevu = rows2.get(s - 2)
                    prev = prevu[0] if prevu else pair_gate_last.get(27)
                    casts, sqs = {}, {}
                    for m in range(NCH):
                        casts[m], sqs[m] = _ln_cast(nc, E, acc, ycast, sqali,
                                                    consts, s, m,
                                                    down_ev[(NEXP - 1, s, m)],
                                                    prev, bstride=512,
                                                    cast_eng="pool")
                    tail_casts[s] = (casts, sqs)

                def stage_stats(s, pstat, psq, *war):
                    sp = s % 2
                    ycast = wgb[0][:, sp * 8192:sp * 8192 + 4096]
                    sqali = wgb[0][:, sp * 8192 + 4096:(sp + 1) * 8192]
                    casts, sqs = tail_casts[s]
                    E.w("pe", *[t for t in war if t is not None])
                    rows2[s] = _ln_stats(nc, E, acc, ycast, sqali,
                                         srB2s[s], srC2s[s], onesq16, consts,
                                         s, None, pstat, psq, None, bstride=512,
                                         cast_toks=casts, sq_toks=sqs)

                def stage_apply(s, pb1, pb2, *war):
                    E.w("pe", *[t for t in war if t is not None])
                    tok_n, per_m = _ln_apply(nc, E, acc, None, srB2s[s],
                                             srC2s[s], onesrf, s, rows2[s],
                                             False, pb1, pb2, bstride=512,
                                             ret_per_m=True)
                    ln2_apply[s] = tok_n
                    if s == 3:
                        chans = ("dy4", "dy3", "dy1", "dy0",
                                 "dsc0", "dsc1", "dy2", "dg0")
                    else:
                        chans = ("dy4", "dy3", "dy1", "dy0",
                                 "dy2", "dy4", "dy3", "dy1")
                    for m in range(NCH):
                        ch = chans[m]
                        eng = ("dsc" if ch.startswith("dsc")
                               else "dg" if ch.startswith("dg") else "dy")
                        E.w(eng, per_m[m])
                        tok_st[ch] = E.dma(ch,
                                           out[m * 128:(m + 1) * 128,
                                               s * 512:(s + 1) * 512],
                                           acc[:, m * T + s * 512:
                                               m * T + (s + 1) * 512])

                tok_cwb_prev = None
                tok_cwb = None
                pair_gate_last = {}
                gexp_last = {}
                for k, (e, tt) in enumerate(pairs):
                    ep = e % 2
                    if tt == 0:
                        E.w("pe", t_cwrow[ep], t_oq, tok_cwb_prev)
                        for wv_ in range(2):
                            mmc = None
                            for hh in range(2):
                                mmc = pe.matmul(
                                    pcw[:, hh * 512:(hh + 1) * 512],
                                    onesq16[:, 0:128],
                                    cwrow[ep][:, wv_ * 1024 + hh * 512:
                                              wv_ * 1024 + (hh + 1) * 512],
                                    start=True, stop=True)
                            tok_mmc = E.tk(mmc, "pe")
                            E.w("dve", tok_mmc)
                            if tok_cwb_prev is not None and wv_ == 0:
                                E.w("dve", swi_toks.get((e - 1, 3, 7), (None,))[0])
                            tok_cwb = E.tk(dve.tensor_copy(
                                cwb[:, wv_ * 1024:(wv_ + 1) * 1024], pcw[:]), "dve")
                            if wv_ == 0:
                                E.w("pe", tok_cwb)
                        tok_cwb_prev = tok_cwb
                        if e + 1 < NEXP:
                            E.w("dg", tok_mmc)
                            t_cwrow[(e + 1) % 2] = E.dma(
                                "dg0", cwrow[(e + 1) % 2][0:1, :],
                                cwT16[e + 1:e + 2, :])
                        if 1 <= e <= 4:
                            s_ = e - 1
                            E.w("pe", t_dbt, tok_cwT16, tok_zpad, tok_zcw)
                            for m_ in range(NCH):
                                dp_ = len(down_hist) % 2
                                if len(down_hist) >= 2:
                                    E.w("pe", down_hist[-2])
                                mmb = pe.matmul(
                                    pd[dp_][:, 0:512],
                                    dbt16[:, m_ * 128:(m_ + 1) * 128],
                                    cwT16[:, s_ * 512:(s_ + 1) * 512],
                                    start=True, stop=True)
                                tok_b = E.tk(mmb, "pe")
                                sl_ = slice(m_ * T + s_ * 512,
                                            m_ * T + (s_ + 1) * 512)
                                E.w("dve", tok_b)
                                tok_db = E.tk(dve.tensor_tensor(
                                    acc[:, sl_], acc[:, sl_],
                                    pd[dp_][:, 0:512], ALU.add), "dve")
                                down_hist.append(tok_db)
                        E.mode(("mm", "k128", "bf16"))
                    tp = k % 2
                    for hc in range(NCH):
                        hp = hc % 2
                        E.w("pe", t_wg[e][0], t_x1sl[k % 2],
                            t_x1lo[k % 2] if k < 2 else None)
                        prevk = None
                        if hc >= 2:
                            prevk = (e, tt, hc - 2)
                        elif k > 0:
                            prevk = pairs[k - 1] + (hc + 6,)
                        if prevk and prevk in swi_toks:
                            E.w("pe", swi_toks[prevk][1])
                        mmg = mmu = None
                        for c in range(NCH):
                            mmg = pe.matmul(
                                pg[hp][:, 0:512],
                                wgb[ep][:, (hc * NCH + c) * 128:(hc * NCH + c + 1) * 128],
                                x1sl[k % 2][:, c * 512:(c + 1) * 512],
                                start=(c == 0), stop=(c == NCH - 1))
                        tok_g = E.tk(mmg, "pe")
                        E.w("pe", t_wg[e][1])
                        for c in range(NCH):
                            mmu = pe.matmul(
                                pu[hp][:, 0:512],
                                wgb[ep][:, ((hc + 8) * NCH + c) * 128:
                                    ((hc + 8) * NCH + c + 1) * 128],
                                x1sl[k % 2][:, c * 512:(c + 1) * 512],
                                start=(c == 0), stop=(c == NCH - 1))
                        tok_u = E.tk(mmu, "pe")
                        gate_last = tok_u
                        bg = e * 16 + hc
                        bu = e * 16 + 8 + hc
                        E.w("act", tok_g, tok_u, t_gubs, t_gub7)
                        if prevk and prevk in swi_toks:
                            E.w("act", swi_toks[prevk][0])
                        tok_sg = E.tk(act.activation(sg_t[hp][:], pg[hp][:],
                                                     AF.Sigmoid,
                                                     bias=gubs[:, bg:bg + 1],
                                                     scale=1.702), "act")
                        tok_rg = E.tk(act.activation(gc_t[hp][:], pg[hp][:],
                                                     AF.Relu,
                                                     bias=gub7[:, bg:bg + 1],
                                                     scale=1.0), "act")
                        tok_ru = E.tk(act.activation(uc_t[hp][:], pu[hp][:],
                                                     AF.Relu,
                                                     bias=gub7[:, bu:bu + 1],
                                                     scale=1.0), "act")
                        E.w("dve", tok_rg, tok_ru)
                        if prevk and prevk in swi_toks:
                            E.w("dve", swi_toks[prevk][0])
                        tok_gc = E.tk(dve.tensor_scalar(gc_t[hp][:], gc_t[hp][:],
                                                        -7.0, 7.0,
                                                        ALU.add, ALU.min), "dve")
                        tok_uc = E.tk(dve.tensor_scalar(uc_t[hp][:], uc_t[hp][:],
                                                        -6.0, 8.0,
                                                        ALU.add, ALU.min), "dve")
                        E.w("dve", tok_gc, tok_sg)
                        tok_m1 = E.tk(dve.tensor_tensor(gc_t[hp][:], gc_t[hp][:],
                                                        sg_t[hp][:], ALU.mult), "dve")
                        E.w("dve", tok_uc, tok_cwb)
                        tok_m2 = E.tk(dve.tensor_tensor(
                            uc_t[hp][:], uc_t[hp][:],
                            cwb[:, tt * 512:(tt + 1) * 512], ALU.mult), "dve")
                        prev_dn = None
                        if k >= 2:
                            pk2 = pairs[k - 2] + (NCH - 1,)
                            prev_dn = down_ev.get(pk2)
                        E.w("dve", tok_m1, tok_m2, prev_dn)
                        tok_as = E.tk(dve.tensor_tensor(
                            asb[tp][:, hc * 512:(hc + 1) * 512],
                            gc_t[hp][:], uc_t[hp][:], ALU.mult), "dve")
                        swi_toks[(e, tt, hc)] = (tok_as, tok_ru)
                        if k > 0:
                            emit_down(k - 1, hc)
                    pair_gate_last[k] = gate_last
                    if k + 2 < len(pairs):
                        e2, tt2 = pairs[k + 2]
                        E.w("dy", pair_gate_last[k])
                        t_x1sl[k % 2] = load_x1(k + 2, e2, tt2)
                    if tt == 0 and e + 1 < NEXP:
                        t_wg[e + 1] = load_wg(
                            e + 1, pair_gate_last.get(4 * (e - 1) + 3)
                            if e >= 1 else None)
                    if e == NEXP - 1:
                        if tt == 1:
                            stage_casts(0)
                        elif tt == 2:
                            stage_casts(1)
                            stage_stats(0, pcw[:, 0:512], pcw[:, 512:1024],
                                        tok_cwb)
                        elif tt == 3:
                            stage_casts(2)
                            swi31 = [swi_toks[(7, 3, h)][0] for h in range(NCH)]
                            stage_stats(1, pg0[:], pg1[:], *swi31)
                            stage_apply(0, pcw[:, 0:512], pcw[:, 512:1024])
                        E.mode(("mm", "k128", "bf16"))
                for m2_ in range(NCH):
                    emit_down(len(pairs) - 1, m2_)
                stage_casts(3)
                stage_apply(1, pu0[:], pu1[:], *swi31)
                stage_stats(2, pcw[:, 0:512], pcw[:, 512:1024], ln2_apply[0])
                stage_stats(3, pd0[:], pd1[:], down_ev[(7, 3, NCH - 1)])
                stage_apply(2, pg0[:], pg1[:], *rows2[1])
                stage_apply(3, pcw[:, 0:512], pcw[:, 512:1024], *rows2[2])
                for chv, tokv in tok_st.items():
                    eng = ("dsc" if chv.startswith("dsc")
                           else "dg" if chv.startswith("dg") else "dy")
                    E.w(eng, tokv)
                E.w("pe", ln2_apply[3])

        if "accmoe" in DBG:
            with nc.sbuf_tensor("dbgacc", [128, T], F32) as tmpd:
                for c in range(NCH):
                    E.w("dve", dbias_ev[(c, 3)])
                    tok = E.tk(dve.tensor_copy(tmpd[:], acc[:, c * T:(c + 1) * T]),
                               "dve")
                    E.w("dy", tok)
                    tok2 = E.dma("dy4", dbg_out["accmoe"][c * 128:(c + 1) * 128, :],
                                 tmpd[:])
                    E.w("dve", tok2)

    return nc


def kernel(L0_attn_x, L0_attn_W_attn, L0_attn_b_attn, L0_attn_W_proj, L0_attn_b_proj,
           L0_ffn_router_w, L0_ffn_router_b, L0_ffn_gate_up_w, L0_ffn_gate_up_b,
           L0_ffn_down_w, L0_ffn_down_b):
    args = [np.asarray(v, np.float32) for v in
            (L0_attn_x, L0_attn_W_attn, L0_attn_b_attn, L0_attn_W_proj,
             L0_attn_b_proj, L0_ffn_router_w, L0_ffn_router_b, L0_ffn_gate_up_w,
             L0_ffn_gate_up_b, L0_ffn_down_w, L0_ffn_down_b)]
    in_maps = _host_layout(*args)
    nc = build_nc()
    res = run_bass_kernel_spmd(nc, in_maps, core_ids=list(range(8)))
    return np.stack([np.ascontiguousarray(r["out"].T) for r in res.results], axis=0)
